# version 24
# baseline (speedup 1.0000x reference)
import contextlib
import numpy as np
import ml_dtypes
import concourse.bass as bass
import concourse.mybir as mybir
from concourse.bass_utils import run_bass_kernel_spmd

F32 = mybir.dt.float32
BF16 = mybir.dt.bfloat16
U8 = mybir.dt.uint8
AF = mybir.ActivationFunctionType
ALU = mybir.AluOpType
AX = mybir.AxisListType

B, T, D, S, I = 4, 2048, 2048, 2048, 8192
NT = 1024
NTT = NT // 128
KD = D // 128
KS = S // 128
KI = I // 128
EPS = 1e-5
PRE = 128

DEBUG_TAPS = False


class Sem:
    def __init__(self, es, nc, name):
        self.h = es.enter_context(nc.semaphore(name))
        self.n = 0
        self.name = name

    def inc(self, instr, k=1):
        instr.then_inc(self.h, k)
        self.n += k
        return self.n


class Waiter:

    def __init__(self):
        self.seen = {}

    def wait(self, eng, engname, sem, val):
        if val is None or val <= 0:
            return
        key = (engname, sem.name)
        if self.seen.get(key, -1) >= val:
            return
        eng.wait_ge(sem.h, val)
        self.seen[key] = val


def build_program():
    nc = bass.Bass("TRN2", target_bir_lowering=False)
    es = contextlib.ExitStack()

    d_x = nc.dram_tensor("x_own", [NT, D], F32, kind="ExternalInput").ap()
    d_xp = nc.dram_tensor("x_pre", [PRE, D], F32, kind="ExternalInput").ap()
    d_mask = nc.dram_tensor("maskv", [128, 1], F32, kind="ExternalInput").ap()
    d_rmsb = nc.dram_tensor("rms_bcast", [128, D], BF16, kind="ExternalInput").ap()
    d_idb = nc.dram_tensor("ident_bf", [128, 128], BF16, kind="ExternalInput").ap()
    d_idf = nc.dram_tensor("ident_f32", [128, 128], F32, kind="ExternalInput").ap()
    d_fb = nc.dram_tensor("f_bias", [128, KS], F32, kind="ExternalInput").ap()
    d_ob = nc.dram_tensor("out_bias", [128, KD], F32, kind="ExternalInput").ap()
    d_gsc = nc.dram_tensor("gate_sc", [128, KI], F32, kind="ExternalInput").ap()
    d_usc = nc.dram_tensor("up_sc", [128, KI], F32, kind="ExternalInput").ap()
    d_dsc = nc.dram_tensor("down_sc", [128, KD], F32, kind="ExternalInput").ap()
    d_wf = nc.dram_tensor("wf", [KS, 128, KD, 128], F32, kind="ExternalInput").ap()
    d_wzg = nc.dram_tensor("wzg", [KS, 128, KD, 128], F32, kind="ExternalInput").ap()
    d_wz = nc.dram_tensor("wz", [KS, 128, KD, 128], F32, kind="ExternalInput").ap()
    d_wga = nc.dram_tensor("wga", [KS, 128, KD, 128], F32, kind="ExternalInput").ap()
    d_wgb = nc.dram_tensor("wgb", [KS, 128, KD, 128], F32, kind="ExternalInput").ap()
    d_wo = nc.dram_tensor("wo", [KD, 128, KS, 128], F32, kind="ExternalInput").ap()
    d_gu = nc.dram_tensor("wgu", [KI, 2, 128, 16, 128], U8, kind="ExternalInput").ap()
    d_dn = nc.dram_tensor("wdn", [KD, 4, 128, 16, 128], U8, kind="ExternalInput").ap()

    d_out = nc.dram_tensor("out", [NT, D], F32, kind="ExternalOutput").ap()

    taps = {}
    if DEBUG_TAPS:
        taps["t_xn"] = nc.dram_tensor("t_xn", [128, KD, NT], BF16, kind="ExternalOutput").ap()
        taps["t_f"] = nc.dram_tensor("t_f", [128, KS, NT], BF16, kind="ExternalOutput").ap()
        taps["t_z"] = nc.dram_tensor("t_z", [128, KS, NT], BF16, kind="ExternalOutput").ap()
        taps["t_carry"] = nc.dram_tensor("t_carry", [128, KS], F32, kind="ExternalOutput").ap()
        taps["t_gated"] = nc.dram_tensor("t_gated", [128, KS, NT], BF16, kind="ExternalOutput").ap()
        taps["t_xln"] = nc.dram_tensor("t_xln", [NT, D], F32, kind="ExternalOutput").ap()
        taps["t_pf"] = nc.dram_tensor("t_pf", [128, KS, NT], BF16, kind="ExternalOutput").ap()
        taps["t_pz"] = nc.dram_tensor("t_pz", [128, KS, NT], BF16, kind="ExternalOutput").ap()
        taps["t_pc"] = nc.dram_tensor("t_pc", [128, KS], F32, kind="ExternalOutput").ap()
        taps["t_y"] = nc.dram_tensor("t_y", [128, 4, D], F32, kind="ExternalOutput").ap()
        taps["t_u"] = nc.dram_tensor("t_u", [128, 4, D], F32, kind="ExternalOutput").ap()
        taps["t_stat"] = nc.dram_tensor("t_stat", [128, 32], F32, kind="ExternalOutput").ap()

    sb = lambda name, shape, dt: es.enter_context(nc.sbuf_tensor(name, shape, dt))
    xn_fm = sb("xn_fm", [128, KD, NT], BF16)
    fbuf = sb("fbuf", [128, KS, NT], BF16)
    zbuf = sb("zbuf", [128, KS, NT], BF16)
    xlnfm = sb("xlnfm", [128, KD, NT], BF16)
    ytm = [sb(f"ytm{j}", [128, 4, D], BF16) for j in range(2)]
    NWB = 4
    wb = [sb(f"wb{j}", [128, 16, 128], BF16) for j in range(NWB)]
    xstage = sb("xstage", [128, D], F32)
    xntm = sb("xntm", [128, D], BF16)
    zgsig = [sb(f"zgsig{j}", [128, 512], BF16) for j in range(2)]
    gasil = [sb(f"gasil{j}", [128, 512], BF16) for j in range(3)]
    evst = [sb(f"evst{j}", [128, 512], F32) for j in range(2)]
    hrot = [sb(f"hrot{j}", [128, NT], BF16) for j in range(2)]
    carry = sb("carry", [128, KS], F32)
    rmsb = sb("rmsb", [128, D], BF16)
    idb = sb("idb", [128, 128], BF16)
    idf = sb("idf", [128, 128], F32)
    fbv = sb("fbv", [128, KS], F32)
    obv = sb("obv", [128, KD], F32)
    gscv = sb("gscv", [128, KI], F32)
    uscv = sb("uscv", [128, KI], F32)
    dscv = sb("dscv", [128, KD], F32)
    maskv = sb("maskv_sb", [128, 1], F32)
    stat = sb("stat", [128, 16], F32)
    stat2 = sb("stat2", [128, 16], F32)
    statp = sb("statp", [128, 16], F32)
    statq = sb("statq", [128, 128], F32)

    psA = es.enter_context(nc.psum_tensor("psA", [128, 4, 512], F32))
    psT = [es.enter_context(nc.psum_tensor(f"psT{j}", [128, 4, 128], F32)) for j in range(2)]
    psB = es.enter_context(nc.psum_tensor("psB", [128, 2, 8, 128], BF16))

    s_w = Sem(es, nc, "s_w")
    s_wb = [Sem(es, nc, f"s_wb{j}") for j in range(4)]
    s_x2 = [Sem(es, nc, f"s_x2{j}") for j in range(3)]
    s_pl = [Sem(es, nc, f"s_pl{j}") for j in range(4)]
    s_x = Sem(es, nc, "s_x")
    s_mm = Sem(es, nc, "s_mm")
    s_ea = Sem(es, nc, "s_ea")
    s_ev = Sem(es, nc, "s_ev")
    s_st = Sem(es, nc, "s_st")
    s_oq = [Sem(es, nc, f"s_oq{c}") for c in range(4)]

    W = Waiter()

    free_psA = [None] * 4
    free_psT = [None] * 2
    free_psB = [None] * 2
    free_wb = [None] * NWB
    free_zgsig = [None] * 2
    free_gasil = [None] * 3
    free_evst = [None] * 2
    free_hrot = [None] * 2
    free_xstage = None
    free_xntm = None

    psA_ctr = [0]
    free_x2 = [None] * 4

    _block = es.enter_context(nc.Block())

    class _EngineBuffer:

        def __init__(self):
            self.pend = {k: [] for k in ("sync", "gpsimd", "scalar", "tensor", "vector")}

        def _add(self, eng, fn):
            self.pend[eng].append(fn)
            if len(self.pend[eng]) >= 48:
                self.flush(eng)

        def sync(self, fn):
            self._add("sync", fn)

        def gpsimd(self, fn):
            self._add("gpsimd", fn)

        def scalar(self, fn):
            self._add("scalar", fn)

        def tensor(self, fn):
            self._add("tensor", fn)

        def vector(self, fn):
            self._add("vector", fn)

        def flush(self, eng=None):
            engs = [eng] if eng else list(self.pend)
            for e in engs:
                fns = self.pend[e]
                if not fns:
                    continue
                self.pend[e] = []
                getattr(_block, e)(lambda handle, fns=fns: [f(handle) for f in fns])

    block = _EngineBuffer()


    def load_consts():
        def sec(sy):
            sy.dma_start(out=rmsb[:], in_=d_rmsb).then_inc(s_x.h, 16)
            sy.dma_start(out=idb[:], in_=d_idb).then_inc(s_x.h, 16)
            sy.dma_start(out=idf[:], in_=d_idf).then_inc(s_x.h, 16)
            sy.dma_start(out=fbv[:], in_=d_fb).then_inc(s_x.h, 16)
            sy.dma_start(out=obv[:], in_=d_ob).then_inc(s_x.h, 16)
            sy.dma_start(out=gscv[:], in_=d_gsc).then_inc(s_x.h, 16)
            sy.dma_start(out=uscv[:], in_=d_usc).then_inc(s_x.h, 16)
            sy.dma_start(out=dscv[:], in_=d_dsc).then_inc(s_x.h, 16)
            sy.dma_start(out=maskv[:], in_=d_mask).then_inc(s_x.h, 16)
        block.sync(sec)
        for _ in range(9):
            s_x.n += 16
    consts_done = [0]

    class RmsDriver:

        def __init__(self, d_src, dst_fm, ntt=NTT):
            self.d_src = d_src
            self.dst = dst_fm
            self.ntt = ntt
            self.tt = 0
            self.done_val = 0
            self.tile_done = []

        def step(self):
            if self.tt >= self.ntt:
                return False
            _rms_tile(self.d_src, self.dst, self.tt, self)
            self.tt += 1
            return True

        def finish(self):
            while self.step():
                pass
            if rms_pend[0] is not None:
                _emit_rms_tr(*rms_pend[0])
                rms_pend[0] = None
            return self.done_val

    rms_xnbuf_free = [None, None]
    rms_statp_free = [None, None]
    rms_pend = [None]

    def _emit_rms_tr(dst_fm, tt, xn_src, xnv, drv):
        for hb in range(2):
            fb_ = free_psB[hb]
            def sec_t(t, hb=hb, fb_=fb_, xnv=xnv, xn_src=xn_src):
                W.wait(t, "tensor", s_ev, xnv)
                W.wait(t, "tensor", s_x, consts_done[0])
                if fb_ is not None:
                    W.wait(t, "tensor", fb_[0], fb_[1])
                for j in range(8):
                    dc = hb * 8 + j
                    ins = t.transpose(psB[:, hb, j, :], xn_src[:, dc * 128:(dc + 1) * 128], idb[:])
                    if j == 7:
                        ins.then_inc(s_mm.h, 1)
            block.tensor(sec_t)
            s_mm.n += 1
            tr_done = s_mm.n
            def sec_vec3(v, tt=tt, hb=hb, tr_done=tr_done, dst_fm=dst_fm):
                W.wait(v, "vector", s_mm, tr_done)
                v.tensor_copy(
                    dst_fm[:, hb * 8:(hb + 1) * 8, tt * 128:(tt + 1) * 128],
                    psB[:, hb, :, :],
                ).then_inc(s_ev.h, 1)
            block.vector(sec_vec3)
            s_ev.n += 1
            free_psB[hb] = (s_ev, s_ev.n)
            drv.done_val = s_ev.n
        drv.tile_done.append(drv.done_val)
        rms_xnbuf_free[tt % 2] = (s_mm, s_mm.n)

    def _rms_tile(d_src, dst_fm, tt, drv):
        nonlocal free_xstage
        if True:
            slot = tt % 3
            x2 = ytm[1][:, slot, :]
            fx = free_x2[slot]
            xsem = s_x2[slot]
            def sec_g(g, tt=tt, fx=fx, x2=x2, xsem=xsem):
                if fx is not None:
                    W.wait(g, "gpsimd", fx[0], fx[1])
                g.dma_start(out=x2, in_=d_src[tt * 128:(tt + 1) * 128, :]).then_inc(xsem.h, 16)
            block.gpsimd(sec_g)
            xsem.n += 16
            x_loaded = (xsem, xsem.n)

            ph = tt % 2
            fp = rms_statp_free[ph]
            def sec_sq4(sc, x_loaded=x_loaded, ph=ph, fp=fp, x2=x2):
                W.wait(sc, "scalar", x_loaded[0], x_loaded[1])
                W.wait(sc, "scalar", s_x, consts_done[0])
                if fp is not None:
                    W.wait(sc, "scalar", fp[0], fp[1])
                for c in range(4):
                    sc.activation(
                        ytm[0][:, ph, c * 512:(c + 1) * 512], x2[:, c * 512:(c + 1) * 512], AF.Square,
                        accum_out=statp[:, 4 * ph + c:4 * ph + c + 1],
                    ).then_inc(s_ea.h, 1)
            block.scalar(sec_sq4)
            s_ea.n += 4
            sq_done = s_ea.n

            base = s_ev.n
            def sec_vec(v, sq_done=sq_done, ph=ph, base=base):
                W.wait(v, "vector", s_ea, sq_done)
                v.tensor_add(stat[:, 0:1], statp[:, 4 * ph:4 * ph + 1], statp[:, 4 * ph + 1:4 * ph + 2]).then_inc(s_ev.h, 1)
                W.wait(v, "vector", s_ev, base + 1)
                v.tensor_add(stat[:, 0:1], stat[:, 0:1], statp[:, 4 * ph + 2:4 * ph + 3]).then_inc(s_ev.h, 1)
                W.wait(v, "vector", s_ev, base + 2)
                v.tensor_add(stat[:, 0:1], stat[:, 0:1], statp[:, 4 * ph + 3:4 * ph + 4]).then_inc(s_ev.h, 1)
                W.wait(v, "vector", s_ev, base + 3)
                v.tensor_scalar(stat[:, 1:2], stat[:, 0:1], 1.0 / D, EPS, ALU.mult, ALU.add).then_inc(s_ev.h, 1)
            block.vector(sec_vec)
            s_ev.n += 4
            stats_done = s_ev.n
            rms_statp_free[ph] = (s_ev, stats_done)

            def sec_sc(sc, stats_done=stats_done):
                W.wait(sc, "scalar", s_ev, stats_done)
                sc.activation(stat[:, 2:3], stat[:, 1:2], AF.Sqrt).then_inc(s_ea.h, 1)
            block.scalar(sec_sc)
            s_ea.n += 1
            sqrt_done = s_ea.n

            xb = tt % 2
            xn_dst = xntm[:] if xb == 0 else ytm[1][:, 3, :]
            fb2 = rms_xnbuf_free[xb]
            base = s_ev.n
            def sec_vec2(v, sqrt_done=sqrt_done, base=base, x2=x2, xn_dst=xn_dst, fb2=fb2):
                W.wait(v, "vector", s_ea, sqrt_done)
                if fb2 is not None:
                    W.wait(v, "vector", fb2[0], fb2[1])
                v.reciprocal(stat[:, 3:4], stat[:, 2:3]).then_inc(s_ev.h, 1)
                W.wait(v, "vector", s_ev, base + 1)
                v.tensor_mul(stat[:, 4:5], stat[:, 3:4], stat[:, 3:4]).then_inc(s_ev.h, 1)
                W.wait(v, "vector", s_ev, base + 2)
                v.tensor_mul(stat[:, 4:5], stat[:, 4:5], stat[:, 1:2]).then_inc(s_ev.h, 1)
                W.wait(v, "vector", s_ev, base + 3)
                v.tensor_scalar(stat[:, 4:5], stat[:, 4:5], -0.5, 1.5, ALU.mult, ALU.add).then_inc(s_ev.h, 1)
                W.wait(v, "vector", s_ev, base + 4)
                v.tensor_mul(stat[:, 3:4], stat[:, 3:4], stat[:, 4:5]).then_inc(s_ev.h, 1)
                W.wait(v, "vector", s_ev, base + 5)
                v.scalar_tensor_tensor(
                    xn_dst, x2, stat[:, 3:4], rmsb[:], ALU.mult, ALU.mult,
                ).then_inc(s_ev.h, 1)
            block.vector(sec_vec2)
            s_ev.n += 6
            xntm_done = s_ev.n
            free_x2[slot] = (s_ev, xntm_done)

            if rms_pend[0] is not None:
                _emit_rms_tr(*rms_pend[0])
            rms_pend[0] = (dst_fm, tt, xn_dst, xntm_done, drv)

    def mm_tile(lhs_view, rhs_view, extra_waits_t, n_out=512):
        q = psA_ctr[0] % 4
        psA_ctr[0] += 1
        fa = free_psA[q]
        nk = KD
        def sec_t(t, q=q, fa=fa, waits=list(extra_waits_t), n_out=n_out):
            for sem, val in waits:
                W.wait(t, "tensor", sem, val)
            if fa is not None:
                W.wait(t, "tensor", fa[0], fa[1])
            for k in range(nk):
                ins = t.matmul(psA[:, q, 0:n_out], lhs_view(k), rhs_view(k),
                               start=(k == 0), stop=(k == nk - 1))
                if k == nk - 1:
                    ins.then_inc(s_mm.h, 1)
        block.tensor(sec_t)
        s_mm.n += 1
        return q, s_mm.n

    wslot_ctr = [0]

    def load_wblock(src_ap, n_tiles):
        j = wslot_ctr[0] % NWB
        wslot_ctr[0] += 1
        fw = free_wb[j]
        assert fw != "pending", "previous tenant of wb slot has no consumption milestone"
        sem = s_wb[j]
        def sec_g(g, j=j, fw=fw, src_ap=src_ap, n_tiles=n_tiles, sem=sem):
            if fw is not None:
                W.wait(g, "gpsimd", fw[0], fw[1])
            g.dma_start(out=wb[j][:, 0:n_tiles, :], in_=src_ap).then_inc(sem.h, 16)
        block.gpsimd(sec_g)
        sem.n += 16
        free_wb[j] = "pending"
        return j, (sem, sem.n)

    def wb_consumed(slot, sem, val):
        free_wb[slot] = (sem, val)

    load_consts()
    consts_done[0] = s_x.n

    p1_done = RmsDriver(d_xp, xn_fm, ntt=PRE // 128).finish()

    scan_done_ks = {}

    def trio_ks(ks, do_pre, do_own, w0, w1):
        wjf, wvf = load_wblock(d_wf[ks], KD)
        wjzg, wvzg = load_wblock(d_wzg[ks], KD)
        wjz, wvz = load_wblock(d_wz[ks], KD)
        mvf_last, mvzg_last, mvz_last = None, None, None
        if do_pre:
            qf, mvf = mm_tile(
                lambda k, wjf=wjf: wb[wjf][:, k, :],
                lambda k: xn_fm[:, k, 0:PRE],
                [wvf, (s_ev, p1_done)], n_out=PRE)
            mvf_last = mvf
            def sec_f(sc, qf=qf, mvf=mvf, ks=ks):
                W.wait(sc, "scalar", s_mm, mvf)
                sc.activation(
                    fbuf[:, ks, 0:PRE], psA[:, qf, 0:PRE], AF.Sigmoid,
                    bias=fbv[:, ks:ks + 1],
                ).then_inc(s_ea.h, 1)
            block.scalar(sec_f)
            s_ea.n += 1
            free_psA[qf] = (s_ea, s_ea.n)
            f_evac_val = s_ea.n
            qzg, mvzg = mm_tile(
                lambda k, wjzg=wjzg: wb[wjzg][:, k, :],
                lambda k: xn_fm[:, k, 0:PRE], [wvzg], n_out=PRE)
            mvzg_last = mvzg
            fz = free_zgsig[0]
            def sec_zg(sc, qzg=qzg, mvzg=mvzg, fz=fz):
                W.wait(sc, "scalar", s_mm, mvzg)
                if fz is not None:
                    W.wait(sc, "scalar", fz[0], fz[1])
                sc.activation(zgsig[0][:, 0:PRE], psA[:, qzg, 0:PRE], AF.Sigmoid).then_inc(s_ea.h, 1)
            block.scalar(sec_zg)
            s_ea.n += 1
            free_psA[qzg] = (s_ea, s_ea.n)
            zgv = s_ea.n
            qz, mvz = mm_tile(
                lambda k, wjz=wjz: wb[wjz][:, k, :],
                lambda k: xn_fm[:, k, 0:PRE], [wvz], n_out=PRE)
            mvz_last = mvz
            def sec_z(v, qz=qz, mvz=mvz, ks=ks, zgv=zgv):
                W.wait(v, "vector", s_mm, mvz)
                W.wait(v, "vector", s_ea, zgv)
                v.tensor_mul(zbuf[:, ks, 0:PRE], psA[:, qz, 0:PRE], zgsig[0][:, 0:PRE]).then_inc(s_ev.h, 1)
            block.vector(sec_z)
            s_ev.n += 1
            free_psA[qz] = (s_ev, s_ev.n)
            free_zgsig[0] = (s_ev, s_ev.n)
            zv = s_ev.n
            sv = s_ev.n + 1
            def sec_scan(v, ks=ks, f_evac_val=f_evac_val, sv=sv, zv=zv):
                W.wait(v, "vector", s_ea, f_evac_val)
                W.wait(v, "vector", s_ev, zv)
                v.tensor_tensor_scan(
                    hrot[0][:, 0:PRE], fbuf[:, ks, 0:PRE], zbuf[:, ks, 0:PRE],
                    0.0, ALU.mult, ALU.add,
                ).then_inc(s_ev.h, 1)
                W.wait(v, "vector", s_ev, sv)
                v.tensor_copy(carry[:, ks:ks + 1], hrot[0][:, PRE - 1:PRE]).then_inc(s_ev.h, 1)
            block.vector(sec_scan)
            s_ev.n += 2
            scan_done_ks[ks] = s_ev.n
        if do_own:
            sdk = scan_done_ks[ks]
            for tc, sl in enumerate([slice(0, 512), slice(512, 1024)]):
                qf, mvf = mm_tile(
                    lambda k, wjf=wjf: wb[wjf][:, k, :],
                    lambda k, sl=sl: xlnfm[:, k, sl],
                    [wvf, (s_ev, w0 if tc == 0 else w1)])
                mvf_last = mvf
                def sec_f2(sc, qf=qf, mvf=mvf, ks=ks, sl=sl, sdk=sdk):
                    W.wait(sc, "scalar", s_mm, mvf)
                    W.wait(sc, "scalar", s_ev, sdk)
                    sc.activation(
                        fbuf[:, ks, sl], psA[:, qf, :], AF.Sigmoid,
                        bias=fbv[:, ks:ks + 1],
                    ).then_inc(s_ea.h, 1)
                block.scalar(sec_f2)
                s_ea.n += 1
                free_psA[qf] = (s_ea, s_ea.n)
            for tc, sl in enumerate([slice(0, 512), slice(512, 1024)]):
                jz = 1 - tc
                wv_a1 = (s_ev, w0 if tc == 0 else w1)
                qzg, mvzg = mm_tile(
                    lambda k, wjzg=wjzg: wb[wjzg][:, k, :],
                    lambda k, sl=sl: xlnfm[:, k, sl], [wvzg, wv_a1])
                mvzg_last = mvzg
                fz = free_zgsig[jz]
                def sec_zg2(sc, qzg=qzg, mvzg=mvzg, jz=jz, fz=fz):
                    W.wait(sc, "scalar", s_mm, mvzg)
                    if fz is not None:
                        W.wait(sc, "scalar", fz[0], fz[1])
                    sc.activation(zgsig[jz][:], psA[:, qzg, :], AF.Sigmoid).then_inc(s_ea.h, 1)
                block.scalar(sec_zg2)
                s_ea.n += 1
                free_psA[qzg] = (s_ea, s_ea.n)
                zgv2 = s_ea.n
                qz, mvz = mm_tile(
                    lambda k, wjz=wjz: wb[wjz][:, k, :],
                    lambda k, sl=sl: xlnfm[:, k, sl], [wvz, wv_a1])
                mvz_last = mvz
                def sec_z2(v, qz=qz, mvz=mvz, jz=jz, ks=ks, sl=sl, zgv2=zgv2):
                    W.wait(v, "vector", s_mm, mvz)
                    W.wait(v, "vector", s_ea, zgv2)
                    v.tensor_mul(zbuf[:, ks, sl], psA[:, qz, :], zgsig[jz][:]).then_inc(s_ev.h, 1)
                block.vector(sec_z2)
                s_ev.n += 1
                free_psA[qz] = (s_ev, s_ev.n)
                free_zgsig[jz] = (s_ev, s_ev.n)
        wb_consumed(wjf, s_mm, mvf_last)
        wb_consumed(wjzg, s_mm, mvzg_last)
        wb_consumed(wjz, s_mm, mvz_last)

    a1_driver = RmsDriver(d_x, xlnfm, ntt=NTT)
    for ks in range(4):
        trio_ks(ks, True, False, 0, 0)
        a1_driver.step()
        a1_driver.step()
    a1_final = a1_driver.finish()
    a1_t3 = a1_driver.tile_done[3]
    a1_t7 = a1_driver.tile_done[7]
    for ks in range(4, KS):
        trio_ks(ks, True, True, a1_t3, a1_t7)

    if DEBUG_TAPS:
        pv = s_ev.n
        pa = s_ea.n
        def sec_tapp(sy, pv=pv, pa=pa):
            W.wait(sy, "sync", s_ev, pv)
            W.wait(sy, "sync", s_ea, pa)
            sy.dma_start(out=taps["t_pf"], in_=fbuf[:]).then_inc(s_st.h, 16)
            sy.dma_start(out=taps["t_pz"], in_=zbuf[:]).then_inc(s_st.h, 16)
            sy.dma_start(out=taps["t_pc"], in_=carry[:]).then_inc(s_st.h, 16)
        block.sync(sec_tapp)
        s_st.n += 48
        prefix_tap_done = s_st.n
        def sec_guard(v, g=prefix_tap_done):
            W.wait(v, "vector", s_st, g)
        block.vector(sec_guard)
        def sec_guard2(sc, g=prefix_tap_done):
            W.wait(sc, "scalar", s_st, g)
        block.scalar(sec_guard2)

    carry_w = s_ev.n
    def sec_mask(v, cw=carry_w):
        W.wait(v, "vector", s_ev, cw)
        v.tensor_scalar(carry[:], carry[:], maskv[:, 0:1], None, ALU.mult).then_inc(s_ev.h, 1)
    block.vector(sec_mask)
    s_ev.n += 1
    mask_done = s_ev.n

    if DEBUG_TAPS:
        cv = s_ev.n
        def sec_tapc(sy, cv=cv):
            W.wait(sy, "sync", s_ev, cv)
            sy.dma_start(out=taps["t_carry"], in_=carry[:]).then_inc(s_st.h, 16)
        block.sync(sec_tapc)
        s_st.n += 16

    for ks in range(4):
        trio_ks(ks, False, True, a1_final, a1_final)
    if rms_xnbuf_free[0] is not None:
        free_xntm = rms_xnbuf_free[0]
    rms_sq_watermark = s_ea.n

    if DEBUG_TAPS:
        av = a1_final
        def sec_tapxn(sy, av=av):
            W.wait(sy, "sync", s_ev, av)
            sy.dma_start(out=taps["t_xn"], in_=xn_fm[:]).then_inc(s_st.h, 16)
        block.sync(sec_tapxn)
        s_st.n += 16

    a2a_f_done = s_ea.n
    a2a_z_done = s_ev.n

    if DEBUG_TAPS:
        def sec_tapfz(sy, fa=a2a_f_done, za=a2a_z_done):
            W.wait(sy, "sync", s_ea, fa)
            W.wait(sy, "sync", s_ev, za)
            sy.dma_start(out=taps["t_f"], in_=fbuf[:]).then_inc(s_st.h, 16)
            sy.dma_start(out=taps["t_z"], in_=zbuf[:]).then_inc(s_st.h, 16)
        block.sync(sec_tapfz)
        s_st.n += 32


    og_guard = [0]
    mask_done_v = s_ev.n
    for ks in range(KS):
        wja, wva = load_wblock(d_wga[ks], KD)
        wjb, wvb = load_wblock(d_wgb[ks], KD)
        jh = ks % 2
        fh = free_hrot[jh]
        def sec_scan(v, ks=ks, jh=jh, fh=fh, zv=a2a_z_done):
            W.wait(v, "vector", s_ea, a2a_f_done)
            W.wait(v, "vector", s_ev, zv)
            if fh is not None:
                W.wait(v, "vector", fh[0], fh[1])
            v.tensor_tensor_scan(
                hrot[jh][:], fbuf[:, ks, :], zbuf[:, ks, :], carry[:, ks:ks + 1],
                ALU.mult, ALU.add,
            ).then_inc(s_ev.h, 1)
        block.vector(sec_scan)
        s_ev.n += 1
        scan_val = s_ev.n
        for tc in range(2):
            sl = slice(tc * 512, (tc + 1) * 512)
            ja = (ks * 2 + tc) % 3
            qa, mva = mm_tile(
                lambda k, wja=wja: wb[wja][:, k, :],
                lambda k, sl=sl: xlnfm[:, k, sl],
                [wva],
            )
            fg = free_gasil[ja]
            def sec_ga(sc, qa=qa, mva=mva, ja=ja, fg=fg):
                W.wait(sc, "scalar", s_mm, mva)
                if fg is not None:
                    W.wait(sc, "scalar", fg[0], fg[1])
                sc.activation(gasil[ja][:], psA[:, qa, :], AF.Silu).then_inc(s_ea.h, 1)
            block.scalar(sec_ga)
            s_ea.n += 1
            free_psA[qa] = (s_ea, s_ea.n)
            ga_done = s_ea.n
            qb, mvb = mm_tile(
                lambda k, wjb=wjb: wb[wjb][:, k, :],
                lambda k, sl=sl: xlnfm[:, k, sl],
                [wvb],
            )
            ogv = s_ev.n + 1
            def sec_gb(v, qb=qb, mvb=mvb, ja=ja, jh=jh, ks=ks, sl=sl, ga_done=ga_done,
                       scv=scan_val, ogv=ogv, ogg=og_guard[0]):
                W.wait(v, "vector", s_mm, mvb)
                W.wait(v, "vector", s_ea, ga_done)
                W.wait(v, "vector", s_ev, max(scv, ogg))
                v.tensor_mul(zgsig[0][:], psA[:, qb, :], gasil[ja][:]).then_inc(s_ev.h, 1)
                W.wait(v, "vector", s_ev, ogv)
                v.tensor_mul(fbuf[:, ks, sl], zgsig[0][:], hrot[jh][:, sl]).then_inc(s_ev.h, 1)
            block.vector(sec_gb)
            s_ev.n += 2
            og_guard[0] = s_ev.n
            free_psA[qb] = (s_ev, s_ev.n)
            free_gasil[ja] = (s_ev, s_ev.n)
            if tc == 1:
                wb_consumed(wja, s_mm, mva)
                wb_consumed(wjb, s_mm, mvb)
        free_hrot[jh] = (s_ev, s_ev.n)
    gated_done = s_ev.n

    preload_done = [0]
    for tc in range(2):
        for j in range(4):
            tt = tc * 4 + j
            dstt = ytm[tc][:, j, :]
            ch = s_pl[j]
            waits = []
            if ch.n > 0:
                waits.append((ch, ch.n))
            if tc == 0:
                waits.append((s_ea, rms_sq_watermark))
            if tc == 1:
                if j < 3 and free_x2[j] is not None:
                    waits.append(free_x2[j])
                if j == 3 and rms_xnbuf_free[1] is not None:
                    waits.append(rms_xnbuf_free[1])
            def sec_gx(g, tt=tt, dstt=dstt, waits=list(waits), ch=ch):
                for sem, val in waits:
                    W.wait(g, "gpsimd", sem, val)
                g.dma_start(out=dstt, in_=d_x[tt * 128:(tt + 1) * 128, :]).then_inc(ch.h, 16)
            block.gpsimd(sec_gx)
            ch.n += 16
            xv = ch.n
            def sec_x2(v, dstt=dstt, xv=xv, ch=ch):
                W.wait(v, "vector", ch, xv)
                v.tensor_scalar(dstt, dstt, 2.0, None, ALU.mult).then_inc(s_ev.h, 1)
            block.vector(sec_x2)
            s_ev.n += 1
    preload_done[0] = s_ev.n

    if DEBUG_TAPS:
        def sec_tapg(sy, gv=gated_done):
            W.wait(sy, "sync", s_ev, gv)
            sy.dma_start(out=taps["t_gated"], in_=fbuf[:]).then_inc(s_st.h, 16)
        block.sync(sec_tapg)
        s_st.n += 16

    ytm_guard = [0, 0]
    fz_guard = [s_st.n]
    xlnfm_done = {}
    ln1_pe_mm = {}
    statp_guard = [0]
    stat_guard = [0]
    act_out_guard = [0]
    ln1_norm = {}
    y_ready = {}
    hh_last = {}
    hviews_all = {}

    def outproj_both():
        def emit_tr(dc, tc, je, yev):
            yb = tc % 2
            jt = (dc * 2 + tc) % 2
            ft = free_psT[jt]
            def sec_tr(t, je=je, jt=jt, ft=ft, yev=yev):
                W.wait(t, "tensor", s_ea, yev)
                if ft is not None:
                    W.wait(t, "tensor", ft[0], ft[1])
                for j in range(4):
                    ins = t.transpose(psT[jt][:, j, :], evst[je][:, j * 128:(j + 1) * 128], idf[:])
                    if j == 3:
                        ins.then_inc(s_mm.h, 1)
            block.tensor(sec_tr)
            s_mm.n += 1
            trv = s_mm.n
            free_evst[je] = (s_mm, trv)
            def sec_ev(v, jt=jt, dc=dc, trv=trv, yg=ytm_guard[yb], yb=yb, tc=tc,
                       pv=preload_done[0]):
                W.wait(v, "vector", s_mm, trv)
                W.wait(v, "vector", s_st, yg)
                W.wait(v, "vector", s_ev, pv)
                for j in range(4):
                    v.scalar_tensor_tensor(
                        ytm[yb][:, j, dc * 128:(dc + 1) * 128], psT[jt][:, j, :], 1.0,
                        ytm[yb][:, j, dc * 128:(dc + 1) * 128], ALU.mult, ALU.add,
                        accum_out=statq[:, dc * 8 + tc * 4 + j:dc * 8 + tc * 4 + j + 1],
                    ).then_inc(s_ev.h, 1)
            block.vector(sec_ev)
            s_ev.n += 4
            free_psT[jt] = (s_ev, s_ev.n)

        pending_tr = None
        for dc in range(KD):
            wjo, wvo = load_wblock(d_wo[dc], KS)
            mv_last = None
            for tc in range(2):
                sl = slice(tc * 512, (tc + 1) * 512)
                qy, mvy = mm_tile(
                    lambda k, wjo=wjo: wb[wjo][:, k, :],
                    lambda k, sl=sl: fbuf[:, k, sl],
                    [wvo, (s_ev, gated_done)],
                )
                mv_last = mvy
                je = (dc * 2 + tc) % 2
                fe = free_evst[je]
                def sec_y(sc, qy=qy, mvy=mvy, dc=dc, je=je, fe=fe):
                    W.wait(sc, "scalar", s_mm, mvy)
                    if fe is not None:
                        W.wait(sc, "scalar", fe[0], fe[1])
                    sc.activation(
                        evst[je][:], psA[:, qy, :], AF.Identity, bias=obv[:, dc:dc + 1],
                    ).then_inc(s_ea.h, 1)
                block.scalar(sec_y)
                s_ea.n += 1
                free_psA[qy] = (s_ea, s_ea.n)
                yev = s_ea.n
                if pending_tr is not None:
                    emit_tr(*pending_tr)
                pending_tr = (dc, tc, je, yev)
            wb_consumed(wjo, s_mm, mv_last)
        emit_tr(*pending_tr)
        y_ready[0] = s_ev.n
        y_ready[1] = s_ev.n

    def sum_sq_partials(act_done, dve_done=0):
        base = s_ev.n
        def sec_sp(v, act_done=act_done, base=base, dve_done=dve_done):
            W.wait(v, "vector", s_ea, act_done)
            W.wait(v, "vector", s_ev, dve_done)
            v.tensor_add(stat2[:, 8:12], statp[:, 0:16:4], statp[:, 1:16:4]).then_inc(s_ev.h, 1)
            W.wait(v, "vector", s_ev, base + 1)
            v.tensor_add(stat2[:, 8:12], stat2[:, 8:12], statp[:, 2:16:4]).then_inc(s_ev.h, 1)
            W.wait(v, "vector", s_ev, base + 2)
            v.tensor_add(stat[:, 4:8], stat2[:, 8:12], statp[:, 3:16:4]).then_inc(s_ev.h, 1)
        block.vector(sec_sp)
        s_ev.n += 3
        return s_ev.n

    def ln_stats_chain(stats_done):
        base = s_ev.n
        def sec_var(v, sd=stats_done, base=base, aog=act_out_guard[0]):
            W.wait(v, "vector", s_ev, sd)
            W.wait(v, "vector", s_ea, aog)
            v.tensor_scalar(stat2[:, 0:4], stat[:, 0:4], 1.0 / D, None, ALU.mult).then_inc(s_ev.h, 1)
            v.tensor_scalar(stat2[:, 4:8], stat[:, 4:8], 1.0 / D, None, ALU.mult).then_inc(s_ev.h, 1)
            W.wait(v, "vector", s_ev, base + 2)
            v.tensor_mul(stat2[:, 8:12], stat2[:, 0:4], stat2[:, 0:4]).then_inc(s_ev.h, 1)
            W.wait(v, "vector", s_ev, base + 3)
            v.tensor_sub(stat2[:, 4:8], stat2[:, 4:8], stat2[:, 8:12]).then_inc(s_ev.h, 1)
            W.wait(v, "vector", s_ev, base + 4)
            v.tensor_scalar(stat2[:, 4:8], stat2[:, 4:8], 1.0, EPS, ALU.mult, ALU.add).then_inc(s_ev.h, 1)
        block.vector(sec_var)
        s_ev.n += 5
        var_done = s_ev.n
        def sec_sq(sc, var_done=var_done):
            W.wait(sc, "scalar", s_ev, var_done)
            sc.activation(stat2[:, 8:12], stat2[:, 4:8], AF.Sqrt).then_inc(s_ea.h, 1)
        block.scalar(sec_sq)
        s_ea.n += 1
        sq_done = s_ea.n
        base = s_ev.n
        def sec_nr(v, sq_done=sq_done, base=base):
            W.wait(v, "vector", s_ea, sq_done)
            v.reciprocal(stat2[:, 12:16], stat2[:, 8:12]).then_inc(s_ev.h, 1)
            W.wait(v, "vector", s_ev, base + 1)
            v.tensor_mul(stat[:, 8:12], stat2[:, 12:16], stat2[:, 12:16]).then_inc(s_ev.h, 1)
            W.wait(v, "vector", s_ev, base + 2)
            v.tensor_mul(stat[:, 8:12], stat[:, 8:12], stat2[:, 4:8]).then_inc(s_ev.h, 1)
            W.wait(v, "vector", s_ev, base + 3)
            v.tensor_scalar(stat[:, 8:12], stat[:, 8:12], -0.5, 1.5, ALU.mult, ALU.add).then_inc(s_ev.h, 1)
            W.wait(v, "vector", s_ev, base + 4)
            v.tensor_mul(stat2[:, 12:16], stat2[:, 12:16], stat[:, 8:12]).then_inc(s_ev.h, 1)
        block.vector(sec_nr)
        s_ev.n += 5
        return s_ev.n

    def ln1_dve(tc):
        yb = tc % 2
        y_done = y_ready[tc]
        for j in range(2):
            def sec_sq4(sc, j=j, y_done=y_done, yb=yb, spg=statp_guard[0], stg=stat_guard[0]):
                W.wait(sc, "scalar", s_ev, y_done)
                W.wait(sc, "scalar", s_ea, spg)
                W.wait(sc, "scalar", s_ev, stg)
                for c in range(4):
                    sc.activation(
                        fbuf[:, j * 4 + c, 0:512], ytm[yb][:, j, c * 512:(c + 1) * 512], AF.Square,
                        accum_out=statp[:, 4 * j + c:4 * j + c + 1],
                    ).then_inc(s_ea.h, 1)
            block.scalar(sec_sq4)
            s_ea.n += 4
        for j in range(2, 4):
            def sec_sqv(v, j=j, y_done=y_done, yb=yb, spg=statp_guard[0], stg=stat_guard[0]):
                W.wait(v, "vector", s_ev, y_done)
                W.wait(v, "vector", s_ea, spg)
                W.wait(v, "vector", s_ev, stg)
                v.scalar_tensor_tensor(
                    fbuf[:, j * 4, 0:1024], ytm[yb][:, j, 0:1024], 1.0, ytm[yb][:, j, 0:1024],
                    ALU.mult, ALU.mult, accum_out=statp[:, 4 * j:4 * j + 1],
                ).then_inc(s_ev.h, 1)
                v.scalar_tensor_tensor(
                    fbuf[:, j * 4 + 1, 0:1024], ytm[yb][:, j, 1024:2048], 1.0, ytm[yb][:, j, 1024:2048],
                    ALU.mult, ALU.mult, accum_out=statp[:, 4 * j + 1:4 * j + 2],
                ).then_inc(s_ev.h, 1)
                v.memset(statp[:, 4 * j + 2:4 * j + 4], 0.0).then_inc(s_ev.h, 1)
            block.vector(sec_sqv)
            s_ev.n += 3
        base = s_ev.n
        def sec_s1(v, yb=yb, tc=tc, y_done=y_done, base=base, stg=stat_guard[0]):
            W.wait(v, "vector", s_ev, y_done)
            W.wait(v, "vector", s_ev, stg)
            for j in range(4):
                v.reduce_sum(stat[:, j:j + 1], statq[:, tc * 4 + j:128:8], axis=AX.X).then_inc(s_ev.h, 1)
        block.vector(sec_s1)
        s_ev.n += 4
        dve_sq = s_ev.n
        sp_done = sum_sq_partials(s_ea.n, dve_sq)
        r_done = ln_stats_chain(sp_done)
        statp_guard[0] = s_ea.n
        stat_guard[0] = r_done
        for j in range(4):
            tt = tc * 4 + j
            def sec_norm(v, j=j, rv=r_done, yb=yb):
                W.wait(v, "vector", s_ev, rv)
                v.tensor_scalar(
                    ytm[yb][:, j, :], ytm[yb][:, j, :], stat2[:, j:j + 1], stat2[:, 12 + j:13 + j],
                    ALU.subtract, ALU.mult,
                ).then_inc(s_ev.h, 1)
            block.vector(sec_norm)
            s_ev.n += 1
            xlnv = s_ev.n
            ln1_norm[(tc, j)] = xlnv

    def ln1_pe(tc):
        yb = tc % 2
        for j in range(4):
            tt = tc * 4 + j
            xlnv = ln1_norm[(tc, j)]
            for hb in range(2):
                fb_ = free_psB[hb]
                def sec_t(t, hb=hb, fb_=fb_, xlnv=xlnv, j=j, yb=yb):
                    W.wait(t, "tensor", s_ev, xlnv)
                    if fb_ is not None:
                        W.wait(t, "tensor", fb_[0], fb_[1])
                    for jj in range(8):
                        dc = hb * 8 + jj
                        ins = t.transpose(psB[:, hb, jj, :], ytm[yb][:, j, dc * 128:(dc + 1) * 128], idb[:])
                        if jj == 7:
                            ins.then_inc(s_mm.h, 1)
                block.tensor(sec_t)
                s_mm.n += 1
                trv = s_mm.n
                def sec_ev2(v, hb=hb, tt=tt, trv=trv):
                    W.wait(v, "vector", s_mm, trv)
                    v.tensor_copy(
                        xn_fm[:, hb * 8:(hb + 1) * 8, tt * 128:(tt + 1) * 128],
                        psB[:, hb, :, :],
                    ).then_inc(s_ev.h, 1)
                block.vector(sec_ev2)
                s_ev.n += 1
                free_psB[hb] = (s_ev, s_ev.n)
        xlnfm_done[tc] = s_ev.n
        ln1_pe_mm[tc] = s_mm.n

    gu_loaded = [0]
    gu_q = []

    def preload_gu(n):
        for _ in range(n):
            if gu_loaded[0] >= 2 * KI:
                return
            ic = gu_loaded[0] % KI
            wjg, wvg = load_wblock(d_gu[ic, 0], 16)
            wju, wvu = load_wblock(d_gu[ic, 1], 16)
            def sec_m128(v, wjg=wjg, wju=wju, wvg=wvg, wvu=wvu):
                W.wait(v, "vector", wvg[0], wvg[1])
                W.wait(v, "vector", wvu[0], wvu[1])
                v.tensor_scalar(wb[wjg][:, 0:16, :], wb[wjg][:, 0:16, :], -128.0, None, ALU.add).then_inc(s_ev.h, 1)
                v.tensor_scalar(wb[wju][:, 0:16, :], wb[wju][:, 0:16, :], -128.0, None, ALU.add).then_inc(s_ev.h, 1)
            block.vector(sec_m128)
            s_ev.n += 2
            gu_q.append((wjg, wju, s_ev.n))
            gu_loaded[0] += 1

    def ffn_gu(tc):
        xln_ready = xlnfm_done[tc]
        sl = slice(tc * 512, (tc + 1) * 512)
        hviews = []
        for ic in range(KI):
            if ic < 32:
                buf, idx = fbuf, ic
            else:
                buf, idx = zbuf, ic - 32
            hviews.append(buf[:].rearrange("p a (b c) -> p (a b) c", b=2)[:, idx, :])
        hviews_all[tc] = hviews
        hh_done = 0
        for ic in range(KI):
            want = tc * KI + ic + 2
            while gu_loaded[0] < min(want, (tc + 1) * KI):
                preload_gu(1)
            wjg, wju, wready = gu_q.pop(0)
            qg, mvg = mm_tile(
                lambda k, wjg=wjg: wb[wjg][:, k, :],
                lambda k, sl=sl: xn_fm[:, k, sl],
                [(s_ev, wready), (s_ev, xln_ready)],
            )
            jg = ic % 3
            fg = free_gasil[jg]
            def sec_g(sc, qg=qg, mvg=mvg, ic=ic, jg=jg, fg=fg):
                W.wait(sc, "scalar", s_mm, mvg)
                if fg is not None:
                    W.wait(sc, "scalar", fg[0], fg[1])
                sc.activation(
                    gasil[jg][:], psA[:, qg, :], AF.Silu, scale=gscv[:, ic:ic + 1],
                ).then_inc(s_ea.h, 1)
            block.scalar(sec_g)
            s_ea.n += 1
            free_psA[qg] = (s_ea, s_ea.n)
            g_done = s_ea.n
            qu, mvu = mm_tile(
                lambda k, wju=wju: wb[wju][:, k, :],
                lambda k, sl=sl: xn_fm[:, k, sl],
                [],
            )
            def sec_u(v, qu=qu, mvu=mvu, ic=ic, jg=jg, g_done=g_done,
                      hv=hviews[ic], fzg=fz_guard[0]):
                W.wait(v, "vector", s_mm, mvu)
                W.wait(v, "vector", s_ea, g_done)
                W.wait(v, "vector", s_st, fzg)
                v.scalar_tensor_tensor(
                    hv, psA[:, qu, :], uscv[:, ic:ic + 1], gasil[jg][:],
                    ALU.mult, ALU.mult,
                ).then_inc(s_ev.h, 1)
            block.vector(sec_u)
            s_ev.n += 1
            free_psA[qu] = (s_ev, s_ev.n)
            free_gasil[jg] = (s_ev, s_ev.n)
            wb_consumed(wjg, s_mm, mvg)
            wb_consumed(wju, s_mm, mvu)
            hh_done = s_ev.n
        hh_last[tc] = hh_done

    def ffn_down(tc, pump=None):
        yb = tc % 2
        hviews = hviews_all[tc]
        hhv_all = hh_last[tc]

        def emit_dtr(hc, je, dev):
            jt = hc % 2
            ft = free_psT[jt]
            def sec_tr(t, je=je, jt=jt, ft=ft, dev=dev):
                W.wait(t, "tensor", s_ea, dev)
                if ft is not None:
                    W.wait(t, "tensor", ft[0], ft[1])
                for j in range(4):
                    ins = t.transpose(psT[jt][:, j, :], evst[je][:, j * 128:(j + 1) * 128], idf[:])
                    if j == 3:
                        ins.then_inc(s_mm.h, 1)
            block.tensor(sec_tr)
            s_mm.n += 1
            trv = s_mm.n
            free_evst[je] = (s_mm, trv)
            def sec_ev(v, jt=jt, hc=hc, trv=trv, yg=ytm_guard[yb], yb=yb, tc=tc):
                W.wait(v, "vector", s_mm, trv)
                W.wait(v, "vector", s_st, yg)
                for j in range(4):
                    v.scalar_tensor_tensor(
                        ytm[yb][:, j, hc * 128:(hc + 1) * 128], psT[jt][:, j, :], 1.0,
                        ytm[yb][:, j, hc * 128:(hc + 1) * 128], ALU.mult, ALU.add,
                        accum_out=statq[:, hc * 8 + tc * 4 + j:hc * 8 + tc * 4 + j + 1],
                    ).then_inc(s_ev.h, 1)
            block.vector(sec_ev)
            s_ev.n += 4
            free_psT[jt] = (s_ev, s_ev.n)

        pending_dtr = None
        dn_q = []

        def load_dn(hc, g4):
            wj, wv = load_wblock(d_dn[hc, g4], 16)
            def sec_m128b(v, wj=wj, wv=wv):
                W.wait(v, "vector", wv[0], wv[1])
                v.tensor_scalar(wb[wj][:, 0:16, :], wb[wj][:, 0:16, :], -128.0, None, ALU.add).then_inc(s_ev.h, 1)
            block.vector(sec_m128b)
            s_ev.n += 1
            dn_q.append((wj, s_ev.n))

        dn_loaded = [0]

        def preload_dn(n):
            while n > 0 and dn_loaded[0] < 4 * KD:
                load_dn(dn_loaded[0] // 4, dn_loaded[0] % 4)
                dn_loaded[0] += 1
                n -= 1

        preload_dn(3)
        for hc in range(KD):
            q = psA_ctr[0] % 4
            psA_ctr[0] += 1
            fa = free_psA[q]
            for g4 in range(4):
                preload_dn(1)
                wj, wready = dn_q.pop(0)
                def sec_dn(t, q=q, fa=fa, g4=g4, wj=wj, wready=wready, hhv=hhv_all):
                    W.wait(t, "tensor", s_ev, wready)
                    W.wait(t, "tensor", s_ev, hhv)
                    if g4 == 0 and fa is not None:
                        W.wait(t, "tensor", fa[0], fa[1])
                    for k in range(16):
                        ins = t.matmul(psA[:, q, :], wb[wj][:, k, :], hviews[g4 * 16 + k],
                                       start=(g4 == 0 and k == 0), stop=(g4 == 3 and k == 15))
                        if k == 15:
                            ins.then_inc(s_mm.h, 1)
                block.tensor(sec_dn)
                s_mm.n += 1
                wb_consumed(wj, s_mm, s_mm.n)
            mvd = s_mm.n
            je = hc % 2
            fe = free_evst[je]
            def sec_de(sc, q=q, mvd=mvd, hc=hc, je=je, fe=fe):
                W.wait(sc, "scalar", s_mm, mvd)
                if fe is not None:
                    W.wait(sc, "scalar", fe[0], fe[1])
                sc.activation(
                    evst[je][:], psA[:, q, :], AF.Identity, scale=dscv[:, hc:hc + 1],
                ).then_inc(s_ea.h, 1)
            block.scalar(sec_de)
            s_ea.n += 1
            free_psA[q] = (s_ea, s_ea.n)
            dev = s_ea.n
            if pending_dtr is not None:
                emit_dtr(*pending_dtr)
            pending_dtr = (hc, je, dev)
            if pump is not None:
                pump()
        emit_dtr(*pending_dtr)
        return s_ev.n

    def ln2_steps(tc, ffn_done):
        steps = []
        hold = {}
        yb = tc % 2

        def tile_step(j):
            def go():
                nonlocal_state_tile(tc, ffn_done, j, yb)
            return go

        def stats_step():
            sp = sum_sq_partials(s_ea.n, s_ev.n)
            hold["r_done"] = ln_stats_chain(sp)
            def sec_mr(v, rv=hold["r_done"]):
                W.wait(v, "vector", s_ev, rv)
                v.scalar_tensor_tensor(
                    stat[:, 12:16], stat2[:, 0:4], -1.0, stat2[:, 12:16],
                    ALU.mult, ALU.mult,
                ).then_inc(s_ev.h, 1)
            block.vector(sec_mr)
            s_ev.n += 1
            hold["mr_done"] = s_ev.n
            statp_guard[0] = s_ea.n
            stat_guard[0] = hold["r_done"]

        def out_step(j):
            def go():
                nonlocal_state_out(tc, j, yb, hold["r_done"], hold["mr_done"])
            return go

        for j in range(4):
            steps.append(tile_step(j))
        steps.append(stats_step)
        for j in range(4):
            steps.append(out_step(j))
        return steps

    ln2_last_store = [0]

    def nonlocal_state_tile(tc, ffn_done, j, yb):
        nonlocal free_xstage
        if j < 2:
            fx = free_xstage
            def sec_sq4(sc, j=j, fd=ffn_done, yb=yb, tc=tc, spg=statp_guard[0],
                        stg=stat_guard[0]):
                W.wait(sc, "scalar", s_ev, fd)
                W.wait(sc, "scalar", s_ea, spg)
                W.wait(sc, "scalar", s_ev, stg)
                for c in range(4):
                    sc.activation(
                        xn_fm[:, j * 4 + c, tc * 512:(tc + 1) * 512], ytm[yb][:, j, c * 512:(c + 1) * 512],
                        AF.Square,
                        accum_out=statp[:, 4 * j + c:4 * j + c + 1],
                    ).then_inc(s_ea.h, 1)
            block.scalar(sec_sq4)
            s_ea.n += 4
            def sec_s1a(v, j=j, tc=tc, fd=ffn_done, stg=stat_guard[0]):
                W.wait(v, "vector", s_ev, fd)
                W.wait(v, "vector", s_ev, stg)
                v.reduce_sum(stat[:, j:j + 1], statq[:, tc * 4 + j:128:8], axis=AX.X).then_inc(s_ev.h, 1)
            block.vector(sec_s1a)
            s_ev.n += 1
        else:
            def sec_sqv(v, j=j, fd=ffn_done, yb=yb, tc=tc, spg=statp_guard[0],
                        stg=stat_guard[0]):
                W.wait(v, "vector", s_ev, fd)
                W.wait(v, "vector", s_ea, spg)
                W.wait(v, "vector", s_ev, stg)
                v.scalar_tensor_tensor(
                    xn_fm[:, j * 4 + tc * 2, :], ytm[yb][:, j, 0:1024], 1.0,
                    ytm[yb][:, j, 0:1024], ALU.mult, ALU.mult,
                    accum_out=statp[:, 4 * j:4 * j + 1],
                ).then_inc(s_ev.h, 1)
                v.scalar_tensor_tensor(
                    xn_fm[:, j * 4 + tc * 2 + 1, :], ytm[yb][:, j, 1024:2048], 1.0,
                    ytm[yb][:, j, 1024:2048], ALU.mult, ALU.mult,
                    accum_out=statp[:, 4 * j + 1:4 * j + 2],
                ).then_inc(s_ev.h, 1)
                v.memset(statp[:, 4 * j + 2:4 * j + 4], 0.0).then_inc(s_ev.h, 1)
            block.vector(sec_sqv)
            s_ev.n += 3
            def sec_s1(v, j=j, tc=tc, fd=ffn_done, stg=stat_guard[0]):
                W.wait(v, "vector", s_ev, fd)
                W.wait(v, "vector", s_ev, stg)
                v.reduce_sum(stat[:, j:j + 1], statq[:, tc * 4 + j:128:8], axis=AX.X).then_inc(s_ev.h, 1)
            block.vector(sec_s1)
            s_ev.n += 1

    qst = {}

    def nonlocal_state_out(tc, j, yb, r_done, mr_done):
        nonlocal free_xstage
        tt = tc * 4 + j
        for c in range(4):
            if tc == 1:
                k = (j * 2 + c // 2) % 2
                ch = c % 2
                if k == 0:
                    reg, key = xstage[:, ch * 512:(ch + 1) * 512], ("x", ch)
                else:
                    reg, key = evst[ch][:], ("e", ch)
            else:
                ch = c
                reg, key = xstage[:, c * 512:(c + 1) * 512], ("x", c)
            qs = qst.get(key)
            src = ytm[yb][:, j, c * 512:(c + 1) * 512]
            dst = d_out[tt * 128:(tt + 1) * 128, c * 512:(c + 1) * 512]
            if c % 2 == 0:
                def sec_oa(sc, j=j, rv=mr_done, qs=qs, reg=reg, src=src):
                    W.wait(sc, "scalar", s_ev, rv)
                    if qs is not None:
                        W.wait(sc, "scalar", qs[0], qs[1])
                    sc.activation(
                        reg, src, AF.Identity,
                        scale=stat2[:, 12 + j:13 + j], bias=stat[:, 12 + j:13 + j],
                    ).then_inc(s_ea.h, 1)
                block.scalar(sec_oa)
                s_ea.n += 1
                pv = (s_ea, s_ea.n)
            else:
                def sec_ov(v, j=j, rv=r_done, qs=qs, reg=reg, src=src):
                    W.wait(v, "vector", s_ev, rv)
                    if qs is not None:
                        W.wait(v, "vector", qs[0], qs[1])
                    v.tensor_scalar(
                        reg, src, stat2[:, j:j + 1], stat2[:, 12 + j:13 + j],
                        ALU.subtract, ALU.mult,
                    ).then_inc(s_ev.h, 1)
                block.vector(sec_ov)
                s_ev.n += 1
                pv = (s_ev, s_ev.n)
            osem = s_oq[c]
            if c % 2 == 0:
                def sec_sta(sy, pv=pv, dst=dst, reg=reg, osem=osem):
                    W.wait(sy, "sync", pv[0], pv[1])
                    sy.dma_start(out=dst, in_=reg).then_inc(osem.h, 16)
                block.sync(sec_sta)
            else:
                def sec_stg(g, pv=pv, dst=dst, reg=reg, osem=osem):
                    W.wait(g, "gpsimd", pv[0], pv[1])
                    g.dma_start(out=dst, in_=reg).then_inc(osem.h, 16)
                block.gpsimd(sec_stg)
            osem.n += 16
            qst[key] = (osem, osem.n)
        act_out_guard[0] = s_ea.n
        ytm_guard[yb] = s_st.n

    outproj_both()
    ln1_dve(0)
    ln1_pe(0)
    ln1_dve(1)
    ffn_gu(0)
    ln1_pe(1)
    f0 = ffn_down(0)
    ffn_gu(1)
    steps0 = ln2_steps(0, f0)
    sit = iter(steps0)
    def pump():
        s = next(sit, None)
        if s is not None:
            s()
    f1 = ffn_down(1, pump=pump)
    for s in sit:
        s()
    for s in ln2_steps(1, f1):
        s()

    def sec_fin(sy):
        sy.wait_ge(s_st.h, s_st.n)
        for c in range(4):
            sy.wait_ge(s_oq[c].h, s_oq[c].n)
    block.sync(sec_fin)
    block.flush()

    es.close()
    return nc


_nc_cache = None


def _get_program():
    global _nc_cache
    if _nc_cache is None:
        _nc_cache = build_program()
    return _nc_cache


def _pack_proj(Wnp):
    WT = np.ascontiguousarray(Wnp.T)
    t = WT.reshape(KD, 128, KS, 128)
    return np.ascontiguousarray(t.transpose(2, 1, 0, 3))


def _pack_wout(Wnp):
    WoT = np.ascontiguousarray(Wnp.T)
    t = WoT.reshape(KS, 128, KD, 128)
    return np.ascontiguousarray(t.transpose(2, 1, 0, 3))


def _int32_to_u8(a):
    a = np.ascontiguousarray(np.asarray(a, dtype=np.int32))
    assert a.dtype.byteorder in ("=", "<")
    return a.view(np.uint8).reshape(*a.shape, 4)[..., 0]


def _pack_gu(gate_w, up_w):
    def tiles(Wq):
        WT = np.ascontiguousarray(Wq.T)
        t = WT.reshape(KD, 128, KI, 128)
        return np.ascontiguousarray(t.transpose(2, 1, 0, 3))
    g = tiles(_int32_to_u8(gate_w))
    u = tiles(_int32_to_u8(up_w))
    return np.ascontiguousarray(np.stack([g, u], axis=1))


def _pack_down(down_w):
    Wq = _int32_to_u8(down_w)
    WT = np.ascontiguousarray(Wq.T)
    t = WT.reshape(KI, 128, KD, 128)
    t = t.transpose(2, 0, 1, 3)
    t = t.reshape(KD, 4, 16, 128, 128).transpose(0, 1, 3, 2, 4)
    return np.ascontiguousarray(t)


def kernel(L0_attn_x, L0_attn_rms_w, L0_attn_Wf_w, L0_attn_Wf_b, L0_attn_Wzg_w,
           L0_attn_Wz_w, L0_attn_Wga_w, L0_attn_Wgb_w, L0_attn_Wout_w,
           L0_attn_Wout_b, L0_ffn_gate_w, L0_ffn_gate_scale, L0_ffn_up_w,
           L0_ffn_up_scale, L0_ffn_down_w, L0_ffn_down_scale):
    x = np.ascontiguousarray(np.asarray(L0_attn_x, dtype=np.float32))
    f32 = lambda a: np.ascontiguousarray(np.asarray(a, dtype=np.float32))
    rms_w = f32(L0_attn_rms_w)

    shared = {
        "rms_bcast": np.ascontiguousarray(np.tile(rms_w[None, :], (128, 1)).astype(ml_dtypes.bfloat16)),
        "ident_bf": np.eye(128, dtype=ml_dtypes.bfloat16),
        "ident_f32": np.eye(128, dtype=np.float32),
        "f_bias": np.ascontiguousarray(f32(L0_attn_Wf_b).reshape(KS, 128).T),
        "out_bias": np.ascontiguousarray(f32(L0_attn_Wout_b).reshape(KD, 128).T),
        "gate_sc": np.ascontiguousarray(f32(L0_ffn_gate_scale).reshape(KI, 128).T),
        "up_sc": np.ascontiguousarray(f32(L0_ffn_up_scale).reshape(KI, 128).T),
        "down_sc": np.ascontiguousarray(f32(L0_ffn_down_scale).reshape(KD, 128).T),
        "wf": _pack_proj(f32(L0_attn_Wf_w)),
        "wzg": _pack_proj(f32(L0_attn_Wzg_w)),
        "wz": _pack_proj(f32(L0_attn_Wz_w)),
        "wga": _pack_proj(f32(L0_attn_Wga_w)),
        "wgb": _pack_proj(f32(L0_attn_Wgb_w)),
        "wo": _pack_wout(f32(L0_attn_Wout_w)),
        "wgu": _pack_gu(L0_ffn_gate_w, L0_ffn_up_w),
        "wdn": _pack_down(L0_ffn_down_w),
    }

    in_maps = []
    for core in range(8):
        b, half = core // 2, core % 2
        m = dict(shared)
        m["x_own"] = np.ascontiguousarray(x[b, half * NT:(half + 1) * NT, :])
        m["x_pre"] = np.ascontiguousarray(x[b, NT - PRE:NT, :])
        m["maskv"] = np.full((128, 1), 1.0 if half == 1 else 0.0, np.float32)
        in_maps.append(m)

    nc = _get_program()
    res = run_bass_kernel_spmd(nc, in_maps, core_ids=list(range(8)))

    out = np.empty((B, T, D), dtype=np.float32)
    for core in range(8):
        b, half = core // 2, core % 2
        out[b, half * NT:(half + 1) * NT, :] = res.results[core]["out"]
    return out


# version 25
# speedup vs baseline: 1.0266x; 1.0266x over previous
import contextlib
import numpy as np
import ml_dtypes
import concourse.bass as bass
import concourse.mybir as mybir
from concourse.bass_utils import run_bass_kernel_spmd

F32 = mybir.dt.float32
BF16 = mybir.dt.bfloat16
U8 = mybir.dt.uint8
AF = mybir.ActivationFunctionType
ALU = mybir.AluOpType
AX = mybir.AxisListType

B, T, D, S, I = 4, 2048, 2048, 2048, 8192
NT = 1024
NTT = NT // 128
KD = D // 128
KS = S // 128
KI = I // 128
EPS = 1e-5
PRE = 128

DEBUG_TAPS = False


class Sem:
    def __init__(self, es, nc, name):
        self.h = es.enter_context(nc.semaphore(name))
        self.n = 0
        self.name = name

    def inc(self, instr, k=1):
        instr.then_inc(self.h, k)
        self.n += k
        return self.n


class Waiter:

    def __init__(self):
        self.seen = {}

    def wait(self, eng, engname, sem, val):
        if val is None or val <= 0:
            return
        key = (engname, sem.name)
        if self.seen.get(key, -1) >= val:
            return
        eng.wait_ge(sem.h, val)
        self.seen[key] = val


def build_program():
    nc = bass.Bass("TRN2", target_bir_lowering=False)
    es = contextlib.ExitStack()

    d_x = nc.dram_tensor("x_own", [NT, D], F32, kind="ExternalInput").ap()
    d_xp = nc.dram_tensor("x_pre", [PRE, D], F32, kind="ExternalInput").ap()
    d_mask = nc.dram_tensor("maskv", [128, 1], F32, kind="ExternalInput").ap()
    d_rmsb = nc.dram_tensor("rms_bcast", [128, D], BF16, kind="ExternalInput").ap()
    d_idb = nc.dram_tensor("ident_bf", [128, 128], BF16, kind="ExternalInput").ap()
    d_idf = nc.dram_tensor("ident_f32", [128, 128], F32, kind="ExternalInput").ap()
    d_fb = nc.dram_tensor("f_bias", [128, KS], F32, kind="ExternalInput").ap()
    d_ob = nc.dram_tensor("out_bias", [128, KD], F32, kind="ExternalInput").ap()
    d_gsc = nc.dram_tensor("gate_sc", [128, KI], F32, kind="ExternalInput").ap()
    d_usc = nc.dram_tensor("up_sc", [128, KI], F32, kind="ExternalInput").ap()
    d_dsc = nc.dram_tensor("down_sc", [128, KD], F32, kind="ExternalInput").ap()
    d_wf = nc.dram_tensor("wf", [KS, 128, KD, 128], F32, kind="ExternalInput").ap()
    d_wzg = nc.dram_tensor("wzg", [KS, 128, KD, 128], F32, kind="ExternalInput").ap()
    d_wz = nc.dram_tensor("wz", [KS, 128, KD, 128], F32, kind="ExternalInput").ap()
    d_wga = nc.dram_tensor("wga", [KS, 128, KD, 128], F32, kind="ExternalInput").ap()
    d_wgb = nc.dram_tensor("wgb", [KS, 128, KD, 128], F32, kind="ExternalInput").ap()
    d_wo = nc.dram_tensor("wo", [KD, 128, KS, 128], F32, kind="ExternalInput").ap()
    d_gu = nc.dram_tensor("wgu", [KI, 2, 128, 16, 128], U8, kind="ExternalInput").ap()
    d_dn = nc.dram_tensor("wdn", [KD, 4, 128, 16, 128], U8, kind="ExternalInput").ap()

    d_out = nc.dram_tensor("out", [NT, D], F32, kind="ExternalOutput").ap()

    taps = {}
    if DEBUG_TAPS:
        taps["t_xn"] = nc.dram_tensor("t_xn", [128, KD, NT], BF16, kind="ExternalOutput").ap()
        taps["t_f"] = nc.dram_tensor("t_f", [128, KS, NT], BF16, kind="ExternalOutput").ap()
        taps["t_z"] = nc.dram_tensor("t_z", [128, KS, NT], BF16, kind="ExternalOutput").ap()
        taps["t_carry"] = nc.dram_tensor("t_carry", [128, KS], F32, kind="ExternalOutput").ap()
        taps["t_gated"] = nc.dram_tensor("t_gated", [128, KS, NT], BF16, kind="ExternalOutput").ap()
        taps["t_xln"] = nc.dram_tensor("t_xln", [NT, D], F32, kind="ExternalOutput").ap()
        taps["t_pf"] = nc.dram_tensor("t_pf", [128, KS, NT], BF16, kind="ExternalOutput").ap()
        taps["t_pz"] = nc.dram_tensor("t_pz", [128, KS, NT], BF16, kind="ExternalOutput").ap()
        taps["t_pc"] = nc.dram_tensor("t_pc", [128, KS], F32, kind="ExternalOutput").ap()
        taps["t_y"] = nc.dram_tensor("t_y", [128, 4, D], F32, kind="ExternalOutput").ap()
        taps["t_u"] = nc.dram_tensor("t_u", [128, 4, D], F32, kind="ExternalOutput").ap()
        taps["t_stat"] = nc.dram_tensor("t_stat", [128, 32], F32, kind="ExternalOutput").ap()

    sb = lambda name, shape, dt: es.enter_context(nc.sbuf_tensor(name, shape, dt))
    xn_fm = sb("xn_fm", [128, KD, NT], BF16)
    fbuf = sb("fbuf", [128, KS, NT], BF16)
    zbuf = sb("zbuf", [128, KS, NT], BF16)
    xlnfm = sb("xlnfm", [128, KD, NT], BF16)
    ytm = [sb(f"ytm{j}", [128, 4, D], BF16) for j in range(2)]
    NWB = 4
    wb = [sb(f"wb{j}", [128, 16, 128], BF16) for j in range(NWB)]
    xstage = sb("xstage", [128, D], F32)
    xntm = sb("xntm", [128, D], BF16)
    zgsig = [sb(f"zgsig{j}", [128, 512], BF16) for j in range(2)]
    gasil = [sb(f"gasil{j}", [128, 512], BF16) for j in range(3)]
    evst = [sb(f"evst{j}", [128, 512], F32) for j in range(2)]
    hrot = [sb(f"hrot{j}", [128, NT], BF16) for j in range(2)]
    carry = sb("carry", [128, KS], F32)
    rmsb = sb("rmsb", [128, D], BF16)
    idb = sb("idb", [128, 128], BF16)
    idf = sb("idf", [128, 128], F32)
    fbv = sb("fbv", [128, KS], F32)
    obv = sb("obv", [128, KD], F32)
    gscv = sb("gscv", [128, KI], F32)
    uscv = sb("uscv", [128, KI], F32)
    dscv = sb("dscv", [128, KD], F32)
    maskv = sb("maskv_sb", [128, 1], F32)
    stat = sb("stat", [128, 16], F32)
    stat2 = sb("stat2", [128, 16], F32)
    statp = sb("statp", [128, 16], F32)
    statq = sb("statq", [128, 128], F32)

    psA = es.enter_context(nc.psum_tensor("psA", [128, 4, 512], F32))
    psT = [es.enter_context(nc.psum_tensor(f"psT{j}", [128, 4, 128], F32)) for j in range(2)]
    psB = es.enter_context(nc.psum_tensor("psB", [128, 2, 8, 128], BF16))

    s_w = Sem(es, nc, "s_w")
    s_wb = [Sem(es, nc, f"s_wb{j}") for j in range(4)]
    s_x2 = [Sem(es, nc, f"s_x2{j}") for j in range(3)]
    s_pl = [Sem(es, nc, f"s_pl{j}") for j in range(4)]
    s_x = Sem(es, nc, "s_x")
    s_mm = Sem(es, nc, "s_mm")
    s_ea = Sem(es, nc, "s_ea")
    s_ev = Sem(es, nc, "s_ev")
    s_st = Sem(es, nc, "s_st")
    s_oq = [Sem(es, nc, f"s_oq{c}") for c in range(4)]

    W = Waiter()

    free_psA = [None] * 4
    free_psT = [None] * 2
    free_psB = [None] * 2
    free_wb = [None] * NWB
    free_zgsig = [None] * 2
    free_gasil = [None] * 3
    free_evst = [None] * 2
    free_hrot = [None] * 2
    free_xstage = None
    free_xntm = None

    psA_ctr = [0]
    free_x2 = [None] * 4

    _block = es.enter_context(nc.Block())

    class _EngineBuffer:

        def __init__(self):
            self.pend = {k: [] for k in ("sync", "gpsimd", "scalar", "tensor", "vector")}

        def _add(self, eng, fn):
            self.pend[eng].append(fn)
            if len(self.pend[eng]) >= 48:
                self.flush(eng)

        def sync(self, fn):
            self._add("sync", fn)

        def gpsimd(self, fn):
            self._add("gpsimd", fn)

        def scalar(self, fn):
            self._add("scalar", fn)

        def tensor(self, fn):
            self._add("tensor", fn)

        def vector(self, fn):
            self._add("vector", fn)

        def flush(self, eng=None):
            engs = [eng] if eng else list(self.pend)
            for e in engs:
                fns = self.pend[e]
                if not fns:
                    continue
                self.pend[e] = []
                getattr(_block, e)(lambda handle, fns=fns: [f(handle) for f in fns])

    block = _EngineBuffer()


    def load_consts():
        def sec(sy):
            sy.dma_start(out=rmsb[:], in_=d_rmsb).then_inc(s_x.h, 16)
            sy.dma_start(out=idb[:], in_=d_idb).then_inc(s_x.h, 16)
            sy.dma_start(out=idf[:], in_=d_idf).then_inc(s_x.h, 16)
            sy.dma_start(out=fbv[:], in_=d_fb).then_inc(s_x.h, 16)
            sy.dma_start(out=obv[:], in_=d_ob).then_inc(s_x.h, 16)
            sy.dma_start(out=gscv[:], in_=d_gsc).then_inc(s_x.h, 16)
            sy.dma_start(out=uscv[:], in_=d_usc).then_inc(s_x.h, 16)
            sy.dma_start(out=dscv[:], in_=d_dsc).then_inc(s_x.h, 16)
            sy.dma_start(out=maskv[:], in_=d_mask).then_inc(s_x.h, 16)
        block.sync(sec)
        for _ in range(9):
            s_x.n += 16
    consts_done = [0]

    class RmsDriver:

        def __init__(self, d_src, dst_fm, ntt=NTT):
            self.d_src = d_src
            self.dst = dst_fm
            self.ntt = ntt
            self.tt = 0
            self.done_val = 0
            self.tile_done = []

        def step(self):
            if self.tt >= self.ntt:
                return False
            _rms_tile(self.d_src, self.dst, self.tt, self)
            self.tt += 1
            return True

        def finish(self):
            while self.step():
                pass
            if rms_pend[0] is not None:
                _emit_rms_tr(*rms_pend[0])
                rms_pend[0] = None
            return self.done_val

    rms_xnbuf_free = [None, None]
    rms_statp_free = [None, None]
    rms_pend = [None]

    def _emit_rms_tr(dst_fm, tt, xn_src, xnv, drv):
        for hb in range(2):
            fb_ = free_psB[hb]
            def sec_t(t, hb=hb, fb_=fb_, xnv=xnv, xn_src=xn_src):
                W.wait(t, "tensor", s_ev, xnv)
                W.wait(t, "tensor", s_x, consts_done[0])
                if fb_ is not None:
                    W.wait(t, "tensor", fb_[0], fb_[1])
                for j in range(8):
                    dc = hb * 8 + j
                    ins = t.transpose(psB[:, hb, j, :], xn_src[:, dc * 128:(dc + 1) * 128], idb[:])
                    if j == 7:
                        ins.then_inc(s_mm.h, 1)
            block.tensor(sec_t)
            s_mm.n += 1
            tr_done = s_mm.n
            def sec_vec3(v, tt=tt, hb=hb, tr_done=tr_done, dst_fm=dst_fm):
                W.wait(v, "vector", s_mm, tr_done)
                v.tensor_copy(
                    dst_fm[:, hb * 8:(hb + 1) * 8, tt * 128:(tt + 1) * 128],
                    psB[:, hb, :, :],
                ).then_inc(s_ev.h, 1)
            block.vector(sec_vec3)
            s_ev.n += 1
            free_psB[hb] = (s_ev, s_ev.n)
            drv.done_val = s_ev.n
        drv.tile_done.append(drv.done_val)
        rms_xnbuf_free[tt % 2] = (s_mm, s_mm.n)

    def _rms_tile(d_src, dst_fm, tt, drv):
        nonlocal free_xstage
        if True:
            if getattr(drv, "via_sync", False):
                slot = tt % 3
                x2 = xstage[:]
                xsem = s_x2[slot]
                def sec_gs(sy, tt=tt, x2=x2, xsem=xsem, d_src=d_src):
                    sy.dma_start(out=x2, in_=d_src[tt * 128:(tt + 1) * 128, :]).then_inc(xsem.h, 16)
                block.sync(sec_gs)
                xsem.n += 16
                x_loaded = (xsem, xsem.n)
            else:
                slot = tt % 3
                x2 = ytm[1][:, slot, :]
                fx = free_x2[slot]
                xsem = s_x2[slot]
                def sec_g(g, tt=tt, fx=fx, x2=x2, xsem=xsem):
                    if fx is not None:
                        W.wait(g, "gpsimd", fx[0], fx[1])
                    g.dma_start(out=x2, in_=d_src[tt * 128:(tt + 1) * 128, :]).then_inc(xsem.h, 16)
                block.gpsimd(sec_g)
                xsem.n += 16
                x_loaded = (xsem, xsem.n)

            ph = tt % 2
            fp = rms_statp_free[ph]
            def sec_sq4(sc, x_loaded=x_loaded, ph=ph, fp=fp, x2=x2):
                W.wait(sc, "scalar", x_loaded[0], x_loaded[1])
                if fp is not None:
                    W.wait(sc, "scalar", fp[0], fp[1])
                for c in range(4):
                    sc.activation(
                        ytm[0][:, ph, c * 512:(c + 1) * 512], x2[:, c * 512:(c + 1) * 512], AF.Square,
                        accum_out=statp[:, 4 * ph + c:4 * ph + c + 1],
                    ).then_inc(s_ea.h, 1)
            block.scalar(sec_sq4)
            s_ea.n += 4
            sq_done = s_ea.n

            base = s_ev.n
            def sec_vec(v, sq_done=sq_done, ph=ph, base=base):
                W.wait(v, "vector", s_ea, sq_done)
                v.tensor_add(stat[:, 0:1], statp[:, 4 * ph:4 * ph + 1], statp[:, 4 * ph + 1:4 * ph + 2]).then_inc(s_ev.h, 1)
                W.wait(v, "vector", s_ev, base + 1)
                v.tensor_add(stat[:, 0:1], stat[:, 0:1], statp[:, 4 * ph + 2:4 * ph + 3]).then_inc(s_ev.h, 1)
                W.wait(v, "vector", s_ev, base + 2)
                v.tensor_add(stat[:, 0:1], stat[:, 0:1], statp[:, 4 * ph + 3:4 * ph + 4]).then_inc(s_ev.h, 1)
                W.wait(v, "vector", s_ev, base + 3)
                v.tensor_scalar(stat[:, 1:2], stat[:, 0:1], 1.0 / D, EPS, ALU.mult, ALU.add).then_inc(s_ev.h, 1)
            block.vector(sec_vec)
            s_ev.n += 4
            stats_done = s_ev.n
            rms_statp_free[ph] = (s_ev, stats_done)

            def sec_sc(sc, stats_done=stats_done):
                W.wait(sc, "scalar", s_ev, stats_done)
                sc.activation(stat[:, 2:3], stat[:, 1:2], AF.Sqrt).then_inc(s_ea.h, 1)
            block.scalar(sec_sc)
            s_ea.n += 1
            sqrt_done = s_ea.n

            xb = tt % 2
            xn_dst = xntm[:] if xb == 0 else ytm[1][:, 3, :]
            fb2 = rms_xnbuf_free[xb]
            base = s_ev.n
            def sec_vec2(v, sqrt_done=sqrt_done, base=base, x2=x2, xn_dst=xn_dst, fb2=fb2):
                W.wait(v, "vector", s_ea, sqrt_done)
                W.wait(v, "vector", s_x, consts_done[0])
                if fb2 is not None:
                    W.wait(v, "vector", fb2[0], fb2[1])
                v.reciprocal(stat[:, 3:4], stat[:, 2:3]).then_inc(s_ev.h, 1)
                W.wait(v, "vector", s_ev, base + 1)
                v.tensor_mul(stat[:, 4:5], stat[:, 3:4], stat[:, 3:4]).then_inc(s_ev.h, 1)
                W.wait(v, "vector", s_ev, base + 2)
                v.tensor_mul(stat[:, 4:5], stat[:, 4:5], stat[:, 1:2]).then_inc(s_ev.h, 1)
                W.wait(v, "vector", s_ev, base + 3)
                v.tensor_scalar(stat[:, 4:5], stat[:, 4:5], -0.5, 1.5, ALU.mult, ALU.add).then_inc(s_ev.h, 1)
                W.wait(v, "vector", s_ev, base + 4)
                v.tensor_mul(stat[:, 3:4], stat[:, 3:4], stat[:, 4:5]).then_inc(s_ev.h, 1)
                W.wait(v, "vector", s_ev, base + 5)
                v.scalar_tensor_tensor(
                    xn_dst, x2, stat[:, 3:4], rmsb[:], ALU.mult, ALU.mult,
                ).then_inc(s_ev.h, 1)
            block.vector(sec_vec2)
            s_ev.n += 6
            xntm_done = s_ev.n
            free_x2[slot] = (s_ev, xntm_done)

            if rms_pend[0] is not None:
                _emit_rms_tr(*rms_pend[0])
            rms_pend[0] = (dst_fm, tt, xn_dst, xntm_done, drv)

    def mm_tile(lhs_view, rhs_view, extra_waits_t, n_out=512):
        q = psA_ctr[0] % 4
        psA_ctr[0] += 1
        fa = free_psA[q]
        nk = KD
        def sec_t(t, q=q, fa=fa, waits=list(extra_waits_t), n_out=n_out):
            for sem, val in waits:
                W.wait(t, "tensor", sem, val)
            if fa is not None:
                W.wait(t, "tensor", fa[0], fa[1])
            for k in range(nk):
                ins = t.matmul(psA[:, q, 0:n_out], lhs_view(k), rhs_view(k),
                               start=(k == 0), stop=(k == nk - 1))
                if k == nk - 1:
                    ins.then_inc(s_mm.h, 1)
        block.tensor(sec_t)
        s_mm.n += 1
        return q, s_mm.n

    wslot_ctr = [0]

    def load_wblock(src_ap, n_tiles):
        j = wslot_ctr[0] % NWB
        wslot_ctr[0] += 1
        fw = free_wb[j]
        assert fw != "pending", "previous tenant of wb slot has no consumption milestone"
        sem = s_wb[j]
        def sec_g(g, j=j, fw=fw, src_ap=src_ap, n_tiles=n_tiles, sem=sem):
            if fw is not None:
                W.wait(g, "gpsimd", fw[0], fw[1])
            g.dma_start(out=wb[j][:, 0:n_tiles, :], in_=src_ap).then_inc(sem.h, 16)
        block.gpsimd(sec_g)
        sem.n += 16
        free_wb[j] = "pending"
        return j, (sem, sem.n)

    def wb_consumed(slot, sem, val):
        free_wb[slot] = (sem, val)

    load_consts()
    consts_done[0] = s_x.n

    _p1drv = RmsDriver(d_xp, xn_fm, ntt=PRE // 128)
    _p1drv.via_sync = True
    p1_done = _p1drv.finish()

    scan_done_ks = {}

    def trio_ks(ks, do_pre, do_own, w0, w1):
        wjf, wvf = load_wblock(d_wf[ks], KD)
        wjzg, wvzg = load_wblock(d_wzg[ks], KD)
        wjz, wvz = load_wblock(d_wz[ks], KD)
        mvf_last, mvzg_last, mvz_last = None, None, None
        if do_pre:
            qf, mvf = mm_tile(
                lambda k, wjf=wjf: wb[wjf][:, k, :],
                lambda k: xn_fm[:, k, 0:PRE],
                [wvf, (s_ev, p1_done)], n_out=PRE)
            mvf_last = mvf
            def sec_f(sc, qf=qf, mvf=mvf, ks=ks):
                W.wait(sc, "scalar", s_mm, mvf)
                sc.activation(
                    fbuf[:, ks, 0:PRE], psA[:, qf, 0:PRE], AF.Sigmoid,
                    bias=fbv[:, ks:ks + 1],
                ).then_inc(s_ea.h, 1)
            block.scalar(sec_f)
            s_ea.n += 1
            free_psA[qf] = (s_ea, s_ea.n)
            f_evac_val = s_ea.n
            qzg, mvzg = mm_tile(
                lambda k, wjzg=wjzg: wb[wjzg][:, k, :],
                lambda k: xn_fm[:, k, 0:PRE], [wvzg], n_out=PRE)
            mvzg_last = mvzg
            fz = free_zgsig[0]
            def sec_zg(sc, qzg=qzg, mvzg=mvzg, fz=fz):
                W.wait(sc, "scalar", s_mm, mvzg)
                if fz is not None:
                    W.wait(sc, "scalar", fz[0], fz[1])
                sc.activation(zgsig[0][:, 0:PRE], psA[:, qzg, 0:PRE], AF.Sigmoid).then_inc(s_ea.h, 1)
            block.scalar(sec_zg)
            s_ea.n += 1
            free_psA[qzg] = (s_ea, s_ea.n)
            zgv = s_ea.n
            qz, mvz = mm_tile(
                lambda k, wjz=wjz: wb[wjz][:, k, :],
                lambda k: xn_fm[:, k, 0:PRE], [wvz], n_out=PRE)
            mvz_last = mvz
            def sec_z(v, qz=qz, mvz=mvz, ks=ks, zgv=zgv):
                W.wait(v, "vector", s_mm, mvz)
                W.wait(v, "vector", s_ea, zgv)
                v.tensor_mul(zbuf[:, ks, 0:PRE], psA[:, qz, 0:PRE], zgsig[0][:, 0:PRE]).then_inc(s_ev.h, 1)
            block.vector(sec_z)
            s_ev.n += 1
            free_psA[qz] = (s_ev, s_ev.n)
            free_zgsig[0] = (s_ev, s_ev.n)
            zv = s_ev.n
            sv = s_ev.n + 1
            def sec_scan(v, ks=ks, f_evac_val=f_evac_val, sv=sv, zv=zv):
                W.wait(v, "vector", s_ea, f_evac_val)
                W.wait(v, "vector", s_ev, zv)
                v.tensor_tensor_scan(
                    hrot[0][:, 0:PRE], fbuf[:, ks, 0:PRE], zbuf[:, ks, 0:PRE],
                    0.0, ALU.mult, ALU.add,
                ).then_inc(s_ev.h, 1)
                W.wait(v, "vector", s_ev, sv)
                v.tensor_copy(carry[:, ks:ks + 1], hrot[0][:, PRE - 1:PRE]).then_inc(s_ev.h, 1)
            block.vector(sec_scan)
            s_ev.n += 2
            scan_done_ks[ks] = s_ev.n
        if do_own:
            sdk = scan_done_ks[ks]
            for tc, sl in enumerate([slice(0, 512), slice(512, 1024)]):
                qf, mvf = mm_tile(
                    lambda k, wjf=wjf: wb[wjf][:, k, :],
                    lambda k, sl=sl: xlnfm[:, k, sl],
                    [wvf, (s_ev, w0 if tc == 0 else w1)])
                mvf_last = mvf
                def sec_f2(sc, qf=qf, mvf=mvf, ks=ks, sl=sl, sdk=sdk):
                    W.wait(sc, "scalar", s_mm, mvf)
                    W.wait(sc, "scalar", s_ev, sdk)
                    sc.activation(
                        fbuf[:, ks, sl], psA[:, qf, :], AF.Sigmoid,
                        bias=fbv[:, ks:ks + 1],
                    ).then_inc(s_ea.h, 1)
                block.scalar(sec_f2)
                s_ea.n += 1
                free_psA[qf] = (s_ea, s_ea.n)
            for tc, sl in enumerate([slice(0, 512), slice(512, 1024)]):
                jz = 1 - tc
                wv_a1 = (s_ev, w0 if tc == 0 else w1)
                qzg, mvzg = mm_tile(
                    lambda k, wjzg=wjzg: wb[wjzg][:, k, :],
                    lambda k, sl=sl: xlnfm[:, k, sl], [wvzg, wv_a1])
                mvzg_last = mvzg
                fz = free_zgsig[jz]
                def sec_zg2(sc, qzg=qzg, mvzg=mvzg, jz=jz, fz=fz):
                    W.wait(sc, "scalar", s_mm, mvzg)
                    if fz is not None:
                        W.wait(sc, "scalar", fz[0], fz[1])
                    sc.activation(zgsig[jz][:], psA[:, qzg, :], AF.Sigmoid).then_inc(s_ea.h, 1)
                block.scalar(sec_zg2)
                s_ea.n += 1
                free_psA[qzg] = (s_ea, s_ea.n)
                zgv2 = s_ea.n
                qz, mvz = mm_tile(
                    lambda k, wjz=wjz: wb[wjz][:, k, :],
                    lambda k, sl=sl: xlnfm[:, k, sl], [wvz, wv_a1])
                mvz_last = mvz
                def sec_z2(v, qz=qz, mvz=mvz, jz=jz, ks=ks, sl=sl, zgv2=zgv2):
                    W.wait(v, "vector", s_mm, mvz)
                    W.wait(v, "vector", s_ea, zgv2)
                    v.tensor_mul(zbuf[:, ks, sl], psA[:, qz, :], zgsig[jz][:]).then_inc(s_ev.h, 1)
                block.vector(sec_z2)
                s_ev.n += 1
                free_psA[qz] = (s_ev, s_ev.n)
                free_zgsig[jz] = (s_ev, s_ev.n)
        wb_consumed(wjf, s_mm, mvf_last)
        wb_consumed(wjzg, s_mm, mvzg_last)
        wb_consumed(wjz, s_mm, mvz_last)

    a1_driver = RmsDriver(d_x, xlnfm, ntt=NTT)
    for ks in range(4):
        trio_ks(ks, True, False, 0, 0)
        a1_driver.step()
        a1_driver.step()
    a1_final = a1_driver.finish()
    a1_t3 = a1_driver.tile_done[3]
    a1_t7 = a1_driver.tile_done[7]
    for ks in range(4, KS):
        trio_ks(ks, True, True, a1_t3, a1_t7)

    if DEBUG_TAPS:
        pv = s_ev.n
        pa = s_ea.n
        def sec_tapp(sy, pv=pv, pa=pa):
            W.wait(sy, "sync", s_ev, pv)
            W.wait(sy, "sync", s_ea, pa)
            sy.dma_start(out=taps["t_pf"], in_=fbuf[:]).then_inc(s_st.h, 16)
            sy.dma_start(out=taps["t_pz"], in_=zbuf[:]).then_inc(s_st.h, 16)
            sy.dma_start(out=taps["t_pc"], in_=carry[:]).then_inc(s_st.h, 16)
        block.sync(sec_tapp)
        s_st.n += 48
        prefix_tap_done = s_st.n
        def sec_guard(v, g=prefix_tap_done):
            W.wait(v, "vector", s_st, g)
        block.vector(sec_guard)
        def sec_guard2(sc, g=prefix_tap_done):
            W.wait(sc, "scalar", s_st, g)
        block.scalar(sec_guard2)

    carry_w = s_ev.n
    def sec_mask(v, cw=carry_w):
        W.wait(v, "vector", s_ev, cw)
        v.tensor_scalar(carry[:], carry[:], maskv[:, 0:1], None, ALU.mult).then_inc(s_ev.h, 1)
    block.vector(sec_mask)
    s_ev.n += 1
    mask_done = s_ev.n

    if DEBUG_TAPS:
        cv = s_ev.n
        def sec_tapc(sy, cv=cv):
            W.wait(sy, "sync", s_ev, cv)
            sy.dma_start(out=taps["t_carry"], in_=carry[:]).then_inc(s_st.h, 16)
        block.sync(sec_tapc)
        s_st.n += 16

    for ks in range(4):
        trio_ks(ks, False, True, a1_final, a1_final)
    if rms_xnbuf_free[0] is not None:
        free_xntm = rms_xnbuf_free[0]
    rms_sq_watermark = s_ea.n

    if DEBUG_TAPS:
        av = a1_final
        def sec_tapxn(sy, av=av):
            W.wait(sy, "sync", s_ev, av)
            sy.dma_start(out=taps["t_xn"], in_=xn_fm[:]).then_inc(s_st.h, 16)
        block.sync(sec_tapxn)
        s_st.n += 16

    a2a_f_done = s_ea.n
    a2a_z_done = s_ev.n

    if DEBUG_TAPS:
        def sec_tapfz(sy, fa=a2a_f_done, za=a2a_z_done):
            W.wait(sy, "sync", s_ea, fa)
            W.wait(sy, "sync", s_ev, za)
            sy.dma_start(out=taps["t_f"], in_=fbuf[:]).then_inc(s_st.h, 16)
            sy.dma_start(out=taps["t_z"], in_=zbuf[:]).then_inc(s_st.h, 16)
        block.sync(sec_tapfz)
        s_st.n += 32


    og_guard = [0]
    mask_done_v = s_ev.n
    for ks in range(KS):
        wja, wva = load_wblock(d_wga[ks], KD)
        wjb, wvb = load_wblock(d_wgb[ks], KD)
        jh = ks % 2
        fh = free_hrot[jh]
        def sec_scan(v, ks=ks, jh=jh, fh=fh, zv=a2a_z_done):
            W.wait(v, "vector", s_ea, a2a_f_done)
            W.wait(v, "vector", s_ev, zv)
            if fh is not None:
                W.wait(v, "vector", fh[0], fh[1])
            v.tensor_tensor_scan(
                hrot[jh][:], fbuf[:, ks, :], zbuf[:, ks, :], carry[:, ks:ks + 1],
                ALU.mult, ALU.add,
            ).then_inc(s_ev.h, 1)
        block.vector(sec_scan)
        s_ev.n += 1
        scan_val = s_ev.n
        for tc in range(2):
            sl = slice(tc * 512, (tc + 1) * 512)
            ja = (ks * 2 + tc) % 3
            qa, mva = mm_tile(
                lambda k, wja=wja: wb[wja][:, k, :],
                lambda k, sl=sl: xlnfm[:, k, sl],
                [wva],
            )
            fg = free_gasil[ja]
            def sec_ga(sc, qa=qa, mva=mva, ja=ja, fg=fg):
                W.wait(sc, "scalar", s_mm, mva)
                if fg is not None:
                    W.wait(sc, "scalar", fg[0], fg[1])
                sc.activation(gasil[ja][:], psA[:, qa, :], AF.Silu).then_inc(s_ea.h, 1)
            block.scalar(sec_ga)
            s_ea.n += 1
            free_psA[qa] = (s_ea, s_ea.n)
            ga_done = s_ea.n
            qb, mvb = mm_tile(
                lambda k, wjb=wjb: wb[wjb][:, k, :],
                lambda k, sl=sl: xlnfm[:, k, sl],
                [wvb],
            )
            ogv = s_ev.n + 1
            def sec_gb(v, qb=qb, mvb=mvb, ja=ja, jh=jh, ks=ks, sl=sl, ga_done=ga_done,
                       scv=scan_val, ogv=ogv, ogg=og_guard[0]):
                W.wait(v, "vector", s_mm, mvb)
                W.wait(v, "vector", s_ea, ga_done)
                W.wait(v, "vector", s_ev, max(scv, ogg))
                v.tensor_mul(zgsig[0][:], psA[:, qb, :], gasil[ja][:]).then_inc(s_ev.h, 1)
                W.wait(v, "vector", s_ev, ogv)
                v.tensor_mul(fbuf[:, ks, sl], zgsig[0][:], hrot[jh][:, sl]).then_inc(s_ev.h, 1)
            block.vector(sec_gb)
            s_ev.n += 2
            og_guard[0] = s_ev.n
            free_psA[qb] = (s_ev, s_ev.n)
            free_gasil[ja] = (s_ev, s_ev.n)
            if tc == 1:
                wb_consumed(wja, s_mm, mva)
                wb_consumed(wjb, s_mm, mvb)
        free_hrot[jh] = (s_ev, s_ev.n)
    gated_done = s_ev.n

    preload_done = [0]
    for tc in range(2):
        for j in range(4):
            tt = tc * 4 + j
            dstt = ytm[tc][:, j, :]
            ch = s_pl[j]
            waits = []
            if ch.n > 0:
                waits.append((ch, ch.n))
            if tc == 0:
                waits.append((s_ea, rms_sq_watermark))
            if tc == 1:
                if j < 3 and free_x2[j] is not None:
                    waits.append(free_x2[j])
                if j == 3 and rms_xnbuf_free[1] is not None:
                    waits.append(rms_xnbuf_free[1])
            def sec_gx(g, tt=tt, dstt=dstt, waits=list(waits), ch=ch):
                for sem, val in waits:
                    W.wait(g, "gpsimd", sem, val)
                g.dma_start(out=dstt, in_=d_x[tt * 128:(tt + 1) * 128, :]).then_inc(ch.h, 16)
            block.gpsimd(sec_gx)
            ch.n += 16
            xv = ch.n
            def sec_x2(v, dstt=dstt, xv=xv, ch=ch):
                W.wait(v, "vector", ch, xv)
                v.tensor_scalar(dstt, dstt, 2.0, None, ALU.mult).then_inc(s_ev.h, 1)
            block.vector(sec_x2)
            s_ev.n += 1
    preload_done[0] = s_ev.n

    if DEBUG_TAPS:
        def sec_tapg(sy, gv=gated_done):
            W.wait(sy, "sync", s_ev, gv)
            sy.dma_start(out=taps["t_gated"], in_=fbuf[:]).then_inc(s_st.h, 16)
        block.sync(sec_tapg)
        s_st.n += 16

    ytm_guard = [0, 0]
    fz_guard = [s_st.n]
    xlnfm_done = {}
    ln1_pe_mm = {}
    statp_guard = [0]
    stat_guard = [0]
    act_out_guard = [0]
    ln1_norm = {}
    y_ready = {}
    hh_last = {}
    hviews_all = {}

    def outproj_both():
        def emit_tr(dc, tc, je, yev):
            yb = tc % 2
            jt = (dc * 2 + tc) % 2
            ft = free_psT[jt]
            def sec_tr(t, je=je, jt=jt, ft=ft, yev=yev):
                W.wait(t, "tensor", s_ea, yev)
                if ft is not None:
                    W.wait(t, "tensor", ft[0], ft[1])
                for j in range(4):
                    ins = t.transpose(psT[jt][:, j, :], evst[je][:, j * 128:(j + 1) * 128], idf[:])
                    if j == 3:
                        ins.then_inc(s_mm.h, 1)
            block.tensor(sec_tr)
            s_mm.n += 1
            trv = s_mm.n
            free_evst[je] = (s_mm, trv)
            def sec_ev(v, jt=jt, dc=dc, trv=trv, yg=ytm_guard[yb], yb=yb, tc=tc,
                       pv=preload_done[0]):
                W.wait(v, "vector", s_mm, trv)
                W.wait(v, "vector", s_st, yg)
                W.wait(v, "vector", s_ev, pv)
                for j in range(4):
                    v.scalar_tensor_tensor(
                        ytm[yb][:, j, dc * 128:(dc + 1) * 128], psT[jt][:, j, :], 1.0,
                        ytm[yb][:, j, dc * 128:(dc + 1) * 128], ALU.mult, ALU.add,
                        accum_out=statq[:, dc * 8 + tc * 4 + j:dc * 8 + tc * 4 + j + 1],
                    ).then_inc(s_ev.h, 1)
            block.vector(sec_ev)
            s_ev.n += 4
            free_psT[jt] = (s_ev, s_ev.n)

        pending_tr = None
        for dc in range(KD):
            wjo, wvo = load_wblock(d_wo[dc], KS)
            mv_last = None
            for tc in range(2):
                sl = slice(tc * 512, (tc + 1) * 512)
                qy, mvy = mm_tile(
                    lambda k, wjo=wjo: wb[wjo][:, k, :],
                    lambda k, sl=sl: fbuf[:, k, sl],
                    [wvo, (s_ev, gated_done)],
                )
                mv_last = mvy
                je = (dc * 2 + tc) % 2
                fe = free_evst[je]
                def sec_y(sc, qy=qy, mvy=mvy, dc=dc, je=je, fe=fe):
                    W.wait(sc, "scalar", s_mm, mvy)
                    if fe is not None:
                        W.wait(sc, "scalar", fe[0], fe[1])
                    sc.activation(
                        evst[je][:], psA[:, qy, :], AF.Identity, bias=obv[:, dc:dc + 1],
                    ).then_inc(s_ea.h, 1)
                block.scalar(sec_y)
                s_ea.n += 1
                free_psA[qy] = (s_ea, s_ea.n)
                yev = s_ea.n
                if pending_tr is not None:
                    emit_tr(*pending_tr)
                pending_tr = (dc, tc, je, yev)
            wb_consumed(wjo, s_mm, mv_last)
        emit_tr(*pending_tr)
        y_ready[0] = s_ev.n
        y_ready[1] = s_ev.n

    def sum_sq_partials(act_done, dve_done=0):
        base = s_ev.n
        def sec_sp(v, act_done=act_done, base=base, dve_done=dve_done):
            W.wait(v, "vector", s_ea, act_done)
            W.wait(v, "vector", s_ev, dve_done)
            v.tensor_add(stat2[:, 8:12], statp[:, 0:16:4], statp[:, 1:16:4]).then_inc(s_ev.h, 1)
            W.wait(v, "vector", s_ev, base + 1)
            v.tensor_add(stat2[:, 8:12], stat2[:, 8:12], statp[:, 2:16:4]).then_inc(s_ev.h, 1)
            W.wait(v, "vector", s_ev, base + 2)
            v.tensor_add(stat[:, 4:8], stat2[:, 8:12], statp[:, 3:16:4]).then_inc(s_ev.h, 1)
        block.vector(sec_sp)
        s_ev.n += 3
        return s_ev.n

    def ln_stats_chain(stats_done):
        base = s_ev.n
        def sec_var(v, sd=stats_done, base=base, aog=act_out_guard[0]):
            W.wait(v, "vector", s_ev, sd)
            W.wait(v, "vector", s_ea, aog)
            v.tensor_scalar(stat2[:, 0:4], stat[:, 0:4], 1.0 / D, None, ALU.mult).then_inc(s_ev.h, 1)
            v.tensor_scalar(stat2[:, 4:8], stat[:, 4:8], 1.0 / D, None, ALU.mult).then_inc(s_ev.h, 1)
            W.wait(v, "vector", s_ev, base + 2)
            v.tensor_mul(stat2[:, 8:12], stat2[:, 0:4], stat2[:, 0:4]).then_inc(s_ev.h, 1)
            W.wait(v, "vector", s_ev, base + 3)
            v.tensor_sub(stat2[:, 4:8], stat2[:, 4:8], stat2[:, 8:12]).then_inc(s_ev.h, 1)
            W.wait(v, "vector", s_ev, base + 4)
            v.tensor_scalar(stat2[:, 4:8], stat2[:, 4:8], 1.0, EPS, ALU.mult, ALU.add).then_inc(s_ev.h, 1)
        block.vector(sec_var)
        s_ev.n += 5
        var_done = s_ev.n
        def sec_sq(sc, var_done=var_done):
            W.wait(sc, "scalar", s_ev, var_done)
            sc.activation(stat2[:, 8:12], stat2[:, 4:8], AF.Sqrt).then_inc(s_ea.h, 1)
        block.scalar(sec_sq)
        s_ea.n += 1
        sq_done = s_ea.n
        base = s_ev.n
        def sec_nr(v, sq_done=sq_done, base=base):
            W.wait(v, "vector", s_ea, sq_done)
            v.reciprocal(stat2[:, 12:16], stat2[:, 8:12]).then_inc(s_ev.h, 1)
            W.wait(v, "vector", s_ev, base + 1)
            v.tensor_mul(stat[:, 8:12], stat2[:, 12:16], stat2[:, 12:16]).then_inc(s_ev.h, 1)
            W.wait(v, "vector", s_ev, base + 2)
            v.tensor_mul(stat[:, 8:12], stat[:, 8:12], stat2[:, 4:8]).then_inc(s_ev.h, 1)
            W.wait(v, "vector", s_ev, base + 3)
            v.tensor_scalar(stat[:, 8:12], stat[:, 8:12], -0.5, 1.5, ALU.mult, ALU.add).then_inc(s_ev.h, 1)
            W.wait(v, "vector", s_ev, base + 4)
            v.tensor_mul(stat2[:, 12:16], stat2[:, 12:16], stat[:, 8:12]).then_inc(s_ev.h, 1)
        block.vector(sec_nr)
        s_ev.n += 5
        return s_ev.n

    def ln1_dve(tc):
        yb = tc % 2
        y_done = y_ready[tc]
        for j in range(2):
            def sec_sq4(sc, j=j, y_done=y_done, yb=yb, spg=statp_guard[0], stg=stat_guard[0]):
                W.wait(sc, "scalar", s_ev, y_done)
                W.wait(sc, "scalar", s_ea, spg)
                W.wait(sc, "scalar", s_ev, stg)
                for c in range(4):
                    sc.activation(
                        fbuf[:, j * 4 + c, 0:512], ytm[yb][:, j, c * 512:(c + 1) * 512], AF.Square,
                        accum_out=statp[:, 4 * j + c:4 * j + c + 1],
                    ).then_inc(s_ea.h, 1)
            block.scalar(sec_sq4)
            s_ea.n += 4
        for j in range(2, 4):
            def sec_sqv(v, j=j, y_done=y_done, yb=yb, spg=statp_guard[0], stg=stat_guard[0]):
                W.wait(v, "vector", s_ev, y_done)
                W.wait(v, "vector", s_ea, spg)
                W.wait(v, "vector", s_ev, stg)
                v.scalar_tensor_tensor(
                    fbuf[:, j * 4, 0:1024], ytm[yb][:, j, 0:1024], 1.0, ytm[yb][:, j, 0:1024],
                    ALU.mult, ALU.mult, accum_out=statp[:, 4 * j:4 * j + 1],
                ).then_inc(s_ev.h, 1)
                v.scalar_tensor_tensor(
                    fbuf[:, j * 4 + 1, 0:1024], ytm[yb][:, j, 1024:2048], 1.0, ytm[yb][:, j, 1024:2048],
                    ALU.mult, ALU.mult, accum_out=statp[:, 4 * j + 1:4 * j + 2],
                ).then_inc(s_ev.h, 1)
                v.memset(statp[:, 4 * j + 2:4 * j + 4], 0.0).then_inc(s_ev.h, 1)
            block.vector(sec_sqv)
            s_ev.n += 3
        base = s_ev.n
        def sec_s1(v, yb=yb, tc=tc, y_done=y_done, base=base, stg=stat_guard[0]):
            W.wait(v, "vector", s_ev, y_done)
            W.wait(v, "vector", s_ev, stg)
            for j in range(4):
                v.reduce_sum(stat[:, j:j + 1], statq[:, tc * 4 + j:128:8], axis=AX.X).then_inc(s_ev.h, 1)
        block.vector(sec_s1)
        s_ev.n += 4
        dve_sq = s_ev.n
        sp_done = sum_sq_partials(s_ea.n, dve_sq)
        r_done = ln_stats_chain(sp_done)
        statp_guard[0] = s_ea.n
        stat_guard[0] = r_done
        for j in range(4):
            tt = tc * 4 + j
            def sec_norm(v, j=j, rv=r_done, yb=yb):
                W.wait(v, "vector", s_ev, rv)
                v.tensor_scalar(
                    ytm[yb][:, j, :], ytm[yb][:, j, :], stat2[:, j:j + 1], stat2[:, 12 + j:13 + j],
                    ALU.subtract, ALU.mult,
                ).then_inc(s_ev.h, 1)
            block.vector(sec_norm)
            s_ev.n += 1
            xlnv = s_ev.n
            ln1_norm[(tc, j)] = xlnv

    def ln1_pe(tc):
        yb = tc % 2
        for j in range(4):
            tt = tc * 4 + j
            xlnv = ln1_norm[(tc, j)]
            for hb in range(2):
                fb_ = free_psB[hb]
                def sec_t(t, hb=hb, fb_=fb_, xlnv=xlnv, j=j, yb=yb):
                    W.wait(t, "tensor", s_ev, xlnv)
                    if fb_ is not None:
                        W.wait(t, "tensor", fb_[0], fb_[1])
                    for jj in range(8):
                        dc = hb * 8 + jj
                        ins = t.transpose(psB[:, hb, jj, :], ytm[yb][:, j, dc * 128:(dc + 1) * 128], idb[:])
                        if jj == 7:
                            ins.then_inc(s_mm.h, 1)
                block.tensor(sec_t)
                s_mm.n += 1
                trv = s_mm.n
                def sec_ev2(v, hb=hb, tt=tt, trv=trv):
                    W.wait(v, "vector", s_mm, trv)
                    v.tensor_copy(
                        xn_fm[:, hb * 8:(hb + 1) * 8, tt * 128:(tt + 1) * 128],
                        psB[:, hb, :, :],
                    ).then_inc(s_ev.h, 1)
                block.vector(sec_ev2)
                s_ev.n += 1
                free_psB[hb] = (s_ev, s_ev.n)
        xlnfm_done[tc] = s_ev.n
        ln1_pe_mm[tc] = s_mm.n

    gu_loaded = [0]
    gu_q = []

    def preload_gu(n):
        for _ in range(n):
            if gu_loaded[0] >= 2 * KI:
                return
            ic = gu_loaded[0] % KI
            wjg, wvg = load_wblock(d_gu[ic, 0], 16)
            wju, wvu = load_wblock(d_gu[ic, 1], 16)
            def sec_m128(v, wjg=wjg, wju=wju, wvg=wvg, wvu=wvu):
                W.wait(v, "vector", wvg[0], wvg[1])
                W.wait(v, "vector", wvu[0], wvu[1])
                v.tensor_scalar(wb[wjg][:, 0:16, :], wb[wjg][:, 0:16, :], -128.0, None, ALU.add).then_inc(s_ev.h, 1)
                v.tensor_scalar(wb[wju][:, 0:16, :], wb[wju][:, 0:16, :], -128.0, None, ALU.add).then_inc(s_ev.h, 1)
            block.vector(sec_m128)
            s_ev.n += 2
            gu_q.append((wjg, wju, s_ev.n))
            gu_loaded[0] += 1

    def ffn_gu(tc):
        xln_ready = xlnfm_done[tc]
        sl = slice(tc * 512, (tc + 1) * 512)
        hviews = []
        for ic in range(KI):
            if ic < 32:
                buf, idx = fbuf, ic
            else:
                buf, idx = zbuf, ic - 32
            hviews.append(buf[:].rearrange("p a (b c) -> p (a b) c", b=2)[:, idx, :])
        hviews_all[tc] = hviews
        hh_done = 0
        for ic in range(KI):
            want = tc * KI + ic + 2
            while gu_loaded[0] < min(want, (tc + 1) * KI):
                preload_gu(1)
            wjg, wju, wready = gu_q.pop(0)
            qg, mvg = mm_tile(
                lambda k, wjg=wjg: wb[wjg][:, k, :],
                lambda k, sl=sl: xn_fm[:, k, sl],
                [(s_ev, wready), (s_ev, xln_ready)],
            )
            jg = ic % 3
            fg = free_gasil[jg]
            def sec_g(sc, qg=qg, mvg=mvg, ic=ic, jg=jg, fg=fg):
                W.wait(sc, "scalar", s_mm, mvg)
                if fg is not None:
                    W.wait(sc, "scalar", fg[0], fg[1])
                sc.activation(
                    gasil[jg][:], psA[:, qg, :], AF.Silu, scale=gscv[:, ic:ic + 1],
                ).then_inc(s_ea.h, 1)
            block.scalar(sec_g)
            s_ea.n += 1
            free_psA[qg] = (s_ea, s_ea.n)
            g_done = s_ea.n
            qu, mvu = mm_tile(
                lambda k, wju=wju: wb[wju][:, k, :],
                lambda k, sl=sl: xn_fm[:, k, sl],
                [],
            )
            def sec_u(v, qu=qu, mvu=mvu, ic=ic, jg=jg, g_done=g_done,
                      hv=hviews[ic], fzg=fz_guard[0]):
                W.wait(v, "vector", s_mm, mvu)
                W.wait(v, "vector", s_ea, g_done)
                W.wait(v, "vector", s_st, fzg)
                v.scalar_tensor_tensor(
                    hv, psA[:, qu, :], uscv[:, ic:ic + 1], gasil[jg][:],
                    ALU.mult, ALU.mult,
                ).then_inc(s_ev.h, 1)
            block.vector(sec_u)
            s_ev.n += 1
            free_psA[qu] = (s_ev, s_ev.n)
            free_gasil[jg] = (s_ev, s_ev.n)
            wb_consumed(wjg, s_mm, mvg)
            wb_consumed(wju, s_mm, mvu)
            hh_done = s_ev.n
        hh_last[tc] = hh_done

    def ffn_down(tc, pump=None):
        yb = tc % 2
        hviews = hviews_all[tc]
        hhv_all = hh_last[tc]

        def emit_dtr(hc, je, dev):
            jt = hc % 2
            ft = free_psT[jt]
            def sec_tr(t, je=je, jt=jt, ft=ft, dev=dev):
                W.wait(t, "tensor", s_ea, dev)
                if ft is not None:
                    W.wait(t, "tensor", ft[0], ft[1])
                for j in range(4):
                    ins = t.transpose(psT[jt][:, j, :], evst[je][:, j * 128:(j + 1) * 128], idf[:])
                    if j == 3:
                        ins.then_inc(s_mm.h, 1)
            block.tensor(sec_tr)
            s_mm.n += 1
            trv = s_mm.n
            free_evst[je] = (s_mm, trv)
            def sec_ev(v, jt=jt, hc=hc, trv=trv, yg=ytm_guard[yb], yb=yb, tc=tc):
                W.wait(v, "vector", s_mm, trv)
                W.wait(v, "vector", s_st, yg)
                for j in range(4):
                    v.scalar_tensor_tensor(
                        ytm[yb][:, j, hc * 128:(hc + 1) * 128], psT[jt][:, j, :], 1.0,
                        ytm[yb][:, j, hc * 128:(hc + 1) * 128], ALU.mult, ALU.add,
                        accum_out=statq[:, hc * 8 + tc * 4 + j:hc * 8 + tc * 4 + j + 1],
                    ).then_inc(s_ev.h, 1)
            block.vector(sec_ev)
            s_ev.n += 4
            free_psT[jt] = (s_ev, s_ev.n)

        pending_dtr = None
        dn_q = []

        def load_dn(hc, g4):
            wj, wv = load_wblock(d_dn[hc, g4], 16)
            def sec_m128b(v, wj=wj, wv=wv):
                W.wait(v, "vector", wv[0], wv[1])
                v.tensor_scalar(wb[wj][:, 0:16, :], wb[wj][:, 0:16, :], -128.0, None, ALU.add).then_inc(s_ev.h, 1)
            block.vector(sec_m128b)
            s_ev.n += 1
            dn_q.append((wj, s_ev.n))

        dn_loaded = [0]

        def preload_dn(n):
            while n > 0 and dn_loaded[0] < 4 * KD:
                load_dn(dn_loaded[0] // 4, dn_loaded[0] % 4)
                dn_loaded[0] += 1
                n -= 1

        preload_dn(3)
        for hc in range(KD):
            q = psA_ctr[0] % 4
            psA_ctr[0] += 1
            fa = free_psA[q]
            for g4 in range(4):
                preload_dn(1)
                wj, wready = dn_q.pop(0)
                def sec_dn(t, q=q, fa=fa, g4=g4, wj=wj, wready=wready, hhv=hhv_all):
                    W.wait(t, "tensor", s_ev, wready)
                    W.wait(t, "tensor", s_ev, hhv)
                    if g4 == 0 and fa is not None:
                        W.wait(t, "tensor", fa[0], fa[1])
                    for k in range(16):
                        ins = t.matmul(psA[:, q, :], wb[wj][:, k, :], hviews[g4 * 16 + k],
                                       start=(g4 == 0 and k == 0), stop=(g4 == 3 and k == 15))
                        if k == 15:
                            ins.then_inc(s_mm.h, 1)
                block.tensor(sec_dn)
                s_mm.n += 1
                wb_consumed(wj, s_mm, s_mm.n)
            mvd = s_mm.n
            je = hc % 2
            fe = free_evst[je]
            def sec_de(sc, q=q, mvd=mvd, hc=hc, je=je, fe=fe):
                W.wait(sc, "scalar", s_mm, mvd)
                if fe is not None:
                    W.wait(sc, "scalar", fe[0], fe[1])
                sc.activation(
                    evst[je][:], psA[:, q, :], AF.Identity, scale=dscv[:, hc:hc + 1],
                ).then_inc(s_ea.h, 1)
            block.scalar(sec_de)
            s_ea.n += 1
            free_psA[q] = (s_ea, s_ea.n)
            dev = s_ea.n
            if pending_dtr is not None:
                emit_dtr(*pending_dtr)
            pending_dtr = (hc, je, dev)
            if pump is not None:
                pump()
        emit_dtr(*pending_dtr)
        return s_ev.n

    def ln2_steps(tc, ffn_done):
        steps = []
        hold = {}
        yb = tc % 2

        def tile_step(j):
            def go():
                nonlocal_state_tile(tc, ffn_done, j, yb)
            return go

        def stats_step():
            sp = sum_sq_partials(s_ea.n, s_ev.n)
            hold["r_done"] = ln_stats_chain(sp)
            def sec_mr(v, rv=hold["r_done"]):
                W.wait(v, "vector", s_ev, rv)
                v.scalar_tensor_tensor(
                    stat[:, 12:16], stat2[:, 0:4], -1.0, stat2[:, 12:16],
                    ALU.mult, ALU.mult,
                ).then_inc(s_ev.h, 1)
            block.vector(sec_mr)
            s_ev.n += 1
            hold["mr_done"] = s_ev.n
            statp_guard[0] = s_ea.n
            stat_guard[0] = hold["r_done"]

        def out_step(j):
            def go():
                nonlocal_state_out(tc, j, yb, hold["r_done"], hold["mr_done"])
            return go

        for j in range(4):
            steps.append(tile_step(j))
        steps.append(stats_step)
        for j in range(4):
            steps.append(out_step(j))
        return steps

    ln2_last_store = [0]

    def nonlocal_state_tile(tc, ffn_done, j, yb):
        nonlocal free_xstage
        if j < 2:
            fx = free_xstage
            def sec_sq4(sc, j=j, fd=ffn_done, yb=yb, tc=tc, spg=statp_guard[0],
                        stg=stat_guard[0]):
                W.wait(sc, "scalar", s_ev, fd)
                W.wait(sc, "scalar", s_ea, spg)
                W.wait(sc, "scalar", s_ev, stg)
                for c in range(4):
                    sc.activation(
                        xn_fm[:, j * 4 + c, tc * 512:(tc + 1) * 512], ytm[yb][:, j, c * 512:(c + 1) * 512],
                        AF.Square,
                        accum_out=statp[:, 4 * j + c:4 * j + c + 1],
                    ).then_inc(s_ea.h, 1)
            block.scalar(sec_sq4)
            s_ea.n += 4
            def sec_s1a(v, j=j, tc=tc, fd=ffn_done, stg=stat_guard[0]):
                W.wait(v, "vector", s_ev, fd)
                W.wait(v, "vector", s_ev, stg)
                v.reduce_sum(stat[:, j:j + 1], statq[:, tc * 4 + j:128:8], axis=AX.X).then_inc(s_ev.h, 1)
            block.vector(sec_s1a)
            s_ev.n += 1
        else:
            def sec_sqv(v, j=j, fd=ffn_done, yb=yb, tc=tc, spg=statp_guard[0],
                        stg=stat_guard[0]):
                W.wait(v, "vector", s_ev, fd)
                W.wait(v, "vector", s_ea, spg)
                W.wait(v, "vector", s_ev, stg)
                v.scalar_tensor_tensor(
                    xn_fm[:, j * 4 + tc * 2, :], ytm[yb][:, j, 0:1024], 1.0,
                    ytm[yb][:, j, 0:1024], ALU.mult, ALU.mult,
                    accum_out=statp[:, 4 * j:4 * j + 1],
                ).then_inc(s_ev.h, 1)
                v.scalar_tensor_tensor(
                    xn_fm[:, j * 4 + tc * 2 + 1, :], ytm[yb][:, j, 1024:2048], 1.0,
                    ytm[yb][:, j, 1024:2048], ALU.mult, ALU.mult,
                    accum_out=statp[:, 4 * j + 1:4 * j + 2],
                ).then_inc(s_ev.h, 1)
                v.memset(statp[:, 4 * j + 2:4 * j + 4], 0.0).then_inc(s_ev.h, 1)
            block.vector(sec_sqv)
            s_ev.n += 3
            def sec_s1(v, j=j, tc=tc, fd=ffn_done, stg=stat_guard[0]):
                W.wait(v, "vector", s_ev, fd)
                W.wait(v, "vector", s_ev, stg)
                v.reduce_sum(stat[:, j:j + 1], statq[:, tc * 4 + j:128:8], axis=AX.X).then_inc(s_ev.h, 1)
            block.vector(sec_s1)
            s_ev.n += 1

    qst = {}

    def nonlocal_state_out(tc, j, yb, r_done, mr_done):
        nonlocal free_xstage
        tt = tc * 4 + j
        for c in range(4):
            if tc == 1:
                k = (j * 2 + c // 2) % 2
                ch = c % 2
                if k == 0:
                    reg, key = xstage[:, ch * 512:(ch + 1) * 512], ("x", ch)
                else:
                    reg, key = evst[ch][:], ("e", ch)
            else:
                ch = c
                reg, key = xstage[:, c * 512:(c + 1) * 512], ("x", c)
            qs = qst.get(key)
            src = ytm[yb][:, j, c * 512:(c + 1) * 512]
            dst = d_out[tt * 128:(tt + 1) * 128, c * 512:(c + 1) * 512]
            if c % 2 == 0:
                def sec_oa(sc, j=j, rv=mr_done, qs=qs, reg=reg, src=src):
                    W.wait(sc, "scalar", s_ev, rv)
                    if qs is not None:
                        W.wait(sc, "scalar", qs[0], qs[1])
                    sc.activation(
                        reg, src, AF.Identity,
                        scale=stat2[:, 12 + j:13 + j], bias=stat[:, 12 + j:13 + j],
                    ).then_inc(s_ea.h, 1)
                block.scalar(sec_oa)
                s_ea.n += 1
                pv = (s_ea, s_ea.n)
            else:
                def sec_ov(v, j=j, rv=r_done, qs=qs, reg=reg, src=src):
                    W.wait(v, "vector", s_ev, rv)
                    if qs is not None:
                        W.wait(v, "vector", qs[0], qs[1])
                    v.tensor_scalar(
                        reg, src, stat2[:, j:j + 1], stat2[:, 12 + j:13 + j],
                        ALU.subtract, ALU.mult,
                    ).then_inc(s_ev.h, 1)
                block.vector(sec_ov)
                s_ev.n += 1
                pv = (s_ev, s_ev.n)
            osem = s_oq[c]
            if c % 2 == 0:
                def sec_sta(sy, pv=pv, dst=dst, reg=reg, osem=osem):
                    W.wait(sy, "sync", pv[0], pv[1])
                    sy.dma_start(out=dst, in_=reg).then_inc(osem.h, 16)
                block.sync(sec_sta)
            else:
                def sec_stg(g, pv=pv, dst=dst, reg=reg, osem=osem):
                    W.wait(g, "gpsimd", pv[0], pv[1])
                    g.dma_start(out=dst, in_=reg).then_inc(osem.h, 16)
                block.gpsimd(sec_stg)
            osem.n += 16
            qst[key] = (osem, osem.n)
        act_out_guard[0] = s_ea.n
        ytm_guard[yb] = s_st.n

    outproj_both()
    ln1_dve(0)
    ln1_pe(0)
    ln1_dve(1)
    ffn_gu(0)
    ln1_pe(1)
    f0 = ffn_down(0)
    ffn_gu(1)
    steps0 = ln2_steps(0, f0)
    sit = iter(steps0)
    def pump():
        s = next(sit, None)
        if s is not None:
            s()
    f1 = ffn_down(1, pump=pump)
    for s in sit:
        s()
    for s in ln2_steps(1, f1):
        s()

    def sec_fin(sy):
        sy.wait_ge(s_st.h, s_st.n)
        for c in range(4):
            sy.wait_ge(s_oq[c].h, s_oq[c].n)
    block.sync(sec_fin)
    block.flush()

    es.close()
    return nc


_nc_cache = None


def _get_program():
    global _nc_cache
    if _nc_cache is None:
        _nc_cache = build_program()
    return _nc_cache


def _pack_proj(Wnp):
    WT = np.ascontiguousarray(Wnp.T)
    t = WT.reshape(KD, 128, KS, 128)
    return np.ascontiguousarray(t.transpose(2, 1, 0, 3))


def _pack_wout(Wnp):
    WoT = np.ascontiguousarray(Wnp.T)
    t = WoT.reshape(KS, 128, KD, 128)
    return np.ascontiguousarray(t.transpose(2, 1, 0, 3))


def _int32_to_u8(a):
    a = np.ascontiguousarray(np.asarray(a, dtype=np.int32))
    assert a.dtype.byteorder in ("=", "<")
    return a.view(np.uint8).reshape(*a.shape, 4)[..., 0]


def _pack_gu(gate_w, up_w):
    def tiles(Wq):
        WT = np.ascontiguousarray(Wq.T)
        t = WT.reshape(KD, 128, KI, 128)
        return np.ascontiguousarray(t.transpose(2, 1, 0, 3))
    g = tiles(_int32_to_u8(gate_w))
    u = tiles(_int32_to_u8(up_w))
    return np.ascontiguousarray(np.stack([g, u], axis=1))


def _pack_down(down_w):
    Wq = _int32_to_u8(down_w)
    WT = np.ascontiguousarray(Wq.T)
    t = WT.reshape(KI, 128, KD, 128)
    t = t.transpose(2, 0, 1, 3)
    t = t.reshape(KD, 4, 16, 128, 128).transpose(0, 1, 3, 2, 4)
    return np.ascontiguousarray(t)


def kernel(L0_attn_x, L0_attn_rms_w, L0_attn_Wf_w, L0_attn_Wf_b, L0_attn_Wzg_w,
           L0_attn_Wz_w, L0_attn_Wga_w, L0_attn_Wgb_w, L0_attn_Wout_w,
           L0_attn_Wout_b, L0_ffn_gate_w, L0_ffn_gate_scale, L0_ffn_up_w,
           L0_ffn_up_scale, L0_ffn_down_w, L0_ffn_down_scale):
    x = np.ascontiguousarray(np.asarray(L0_attn_x, dtype=np.float32))
    f32 = lambda a: np.ascontiguousarray(np.asarray(a, dtype=np.float32))
    rms_w = f32(L0_attn_rms_w)

    shared = {
        "rms_bcast": np.ascontiguousarray(np.tile(rms_w[None, :], (128, 1)).astype(ml_dtypes.bfloat16)),
        "ident_bf": np.eye(128, dtype=ml_dtypes.bfloat16),
        "ident_f32": np.eye(128, dtype=np.float32),
        "f_bias": np.ascontiguousarray(f32(L0_attn_Wf_b).reshape(KS, 128).T),
        "out_bias": np.ascontiguousarray(f32(L0_attn_Wout_b).reshape(KD, 128).T),
        "gate_sc": np.ascontiguousarray(f32(L0_ffn_gate_scale).reshape(KI, 128).T),
        "up_sc": np.ascontiguousarray(f32(L0_ffn_up_scale).reshape(KI, 128).T),
        "down_sc": np.ascontiguousarray(f32(L0_ffn_down_scale).reshape(KD, 128).T),
        "wf": _pack_proj(f32(L0_attn_Wf_w)),
        "wzg": _pack_proj(f32(L0_attn_Wzg_w)),
        "wz": _pack_proj(f32(L0_attn_Wz_w)),
        "wga": _pack_proj(f32(L0_attn_Wga_w)),
        "wgb": _pack_proj(f32(L0_attn_Wgb_w)),
        "wo": _pack_wout(f32(L0_attn_Wout_w)),
        "wgu": _pack_gu(L0_ffn_gate_w, L0_ffn_up_w),
        "wdn": _pack_down(L0_ffn_down_w),
    }

    in_maps = []
    for core in range(8):
        b, half = core // 2, core % 2
        m = dict(shared)
        m["x_own"] = np.ascontiguousarray(x[b, half * NT:(half + 1) * NT, :])
        m["x_pre"] = np.ascontiguousarray(x[b, NT - PRE:NT, :])
        m["maskv"] = np.full((128, 1), 1.0 if half == 1 else 0.0, np.float32)
        in_maps.append(m)

    nc = _get_program()
    res = run_bass_kernel_spmd(nc, in_maps, core_ids=list(range(8)))

    out = np.empty((B, T, D), dtype=np.float32)
    for core in range(8):
        b, half = core // 2, core % 2
        out[b, half * NT:(half + 1) * NT, :] = res.results[core]["out"]
    return out


# version 27
# speedup vs baseline: 1.0788x; 1.0509x over previous
import contextlib
import numpy as np
import ml_dtypes
import concourse.bass as bass
import concourse.mybir as mybir
from concourse.bass_utils import run_bass_kernel_spmd

F32 = mybir.dt.float32
BF16 = mybir.dt.bfloat16
U8 = mybir.dt.uint8
AF = mybir.ActivationFunctionType
ALU = mybir.AluOpType
AX = mybir.AxisListType

B, T, D, S, I = 4, 2048, 2048, 2048, 8192
NT = 1024
NTT = NT // 128
KD = D // 128
KS = S // 128
KI = I // 128
EPS = 1e-5
PRE = 128

DEBUG_TAPS = False


class Sem:
    def __init__(self, es, nc, name):
        self.h = es.enter_context(nc.semaphore(name))
        self.n = 0
        self.name = name

    def inc(self, instr, k=1):
        instr.then_inc(self.h, k)
        self.n += k
        return self.n


class Waiter:

    def __init__(self):
        self.seen = {}

    def wait(self, eng, engname, sem, val):
        if val is None or val <= 0:
            return
        key = (engname, sem.name)
        if self.seen.get(key, -1) >= val:
            return
        eng.wait_ge(sem.h, val)
        self.seen[key] = val


def build_program():
    nc = bass.Bass("TRN2", target_bir_lowering=False)
    es = contextlib.ExitStack()

    d_x = nc.dram_tensor("x_own", [NT, D], F32, kind="ExternalInput").ap()
    d_xp = nc.dram_tensor("x_pre", [PRE, D], F32, kind="ExternalInput").ap()
    d_mask = nc.dram_tensor("maskv", [128, 1], F32, kind="ExternalInput").ap()
    d_rmsb = nc.dram_tensor("rms_bcast", [128, D], BF16, kind="ExternalInput").ap()
    d_idb = nc.dram_tensor("ident_bf", [128, 128], BF16, kind="ExternalInput").ap()
    d_idf = nc.dram_tensor("ident_f32", [128, 128], F32, kind="ExternalInput").ap()
    d_fb = nc.dram_tensor("f_bias", [128, KS], F32, kind="ExternalInput").ap()
    d_ob = nc.dram_tensor("out_bias", [128, KD], F32, kind="ExternalInput").ap()
    d_gsc = nc.dram_tensor("gate_sc", [128, KI], F32, kind="ExternalInput").ap()
    d_usc = nc.dram_tensor("up_sc", [128, KI], F32, kind="ExternalInput").ap()
    d_dsc = nc.dram_tensor("down_sc", [128, KD], F32, kind="ExternalInput").ap()
    d_wf = nc.dram_tensor("wf", [KS, 128, KD, 128], F32, kind="ExternalInput").ap()
    d_wzg = nc.dram_tensor("wzg", [KS, 128, KD, 128], F32, kind="ExternalInput").ap()
    d_wz = nc.dram_tensor("wz", [KS, 128, KD, 128], F32, kind="ExternalInput").ap()
    d_wga = nc.dram_tensor("wga", [KS, 128, KD, 128], F32, kind="ExternalInput").ap()
    d_wgb = nc.dram_tensor("wgb", [KS, 128, KD, 128], F32, kind="ExternalInput").ap()
    d_wo = nc.dram_tensor("wo", [KD, 128, KS, 128], F32, kind="ExternalInput").ap()
    d_gu = nc.dram_tensor("wgu", [KI, 2, 128, 16, 128], U8, kind="ExternalInput").ap()
    d_dn = nc.dram_tensor("wdn", [KD, 4, 128, 16, 128], U8, kind="ExternalInput").ap()

    d_out = nc.dram_tensor("out", [NT, D], F32, kind="ExternalOutput").ap()

    taps = {}
    if DEBUG_TAPS:
        taps["t_xn"] = nc.dram_tensor("t_xn", [128, KD, NT], BF16, kind="ExternalOutput").ap()
        taps["t_f"] = nc.dram_tensor("t_f", [128, KS, NT], BF16, kind="ExternalOutput").ap()
        taps["t_z"] = nc.dram_tensor("t_z", [128, KS, NT], BF16, kind="ExternalOutput").ap()
        taps["t_carry"] = nc.dram_tensor("t_carry", [128, KS], F32, kind="ExternalOutput").ap()
        taps["t_gated"] = nc.dram_tensor("t_gated", [128, KS, NT], BF16, kind="ExternalOutput").ap()
        taps["t_xln"] = nc.dram_tensor("t_xln", [NT, D], F32, kind="ExternalOutput").ap()
        taps["t_pf"] = nc.dram_tensor("t_pf", [128, KS, NT], BF16, kind="ExternalOutput").ap()
        taps["t_pz"] = nc.dram_tensor("t_pz", [128, KS, NT], BF16, kind="ExternalOutput").ap()
        taps["t_pc"] = nc.dram_tensor("t_pc", [128, KS], F32, kind="ExternalOutput").ap()
        taps["t_y"] = nc.dram_tensor("t_y", [128, 4, D], F32, kind="ExternalOutput").ap()
        taps["t_u"] = nc.dram_tensor("t_u", [128, 4, D], F32, kind="ExternalOutput").ap()
        taps["t_stat"] = nc.dram_tensor("t_stat", [128, 32], F32, kind="ExternalOutput").ap()

    sb = lambda name, shape, dt: es.enter_context(nc.sbuf_tensor(name, shape, dt))
    xn_fm = sb("xn_fm", [128, KD, NT], BF16)
    fbuf = sb("fbuf", [128, KS, NT], BF16)
    zbuf = sb("zbuf", [128, KS, NT], BF16)
    xlnfm = sb("xlnfm", [128, KD, NT], BF16)
    ytm = [sb(f"ytm{j}", [128, 4, D], BF16) for j in range(2)]
    NWB = 4
    wb = [sb(f"wb{j}", [128, 16, 128], BF16) for j in range(NWB)]
    xstage = sb("xstage", [128, D], F32)
    xntm = sb("xntm", [128, D], BF16)
    zgsig = [sb(f"zgsig{j}", [128, 512], BF16) for j in range(2)]
    gasil = [sb(f"gasil{j}", [128, 512], BF16) for j in range(3)]
    evst = [sb(f"evst{j}", [128, 512], F32) for j in range(2)]
    hrot = [sb(f"hrot{j}", [128, NT], BF16) for j in range(2)]
    carry = sb("carry", [128, KS], F32)
    rmsb = sb("rmsb", [128, D], BF16)
    idb = sb("idb", [128, 128], BF16)
    idf = sb("idf", [128, 128], F32)
    fbv = sb("fbv", [128, KS], F32)
    obv = sb("obv", [128, KD], F32)
    gscv = sb("gscv", [128, KI], F32)
    uscv = sb("uscv", [128, KI], F32)
    dscv = sb("dscv", [128, KD], F32)
    maskv = sb("maskv_sb", [128, 1], F32)
    stat = sb("stat", [128, 16], F32)
    stat2 = sb("stat2", [128, 16], F32)
    statp = sb("statp", [128, 16], F32)
    statq = sb("statq", [128, 128], F32)
    statq2 = sb("statq2", [128, 128], F32)

    psA = es.enter_context(nc.psum_tensor("psA", [128, 4, 512], F32))
    psT = [es.enter_context(nc.psum_tensor(f"psT{j}", [128, 4, 128], F32)) for j in range(2)]
    psB = es.enter_context(nc.psum_tensor("psB", [128, 2, 8, 128], BF16))

    s_w = Sem(es, nc, "s_w")
    s_wb = [Sem(es, nc, f"s_wb{j}") for j in range(4)]
    s_x2 = [Sem(es, nc, f"s_x2{j}") for j in range(3)]
    s_pl = [Sem(es, nc, f"s_pl{j}") for j in range(4)]
    s_x = Sem(es, nc, "s_x")
    s_mm = Sem(es, nc, "s_mm")
    s_ea = Sem(es, nc, "s_ea")
    s_ev = Sem(es, nc, "s_ev")
    s_st = Sem(es, nc, "s_st")
    s_oq = [Sem(es, nc, f"s_oq{c}") for c in range(4)]

    W = Waiter()

    free_psA = [None] * 4
    free_psT = [None] * 2
    free_psB = [None] * 2
    free_wb = [None] * NWB
    free_zgsig = [None] * 2
    free_gasil = [None] * 3
    free_evst = [None] * 2
    free_hrot = [None] * 2
    free_xstage = None
    free_xntm = None

    psA_ctr = [0]
    free_x2 = [None] * 4

    _block = es.enter_context(nc.Block())

    class _EngineBuffer:

        def __init__(self):
            self.pend = {k: [] for k in ("sync", "gpsimd", "scalar", "tensor", "vector")}

        def _add(self, eng, fn):
            self.pend[eng].append(fn)
            if len(self.pend[eng]) >= 48:
                self.flush(eng)

        def sync(self, fn):
            self._add("sync", fn)

        def gpsimd(self, fn):
            self._add("gpsimd", fn)

        def scalar(self, fn):
            self._add("scalar", fn)

        def tensor(self, fn):
            self._add("tensor", fn)

        def vector(self, fn):
            self._add("vector", fn)

        def flush(self, eng=None):
            engs = [eng] if eng else list(self.pend)
            for e in engs:
                fns = self.pend[e]
                if not fns:
                    continue
                self.pend[e] = []
                getattr(_block, e)(lambda handle, fns=fns: [f(handle) for f in fns])

    block = _EngineBuffer()


    def load_consts():
        def sec(sy):
            sy.dma_start(out=rmsb[:], in_=d_rmsb).then_inc(s_x.h, 16)
            sy.dma_start(out=idb[:], in_=d_idb).then_inc(s_x.h, 16)
            sy.dma_start(out=idf[:], in_=d_idf).then_inc(s_x.h, 16)
            sy.dma_start(out=fbv[:], in_=d_fb).then_inc(s_x.h, 16)
            sy.dma_start(out=obv[:], in_=d_ob).then_inc(s_x.h, 16)
            sy.dma_start(out=gscv[:], in_=d_gsc).then_inc(s_x.h, 16)
            sy.dma_start(out=uscv[:], in_=d_usc).then_inc(s_x.h, 16)
            sy.dma_start(out=dscv[:], in_=d_dsc).then_inc(s_x.h, 16)
            sy.dma_start(out=maskv[:], in_=d_mask).then_inc(s_x.h, 16)
        block.sync(sec)
        for _ in range(9):
            s_x.n += 16
    consts_done = [0]

    class RmsDriver:

        def __init__(self, d_src, dst_fm, ntt=NTT):
            self.d_src = d_src
            self.dst = dst_fm
            self.ntt = ntt
            self.tt = 0
            self.done_val = 0
            self.tile_done = []

        def step(self):
            if self.tt >= self.ntt:
                return False
            _rms_tile(self.d_src, self.dst, self.tt, self)
            self.tt += 1
            return True

        def finish(self):
            while self.step():
                pass
            if rms_pend[0] is not None:
                _emit_rms_tr(*rms_pend[0])
                rms_pend[0] = None
            return self.done_val

    rms_xnbuf_free = [None, None]
    rms_statp_free = [None, None]
    rms_pend = [None]

    def _emit_rms_tr(dst_fm, tt, xn_src, xnv, drv):
        for hb in range(2):
            fb_ = free_psB[hb]
            def sec_t(t, hb=hb, fb_=fb_, xnv=xnv, xn_src=xn_src):
                W.wait(t, "tensor", s_ev, xnv)
                W.wait(t, "tensor", s_x, consts_done[0])
                if fb_ is not None:
                    W.wait(t, "tensor", fb_[0], fb_[1])
                for j in range(8):
                    dc = hb * 8 + j
                    ins = t.transpose(psB[:, hb, j, :], xn_src[:, dc * 128:(dc + 1) * 128], idb[:])
                    if j == 7:
                        ins.then_inc(s_mm.h, 1)
            block.tensor(sec_t)
            s_mm.n += 1
            tr_done = s_mm.n
            def sec_vec3(v, tt=tt, hb=hb, tr_done=tr_done, dst_fm=dst_fm):
                W.wait(v, "vector", s_mm, tr_done)
                v.tensor_copy(
                    dst_fm[:, hb * 8:(hb + 1) * 8, tt * 128:(tt + 1) * 128],
                    psB[:, hb, :, :],
                ).then_inc(s_ev.h, 1)
            block.vector(sec_vec3)
            s_ev.n += 1
            free_psB[hb] = (s_ev, s_ev.n)
            drv.done_val = s_ev.n
        drv.tile_done.append(drv.done_val)
        rms_xnbuf_free[tt % 2] = (s_mm, s_mm.n)

    def _rms_tile(d_src, dst_fm, tt, drv):
        nonlocal free_xstage
        if True:
            if getattr(drv, "via_sync", False):
                slot = tt % 3
                x2 = xstage[:]
                xsem = s_x2[slot]
                def sec_gs(sy, tt=tt, x2=x2, xsem=xsem, d_src=d_src):
                    sy.dma_start(out=x2, in_=d_src[tt * 128:(tt + 1) * 128, :]).then_inc(xsem.h, 16)
                block.sync(sec_gs)
                xsem.n += 16
                x_loaded = (xsem, xsem.n)
            else:
                slot = tt % 3
                x2 = ytm[1][:, slot, :]
                fx = free_x2[slot]
                xsem = s_x2[slot]
                def sec_g(g, tt=tt, fx=fx, x2=x2, xsem=xsem):
                    if fx is not None:
                        W.wait(g, "gpsimd", fx[0], fx[1])
                    g.dma_start(out=x2, in_=d_src[tt * 128:(tt + 1) * 128, :]).then_inc(xsem.h, 16)
                block.gpsimd(sec_g)
                xsem.n += 16
                x_loaded = (xsem, xsem.n)

            ph = tt % 2
            fp = rms_statp_free[ph]
            def sec_sq4(sc, x_loaded=x_loaded, ph=ph, fp=fp, x2=x2):
                W.wait(sc, "scalar", x_loaded[0], x_loaded[1])
                if fp is not None:
                    W.wait(sc, "scalar", fp[0], fp[1])
                for c in range(4):
                    sc.activation(
                        ytm[0][:, ph, c * 512:(c + 1) * 512], x2[:, c * 512:(c + 1) * 512], AF.Square,
                        accum_out=statp[:, 4 * ph + c:4 * ph + c + 1],
                    ).then_inc(s_ea.h, 1)
            block.scalar(sec_sq4)
            s_ea.n += 4
            sq_done = s_ea.n

            base = s_ev.n
            def sec_vec(v, sq_done=sq_done, ph=ph, base=base):
                W.wait(v, "vector", s_ea, sq_done)
                v.tensor_add(stat[:, 0:1], statp[:, 4 * ph:4 * ph + 1], statp[:, 4 * ph + 1:4 * ph + 2]).then_inc(s_ev.h, 1)
                W.wait(v, "vector", s_ev, base + 1)
                v.tensor_add(stat[:, 0:1], stat[:, 0:1], statp[:, 4 * ph + 2:4 * ph + 3]).then_inc(s_ev.h, 1)
                W.wait(v, "vector", s_ev, base + 2)
                v.tensor_add(stat[:, 0:1], stat[:, 0:1], statp[:, 4 * ph + 3:4 * ph + 4]).then_inc(s_ev.h, 1)
                W.wait(v, "vector", s_ev, base + 3)
                v.tensor_scalar(stat[:, 1:2], stat[:, 0:1], 1.0 / D, EPS, ALU.mult, ALU.add).then_inc(s_ev.h, 1)
            block.vector(sec_vec)
            s_ev.n += 4
            stats_done = s_ev.n
            rms_statp_free[ph] = (s_ev, stats_done)

            def sec_sc(sc, stats_done=stats_done):
                W.wait(sc, "scalar", s_ev, stats_done)
                sc.activation(stat[:, 2:3], stat[:, 1:2], AF.Sqrt).then_inc(s_ea.h, 1)
            block.scalar(sec_sc)
            s_ea.n += 1
            sqrt_done = s_ea.n

            xb = tt % 2
            xn_dst = xntm[:] if xb == 0 else ytm[1][:, 3, :]
            fb2 = rms_xnbuf_free[xb]
            base = s_ev.n
            def sec_vec2(v, sqrt_done=sqrt_done, base=base, x2=x2, xn_dst=xn_dst, fb2=fb2):
                W.wait(v, "vector", s_ea, sqrt_done)
                W.wait(v, "vector", s_x, consts_done[0])
                if fb2 is not None:
                    W.wait(v, "vector", fb2[0], fb2[1])
                v.reciprocal(stat[:, 3:4], stat[:, 2:3]).then_inc(s_ev.h, 1)
                W.wait(v, "vector", s_ev, base + 1)
                v.tensor_mul(stat[:, 4:5], stat[:, 3:4], stat[:, 3:4]).then_inc(s_ev.h, 1)
                W.wait(v, "vector", s_ev, base + 2)
                v.tensor_mul(stat[:, 4:5], stat[:, 4:5], stat[:, 1:2]).then_inc(s_ev.h, 1)
                W.wait(v, "vector", s_ev, base + 3)
                v.tensor_scalar(stat[:, 4:5], stat[:, 4:5], -0.5, 1.5, ALU.mult, ALU.add).then_inc(s_ev.h, 1)
                W.wait(v, "vector", s_ev, base + 4)
                v.tensor_mul(stat[:, 3:4], stat[:, 3:4], stat[:, 4:5]).then_inc(s_ev.h, 1)
                W.wait(v, "vector", s_ev, base + 5)
                v.scalar_tensor_tensor(
                    xn_dst, x2, stat[:, 3:4], rmsb[:], ALU.mult, ALU.mult,
                ).then_inc(s_ev.h, 1)
            block.vector(sec_vec2)
            s_ev.n += 6
            xntm_done = s_ev.n
            free_x2[slot] = (s_ev, xntm_done)

            if rms_pend[0] is not None:
                _emit_rms_tr(*rms_pend[0])
            rms_pend[0] = (dst_fm, tt, xn_dst, xntm_done, drv)

    def mm_tile(lhs_view, rhs_view, extra_waits_t, n_out=512):
        q = psA_ctr[0] % 4
        psA_ctr[0] += 1
        fa = free_psA[q]
        nk = KD
        def sec_t(t, q=q, fa=fa, waits=list(extra_waits_t), n_out=n_out):
            for sem, val in waits:
                W.wait(t, "tensor", sem, val)
            if fa is not None:
                W.wait(t, "tensor", fa[0], fa[1])
            for k in range(nk):
                ins = t.matmul(psA[:, q, 0:n_out], lhs_view(k), rhs_view(k),
                               start=(k == 0), stop=(k == nk - 1))
                if k == nk - 1:
                    ins.then_inc(s_mm.h, 1)
        block.tensor(sec_t)
        s_mm.n += 1
        return q, s_mm.n

    wslot_ctr = [0]

    def load_wblock(src_ap, n_tiles):
        j = wslot_ctr[0] % NWB
        wslot_ctr[0] += 1
        fw = free_wb[j]
        assert fw != "pending", "previous tenant of wb slot has no consumption milestone"
        sem = s_wb[j]
        def sec_g(g, j=j, fw=fw, src_ap=src_ap, n_tiles=n_tiles, sem=sem):
            if fw is not None:
                W.wait(g, "gpsimd", fw[0], fw[1])
            g.dma_start(out=wb[j][:, 0:n_tiles, :], in_=src_ap).then_inc(sem.h, 16)
        block.gpsimd(sec_g)
        sem.n += 16
        free_wb[j] = "pending"
        return j, (sem, sem.n)

    def wb_consumed(slot, sem, val):
        free_wb[slot] = (sem, val)

    load_consts()
    consts_done[0] = s_x.n

    _p1drv = RmsDriver(d_xp, xn_fm, ntt=PRE // 128)
    _p1drv.via_sync = True
    p1_done = _p1drv.finish()

    scan_done_ks = {}

    def trio_ks(ks, do_pre, do_own, w0, w1):
        wjf, wvf = load_wblock(d_wf[ks], KD)
        wjzg, wvzg = load_wblock(d_wzg[ks], KD)
        wjz, wvz = load_wblock(d_wz[ks], KD)
        mvf_last, mvzg_last, mvz_last = None, None, None
        if do_pre:
            qf, mvf = mm_tile(
                lambda k, wjf=wjf: wb[wjf][:, k, :],
                lambda k: xn_fm[:, k, 0:PRE],
                [wvf, (s_ev, p1_done)], n_out=PRE)
            mvf_last = mvf
            def sec_f(sc, qf=qf, mvf=mvf, ks=ks):
                W.wait(sc, "scalar", s_mm, mvf)
                sc.activation(
                    fbuf[:, ks, 0:PRE], psA[:, qf, 0:PRE], AF.Sigmoid,
                    bias=fbv[:, ks:ks + 1],
                ).then_inc(s_ea.h, 1)
            block.scalar(sec_f)
            s_ea.n += 1
            free_psA[qf] = (s_ea, s_ea.n)
            f_evac_val = s_ea.n
            qzg, mvzg = mm_tile(
                lambda k, wjzg=wjzg: wb[wjzg][:, k, :],
                lambda k: xn_fm[:, k, 0:PRE], [wvzg], n_out=PRE)
            mvzg_last = mvzg
            fz = free_zgsig[0]
            def sec_zg(sc, qzg=qzg, mvzg=mvzg, fz=fz):
                W.wait(sc, "scalar", s_mm, mvzg)
                if fz is not None:
                    W.wait(sc, "scalar", fz[0], fz[1])
                sc.activation(zgsig[0][:, 0:PRE], psA[:, qzg, 0:PRE], AF.Sigmoid).then_inc(s_ea.h, 1)
            block.scalar(sec_zg)
            s_ea.n += 1
            free_psA[qzg] = (s_ea, s_ea.n)
            zgv = s_ea.n
            qz, mvz = mm_tile(
                lambda k, wjz=wjz: wb[wjz][:, k, :],
                lambda k: xn_fm[:, k, 0:PRE], [wvz], n_out=PRE)
            mvz_last = mvz
            def sec_z(v, qz=qz, mvz=mvz, ks=ks, zgv=zgv):
                W.wait(v, "vector", s_mm, mvz)
                W.wait(v, "vector", s_ea, zgv)
                v.tensor_mul(zbuf[:, ks, 0:PRE], psA[:, qz, 0:PRE], zgsig[0][:, 0:PRE]).then_inc(s_ev.h, 1)
            block.vector(sec_z)
            s_ev.n += 1
            free_psA[qz] = (s_ev, s_ev.n)
            free_zgsig[0] = (s_ev, s_ev.n)
            zv = s_ev.n
            sv = s_ev.n + 1
            def sec_scan(v, ks=ks, f_evac_val=f_evac_val, sv=sv, zv=zv):
                W.wait(v, "vector", s_ea, f_evac_val)
                W.wait(v, "vector", s_ev, zv)
                v.tensor_tensor_scan(
                    hrot[0][:, 0:PRE], fbuf[:, ks, 0:PRE], zbuf[:, ks, 0:PRE],
                    0.0, ALU.mult, ALU.add,
                ).then_inc(s_ev.h, 1)
                W.wait(v, "vector", s_ev, sv)
                v.tensor_copy(carry[:, ks:ks + 1], hrot[0][:, PRE - 1:PRE]).then_inc(s_ev.h, 1)
            block.vector(sec_scan)
            s_ev.n += 2
            scan_done_ks[ks] = s_ev.n
        if do_own:
            sdk = scan_done_ks[ks]
            for tc, sl in enumerate([slice(0, 512), slice(512, 1024)]):
                qf, mvf = mm_tile(
                    lambda k, wjf=wjf: wb[wjf][:, k, :],
                    lambda k, sl=sl: xlnfm[:, k, sl],
                    [wvf, (s_ev, w0 if tc == 0 else w1)])
                mvf_last = mvf
                def sec_f2(sc, qf=qf, mvf=mvf, ks=ks, sl=sl, sdk=sdk):
                    W.wait(sc, "scalar", s_mm, mvf)
                    W.wait(sc, "scalar", s_ev, sdk)
                    sc.activation(
                        fbuf[:, ks, sl], psA[:, qf, :], AF.Sigmoid,
                        bias=fbv[:, ks:ks + 1],
                    ).then_inc(s_ea.h, 1)
                block.scalar(sec_f2)
                s_ea.n += 1
                free_psA[qf] = (s_ea, s_ea.n)
            for tc, sl in enumerate([slice(0, 512), slice(512, 1024)]):
                jz = 1 - tc
                wv_a1 = (s_ev, w0 if tc == 0 else w1)
                qzg, mvzg = mm_tile(
                    lambda k, wjzg=wjzg: wb[wjzg][:, k, :],
                    lambda k, sl=sl: xlnfm[:, k, sl], [wvzg, wv_a1])
                mvzg_last = mvzg
                fz = free_zgsig[jz]
                def sec_zg2(sc, qzg=qzg, mvzg=mvzg, jz=jz, fz=fz):
                    W.wait(sc, "scalar", s_mm, mvzg)
                    if fz is not None:
                        W.wait(sc, "scalar", fz[0], fz[1])
                    sc.activation(zgsig[jz][:], psA[:, qzg, :], AF.Sigmoid).then_inc(s_ea.h, 1)
                block.scalar(sec_zg2)
                s_ea.n += 1
                free_psA[qzg] = (s_ea, s_ea.n)
                zgv2 = s_ea.n
                qz, mvz = mm_tile(
                    lambda k, wjz=wjz: wb[wjz][:, k, :],
                    lambda k, sl=sl: xlnfm[:, k, sl], [wvz, wv_a1])
                mvz_last = mvz
                def sec_z2(v, qz=qz, mvz=mvz, jz=jz, ks=ks, sl=sl, zgv2=zgv2):
                    W.wait(v, "vector", s_mm, mvz)
                    W.wait(v, "vector", s_ea, zgv2)
                    v.tensor_mul(zbuf[:, ks, sl], psA[:, qz, :], zgsig[jz][:]).then_inc(s_ev.h, 1)
                block.vector(sec_z2)
                s_ev.n += 1
                free_psA[qz] = (s_ev, s_ev.n)
                free_zgsig[jz] = (s_ev, s_ev.n)
        wb_consumed(wjf, s_mm, mvf_last)
        wb_consumed(wjzg, s_mm, mvzg_last)
        wb_consumed(wjz, s_mm, mvz_last)

    a1_driver = RmsDriver(d_x, xlnfm, ntt=NTT)
    for ks in range(4):
        trio_ks(ks, True, False, 0, 0)
        a1_driver.step()
        a1_driver.step()
    a1_final = a1_driver.finish()
    a1_t3 = a1_driver.tile_done[3]
    a1_t7 = a1_driver.tile_done[7]
    for ks in range(4, KS):
        trio_ks(ks, True, True, a1_t3, a1_t7)

    if DEBUG_TAPS:
        pv = s_ev.n
        pa = s_ea.n
        def sec_tapp(sy, pv=pv, pa=pa):
            W.wait(sy, "sync", s_ev, pv)
            W.wait(sy, "sync", s_ea, pa)
            sy.dma_start(out=taps["t_pf"], in_=fbuf[:]).then_inc(s_st.h, 16)
            sy.dma_start(out=taps["t_pz"], in_=zbuf[:]).then_inc(s_st.h, 16)
            sy.dma_start(out=taps["t_pc"], in_=carry[:]).then_inc(s_st.h, 16)
        block.sync(sec_tapp)
        s_st.n += 48
        prefix_tap_done = s_st.n
        def sec_guard(v, g=prefix_tap_done):
            W.wait(v, "vector", s_st, g)
        block.vector(sec_guard)
        def sec_guard2(sc, g=prefix_tap_done):
            W.wait(sc, "scalar", s_st, g)
        block.scalar(sec_guard2)

    carry_w = s_ev.n
    def sec_mask(v, cw=carry_w):
        W.wait(v, "vector", s_ev, cw)
        v.tensor_scalar(carry[:], carry[:], maskv[:, 0:1], None, ALU.mult).then_inc(s_ev.h, 1)
    block.vector(sec_mask)
    s_ev.n += 1
    mask_done = s_ev.n

    if DEBUG_TAPS:
        cv = s_ev.n
        def sec_tapc(sy, cv=cv):
            W.wait(sy, "sync", s_ev, cv)
            sy.dma_start(out=taps["t_carry"], in_=carry[:]).then_inc(s_st.h, 16)
        block.sync(sec_tapc)
        s_st.n += 16

    for ks in range(4):
        trio_ks(ks, False, True, a1_final, a1_final)
    if rms_xnbuf_free[0] is not None:
        free_xntm = rms_xnbuf_free[0]
    rms_sq_watermark = s_ea.n

    if DEBUG_TAPS:
        av = a1_final
        def sec_tapxn(sy, av=av):
            W.wait(sy, "sync", s_ev, av)
            sy.dma_start(out=taps["t_xn"], in_=xn_fm[:]).then_inc(s_st.h, 16)
        block.sync(sec_tapxn)
        s_st.n += 16

    a2a_f_done = s_ea.n
    a2a_z_done = s_ev.n

    if DEBUG_TAPS:
        def sec_tapfz(sy, fa=a2a_f_done, za=a2a_z_done):
            W.wait(sy, "sync", s_ea, fa)
            W.wait(sy, "sync", s_ev, za)
            sy.dma_start(out=taps["t_f"], in_=fbuf[:]).then_inc(s_st.h, 16)
            sy.dma_start(out=taps["t_z"], in_=zbuf[:]).then_inc(s_st.h, 16)
        block.sync(sec_tapfz)
        s_st.n += 32


    og_guard = [0]
    mask_done_v = s_ev.n
    for ks in range(KS):
        wja, wva = load_wblock(d_wga[ks], KD)
        wjb, wvb = load_wblock(d_wgb[ks], KD)
        jh = ks % 2
        fh = free_hrot[jh]
        def sec_scan(v, ks=ks, jh=jh, fh=fh, zv=a2a_z_done):
            W.wait(v, "vector", s_ea, a2a_f_done)
            W.wait(v, "vector", s_ev, zv)
            if fh is not None:
                W.wait(v, "vector", fh[0], fh[1])
            v.tensor_tensor_scan(
                hrot[jh][:], fbuf[:, ks, :], zbuf[:, ks, :], carry[:, ks:ks + 1],
                ALU.mult, ALU.add,
            ).then_inc(s_ev.h, 1)
        block.vector(sec_scan)
        s_ev.n += 1
        scan_val = s_ev.n
        for tc in range(2):
            sl = slice(tc * 512, (tc + 1) * 512)
            ja = (ks * 2 + tc) % 3
            qa, mva = mm_tile(
                lambda k, wja=wja: wb[wja][:, k, :],
                lambda k, sl=sl: xlnfm[:, k, sl],
                [wva],
            )
            fg = free_gasil[ja]
            def sec_ga(sc, qa=qa, mva=mva, ja=ja, fg=fg):
                W.wait(sc, "scalar", s_mm, mva)
                if fg is not None:
                    W.wait(sc, "scalar", fg[0], fg[1])
                sc.activation(gasil[ja][:], psA[:, qa, :], AF.Silu).then_inc(s_ea.h, 1)
            block.scalar(sec_ga)
            s_ea.n += 1
            free_psA[qa] = (s_ea, s_ea.n)
            ga_done = s_ea.n
            qb, mvb = mm_tile(
                lambda k, wjb=wjb: wb[wjb][:, k, :],
                lambda k, sl=sl: xlnfm[:, k, sl],
                [wvb],
            )
            ogv = s_ev.n + 1
            def sec_gb(v, qb=qb, mvb=mvb, ja=ja, jh=jh, ks=ks, sl=sl, ga_done=ga_done,
                       scv=scan_val, ogv=ogv, ogg=og_guard[0]):
                W.wait(v, "vector", s_mm, mvb)
                W.wait(v, "vector", s_ea, ga_done)
                W.wait(v, "vector", s_ev, max(scv, ogg))
                v.tensor_mul(zgsig[0][:], psA[:, qb, :], gasil[ja][:]).then_inc(s_ev.h, 1)
                W.wait(v, "vector", s_ev, ogv)
                v.tensor_mul(fbuf[:, ks, sl], zgsig[0][:], hrot[jh][:, sl]).then_inc(s_ev.h, 1)
            block.vector(sec_gb)
            s_ev.n += 2
            og_guard[0] = s_ev.n
            free_psA[qb] = (s_ev, s_ev.n)
            free_gasil[ja] = (s_ev, s_ev.n)
            if tc == 1:
                wb_consumed(wja, s_mm, mva)
                wb_consumed(wjb, s_mm, mvb)
        free_hrot[jh] = (s_ev, s_ev.n)
    gated_done = s_ev.n

    preload_done = [0]
    for tc in range(2):
        for j in range(4):
            tt = tc * 4 + j
            dstt = ytm[tc][:, j, :]
            ch = s_pl[j]
            waits = []
            if ch.n > 0:
                waits.append((ch, ch.n))
            if tc == 0:
                waits.append((s_ea, rms_sq_watermark))
            if tc == 1:
                if j < 3 and free_x2[j] is not None:
                    waits.append(free_x2[j])
                if j == 3 and rms_xnbuf_free[1] is not None:
                    waits.append(rms_xnbuf_free[1])
            def sec_gx(g, tt=tt, dstt=dstt, waits=list(waits), ch=ch):
                for sem, val in waits:
                    W.wait(g, "gpsimd", sem, val)
                g.dma_start(out=dstt, in_=d_x[tt * 128:(tt + 1) * 128, :]).then_inc(ch.h, 16)
            block.gpsimd(sec_gx)
            ch.n += 16
            xv = ch.n
            def sec_x2(v, dstt=dstt, xv=xv, ch=ch):
                W.wait(v, "vector", ch, xv)
                v.tensor_scalar(dstt, dstt, 2.0, None, ALU.mult).then_inc(s_ev.h, 1)
            block.vector(sec_x2)
            s_ev.n += 1
    preload_done[0] = s_ev.n

    if DEBUG_TAPS:
        def sec_tapg(sy, gv=gated_done):
            W.wait(sy, "sync", s_ev, gv)
            sy.dma_start(out=taps["t_gated"], in_=fbuf[:]).then_inc(s_st.h, 16)
        block.sync(sec_tapg)
        s_st.n += 16

    ytm_guard = [0, 0]
    fz_guard = [s_st.n]
    xlnfm_done = {}
    ln1_pe_mm = {}
    statp_guard = [0]
    stat_guard = [0]
    act_out_guard = [0]
    ln1_norm = {}
    y_ready = {}
    hh_last = {}
    hviews_all = {}

    def outproj_both():
        def emit_tr(dc, tc, je, yev):
            yb = tc % 2
            jt = (dc * 2 + tc) % 2
            ft = free_psT[jt]
            def sec_tr(t, je=je, jt=jt, ft=ft, yev=yev):
                W.wait(t, "tensor", s_ea, yev)
                if ft is not None:
                    W.wait(t, "tensor", ft[0], ft[1])
                for j in range(4):
                    ins = t.transpose(psT[jt][:, j, :], evst[je][:, j * 128:(j + 1) * 128], idf[:])
                    if j == 3:
                        ins.then_inc(s_mm.h, 1)
            block.tensor(sec_tr)
            s_mm.n += 1
            trv = s_mm.n
            free_evst[je] = (s_mm, trv)
            def sec_ev(v, jt=jt, dc=dc, trv=trv, yg=ytm_guard[yb], yb=yb, tc=tc,
                       pv=preload_done[0]):
                W.wait(v, "vector", s_mm, trv)
                W.wait(v, "vector", s_st, yg)
                W.wait(v, "vector", s_ev, pv)
                for j in range(4):
                    v.scalar_tensor_tensor(
                        ytm[yb][:, j, dc * 128:(dc + 1) * 128], psT[jt][:, j, :], 1.0,
                        ytm[yb][:, j, dc * 128:(dc + 1) * 128], ALU.mult, ALU.add,
                        accum_out=statq[:, dc * 8 + tc * 4 + j:dc * 8 + tc * 4 + j + 1],
                    ).then_inc(s_ev.h, 1)
            block.vector(sec_ev)
            s_ev.n += 4
            free_psT[jt] = (s_ev, s_ev.n)

            def sec_sq(v, dc=dc, yb=yb, tc=tc):
                for j in range(4):
                    v.scalar_tensor_tensor(
                        xn_fm[:, 0, 0:128], ytm[yb][:, j, dc * 128:(dc + 1) * 128], 1.0,
                        ytm[yb][:, j, dc * 128:(dc + 1) * 128], ALU.mult, ALU.mult,
                        accum_out=statq2[:, dc * 8 + tc * 4 + j:dc * 8 + tc * 4 + j + 1],
                    ).then_inc(s_ev.h, 1)
            block.vector(sec_sq)
            s_ev.n += 4

        pending_tr = None
        for dc in range(KD):
            wjo, wvo = load_wblock(d_wo[dc], KS)
            mv_last = None
            for tc in range(2):
                sl = slice(tc * 512, (tc + 1) * 512)
                qy, mvy = mm_tile(
                    lambda k, wjo=wjo: wb[wjo][:, k, :],
                    lambda k, sl=sl: fbuf[:, k, sl],
                    [wvo, (s_ev, gated_done)],
                )
                mv_last = mvy
                je = (dc * 2 + tc) % 2
                fe = free_evst[je]
                def sec_y(sc, qy=qy, mvy=mvy, dc=dc, je=je, fe=fe):
                    W.wait(sc, "scalar", s_mm, mvy)
                    if fe is not None:
                        W.wait(sc, "scalar", fe[0], fe[1])
                    sc.activation(
                        evst[je][:], psA[:, qy, :], AF.Identity, bias=obv[:, dc:dc + 1],
                    ).then_inc(s_ea.h, 1)
                block.scalar(sec_y)
                s_ea.n += 1
                free_psA[qy] = (s_ea, s_ea.n)
                yev = s_ea.n
                if pending_tr is not None:
                    emit_tr(*pending_tr)
                pending_tr = (dc, tc, je, yev)
            wb_consumed(wjo, s_mm, mv_last)
        emit_tr(*pending_tr)
        y_ready[0] = s_ev.n
        y_ready[1] = s_ev.n

    def ln_stats_chain(stats_done):
        base = s_ev.n
        def sec_var(v, sd=stats_done, base=base, aog=act_out_guard[0]):
            W.wait(v, "vector", s_ev, sd)
            W.wait(v, "vector", s_ea, aog)
            v.tensor_scalar(stat2[:, 0:4], stat[:, 0:4], 1.0 / D, None, ALU.mult).then_inc(s_ev.h, 1)
            v.tensor_scalar(stat2[:, 4:8], stat[:, 4:8], 1.0 / D, None, ALU.mult).then_inc(s_ev.h, 1)
            W.wait(v, "vector", s_ev, base + 2)
            v.tensor_mul(stat2[:, 8:12], stat2[:, 0:4], stat2[:, 0:4]).then_inc(s_ev.h, 1)
            W.wait(v, "vector", s_ev, base + 3)
            v.tensor_sub(stat2[:, 4:8], stat2[:, 4:8], stat2[:, 8:12]).then_inc(s_ev.h, 1)
            W.wait(v, "vector", s_ev, base + 4)
            v.tensor_scalar(stat2[:, 4:8], stat2[:, 4:8], 1.0, EPS, ALU.mult, ALU.add).then_inc(s_ev.h, 1)
        block.vector(sec_var)
        s_ev.n += 5
        var_done = s_ev.n
        def sec_sq(sc, var_done=var_done):
            W.wait(sc, "scalar", s_ev, var_done)
            sc.activation(stat2[:, 8:12], stat2[:, 4:8], AF.Sqrt).then_inc(s_ea.h, 1)
        block.scalar(sec_sq)
        s_ea.n += 1
        sq_done = s_ea.n
        base = s_ev.n
        def sec_nr(v, sq_done=sq_done, base=base):
            W.wait(v, "vector", s_ea, sq_done)
            v.reciprocal(stat2[:, 12:16], stat2[:, 8:12]).then_inc(s_ev.h, 1)
            W.wait(v, "vector", s_ev, base + 1)
            v.tensor_mul(stat[:, 8:12], stat2[:, 12:16], stat2[:, 12:16]).then_inc(s_ev.h, 1)
            W.wait(v, "vector", s_ev, base + 2)
            v.tensor_mul(stat[:, 8:12], stat[:, 8:12], stat2[:, 4:8]).then_inc(s_ev.h, 1)
            W.wait(v, "vector", s_ev, base + 3)
            v.tensor_scalar(stat[:, 8:12], stat[:, 8:12], -0.5, 1.5, ALU.mult, ALU.add).then_inc(s_ev.h, 1)
            W.wait(v, "vector", s_ev, base + 4)
            v.tensor_mul(stat2[:, 12:16], stat2[:, 12:16], stat[:, 8:12]).then_inc(s_ev.h, 1)
        block.vector(sec_nr)
        s_ev.n += 5
        return s_ev.n

    def ln1_dve(tc):
        yb = tc % 2
        y_done = y_ready[tc]
        def sec_s1(v, yb=yb, tc=tc, y_done=y_done, stg=stat_guard[0]):
            W.wait(v, "vector", s_ev, y_done)
            W.wait(v, "vector", s_ev, stg)
            for j in range(4):
                v.reduce_sum(stat[:, j:j + 1], statq[:, tc * 4 + j:128:8], axis=AX.X).then_inc(s_ev.h, 1)
                v.reduce_sum(stat[:, 4 + j:5 + j], statq2[:, tc * 4 + j:128:8], axis=AX.X).then_inc(s_ev.h, 1)
        block.vector(sec_s1)
        s_ev.n += 8
        r_done = ln_stats_chain(s_ev.n)
        statp_guard[0] = s_ea.n
        stat_guard[0] = r_done
        for j in range(4):
            tt = tc * 4 + j
            def sec_norm(v, j=j, rv=r_done, yb=yb):
                W.wait(v, "vector", s_ev, rv)
                v.tensor_scalar(
                    ytm[yb][:, j, :], ytm[yb][:, j, :], stat2[:, j:j + 1], stat2[:, 12 + j:13 + j],
                    ALU.subtract, ALU.mult,
                ).then_inc(s_ev.h, 1)
            block.vector(sec_norm)
            s_ev.n += 1
            xlnv = s_ev.n
            ln1_norm[(tc, j)] = xlnv

    def ln1_pe(tc):
        yb = tc % 2
        for j in range(4):
            tt = tc * 4 + j
            xlnv = ln1_norm[(tc, j)]
            for hb in range(2):
                fb_ = free_psB[hb]
                def sec_t(t, hb=hb, fb_=fb_, xlnv=xlnv, j=j, yb=yb):
                    W.wait(t, "tensor", s_ev, xlnv)
                    if fb_ is not None:
                        W.wait(t, "tensor", fb_[0], fb_[1])
                    for jj in range(8):
                        dc = hb * 8 + jj
                        ins = t.transpose(psB[:, hb, jj, :], ytm[yb][:, j, dc * 128:(dc + 1) * 128], idb[:])
                        if jj == 7:
                            ins.then_inc(s_mm.h, 1)
                block.tensor(sec_t)
                s_mm.n += 1
                trv = s_mm.n
                def sec_ev2(v, hb=hb, tt=tt, trv=trv):
                    W.wait(v, "vector", s_mm, trv)
                    v.tensor_copy(
                        xn_fm[:, hb * 8:(hb + 1) * 8, tt * 128:(tt + 1) * 128],
                        psB[:, hb, :, :],
                    ).then_inc(s_ev.h, 1)
                block.vector(sec_ev2)
                s_ev.n += 1
                free_psB[hb] = (s_ev, s_ev.n)
        xlnfm_done[tc] = s_ev.n
        ln1_pe_mm[tc] = s_mm.n

    gu_loaded = [0]
    gu_q = []

    def preload_gu(n):
        for _ in range(n):
            if gu_loaded[0] >= 2 * KI:
                return
            ic = gu_loaded[0] % KI
            wjg, wvg = load_wblock(d_gu[ic, 0], 16)
            wju, wvu = load_wblock(d_gu[ic, 1], 16)
            def sec_m128(v, wjg=wjg, wju=wju, wvg=wvg, wvu=wvu):
                W.wait(v, "vector", wvg[0], wvg[1])
                W.wait(v, "vector", wvu[0], wvu[1])
                v.tensor_scalar(wb[wjg][:, 0:16, :], wb[wjg][:, 0:16, :], -128.0, None, ALU.add).then_inc(s_ev.h, 1)
                v.tensor_scalar(wb[wju][:, 0:16, :], wb[wju][:, 0:16, :], -128.0, None, ALU.add).then_inc(s_ev.h, 1)
            block.vector(sec_m128)
            s_ev.n += 2
            gu_q.append((wjg, wju, s_ev.n))
            gu_loaded[0] += 1

    def ffn_gu(tc):
        xln_ready = xlnfm_done[tc]
        sl = slice(tc * 512, (tc + 1) * 512)
        hviews = []
        for ic in range(KI):
            if ic < 32:
                buf, idx = fbuf, ic
            else:
                buf, idx = zbuf, ic - 32
            hviews.append(buf[:].rearrange("p a (b c) -> p (a b) c", b=2)[:, idx, :])
        hviews_all[tc] = hviews
        hh_done = 0
        for ic in range(KI):
            want = tc * KI + ic + 2
            while gu_loaded[0] < min(want, (tc + 1) * KI):
                preload_gu(1)
            wjg, wju, wready = gu_q.pop(0)
            qg, mvg = mm_tile(
                lambda k, wjg=wjg: wb[wjg][:, k, :],
                lambda k, sl=sl: xn_fm[:, k, sl],
                [(s_ev, wready), (s_ev, xln_ready)],
            )
            jg = ic % 3
            fg = free_gasil[jg]
            def sec_g(sc, qg=qg, mvg=mvg, ic=ic, jg=jg, fg=fg):
                W.wait(sc, "scalar", s_mm, mvg)
                if fg is not None:
                    W.wait(sc, "scalar", fg[0], fg[1])
                sc.activation(
                    gasil[jg][:], psA[:, qg, :], AF.Silu, scale=gscv[:, ic:ic + 1],
                ).then_inc(s_ea.h, 1)
            block.scalar(sec_g)
            s_ea.n += 1
            free_psA[qg] = (s_ea, s_ea.n)
            g_done = s_ea.n
            qu, mvu = mm_tile(
                lambda k, wju=wju: wb[wju][:, k, :],
                lambda k, sl=sl: xn_fm[:, k, sl],
                [],
            )
            def sec_u(v, qu=qu, mvu=mvu, ic=ic, jg=jg, g_done=g_done,
                      hv=hviews[ic], fzg=fz_guard[0]):
                W.wait(v, "vector", s_mm, mvu)
                W.wait(v, "vector", s_ea, g_done)
                W.wait(v, "vector", s_st, fzg)
                v.scalar_tensor_tensor(
                    hv, psA[:, qu, :], uscv[:, ic:ic + 1], gasil[jg][:],
                    ALU.mult, ALU.mult,
                ).then_inc(s_ev.h, 1)
            block.vector(sec_u)
            s_ev.n += 1
            free_psA[qu] = (s_ev, s_ev.n)
            free_gasil[jg] = (s_ev, s_ev.n)
            wb_consumed(wjg, s_mm, mvg)
            wb_consumed(wju, s_mm, mvu)
            hh_done = s_ev.n
        hh_last[tc] = hh_done

    def ffn_down(tc, pump=None):
        yb = tc % 2
        hviews = hviews_all[tc]
        hhv_all = hh_last[tc]

        def emit_dtr(hc, je, dev):
            jt = hc % 2
            ft = free_psT[jt]
            def sec_tr(t, je=je, jt=jt, ft=ft, dev=dev):
                W.wait(t, "tensor", s_ea, dev)
                if ft is not None:
                    W.wait(t, "tensor", ft[0], ft[1])
                for j in range(4):
                    ins = t.transpose(psT[jt][:, j, :], evst[je][:, j * 128:(j + 1) * 128], idf[:])
                    if j == 3:
                        ins.then_inc(s_mm.h, 1)
            block.tensor(sec_tr)
            s_mm.n += 1
            trv = s_mm.n
            free_evst[je] = (s_mm, trv)
            def sec_ev(v, jt=jt, hc=hc, trv=trv, yg=ytm_guard[yb], yb=yb, tc=tc):
                W.wait(v, "vector", s_mm, trv)
                W.wait(v, "vector", s_st, yg)
                for j in range(4):
                    v.scalar_tensor_tensor(
                        ytm[yb][:, j, hc * 128:(hc + 1) * 128], psT[jt][:, j, :], 1.0,
                        ytm[yb][:, j, hc * 128:(hc + 1) * 128], ALU.mult, ALU.add,
                        accum_out=statq[:, hc * 8 + tc * 4 + j:hc * 8 + tc * 4 + j + 1],
                    ).then_inc(s_ev.h, 1)
            block.vector(sec_ev)
            s_ev.n += 4
            free_psT[jt] = (s_ev, s_ev.n)

            def sec_sq(v, hc=hc, yb=yb, tc=tc):
                for j in range(4):
                    v.scalar_tensor_tensor(
                        xn_fm[:, 0, 0:128], ytm[yb][:, j, hc * 128:(hc + 1) * 128], 1.0,
                        ytm[yb][:, j, hc * 128:(hc + 1) * 128], ALU.mult, ALU.mult,
                        accum_out=statq2[:, hc * 8 + tc * 4 + j:hc * 8 + tc * 4 + j + 1],
                    ).then_inc(s_ev.h, 1)
            block.vector(sec_sq)
            s_ev.n += 4

        pending_dtr = None
        dn_q = []

        def load_dn(hc, g4):
            wj, wv = load_wblock(d_dn[hc, g4], 16)
            def sec_m128b(v, wj=wj, wv=wv):
                W.wait(v, "vector", wv[0], wv[1])
                v.tensor_scalar(wb[wj][:, 0:16, :], wb[wj][:, 0:16, :], -128.0, None, ALU.add).then_inc(s_ev.h, 1)
            block.vector(sec_m128b)
            s_ev.n += 1
            dn_q.append((wj, s_ev.n))

        dn_loaded = [0]

        def preload_dn(n):
            while n > 0 and dn_loaded[0] < 4 * KD:
                load_dn(dn_loaded[0] // 4, dn_loaded[0] % 4)
                dn_loaded[0] += 1
                n -= 1

        preload_dn(3)
        for hc in range(KD):
            q = psA_ctr[0] % 4
            psA_ctr[0] += 1
            fa = free_psA[q]
            for g4 in range(4):
                preload_dn(1)
                wj, wready = dn_q.pop(0)
                def sec_dn(t, q=q, fa=fa, g4=g4, wj=wj, wready=wready, hhv=hhv_all):
                    W.wait(t, "tensor", s_ev, wready)
                    W.wait(t, "tensor", s_ev, hhv)
                    if g4 == 0 and fa is not None:
                        W.wait(t, "tensor", fa[0], fa[1])
                    for k in range(16):
                        ins = t.matmul(psA[:, q, :], wb[wj][:, k, :], hviews[g4 * 16 + k],
                                       start=(g4 == 0 and k == 0), stop=(g4 == 3 and k == 15))
                        if k == 15:
                            ins.then_inc(s_mm.h, 1)
                block.tensor(sec_dn)
                s_mm.n += 1
                wb_consumed(wj, s_mm, s_mm.n)
            mvd = s_mm.n
            je = hc % 2
            fe = free_evst[je]
            def sec_de(sc, q=q, mvd=mvd, hc=hc, je=je, fe=fe):
                W.wait(sc, "scalar", s_mm, mvd)
                if fe is not None:
                    W.wait(sc, "scalar", fe[0], fe[1])
                sc.activation(
                    evst[je][:], psA[:, q, :], AF.Identity, scale=dscv[:, hc:hc + 1],
                ).then_inc(s_ea.h, 1)
            block.scalar(sec_de)
            s_ea.n += 1
            free_psA[q] = (s_ea, s_ea.n)
            dev = s_ea.n
            if pending_dtr is not None:
                emit_dtr(*pending_dtr)
            pending_dtr = (hc, je, dev)
            if pump is not None:
                pump()
        emit_dtr(*pending_dtr)
        return s_ev.n

    def ln2_steps(tc, ffn_done):
        steps = []
        hold = {}
        yb = tc % 2

        def tile_step(j):
            def go():
                def sec_s1(v, j=j, tc=tc, fd=ffn_done, stg=stat_guard[0]):
                    W.wait(v, "vector", s_ev, fd)
                    W.wait(v, "vector", s_ev, stg)
                    v.reduce_sum(stat[:, j:j + 1], statq[:, tc * 4 + j:128:8], axis=AX.X).then_inc(s_ev.h, 1)
                    v.reduce_sum(stat[:, 4 + j:5 + j], statq2[:, tc * 4 + j:128:8], axis=AX.X).then_inc(s_ev.h, 1)
                block.vector(sec_s1)
                s_ev.n += 2
            return go

        def stats_step():
            hold["r_done"] = ln_stats_chain(s_ev.n)
            def sec_mr(v, rv=hold["r_done"]):
                W.wait(v, "vector", s_ev, rv)
                v.scalar_tensor_tensor(
                    stat[:, 12:16], stat2[:, 0:4], -1.0, stat2[:, 12:16],
                    ALU.mult, ALU.mult,
                ).then_inc(s_ev.h, 1)
            block.vector(sec_mr)
            s_ev.n += 1
            hold["mr_done"] = s_ev.n
            statp_guard[0] = s_ea.n
            stat_guard[0] = hold["r_done"]

        def out_step(j):
            def go():
                nonlocal_state_out(tc, j, yb, hold["r_done"], hold["mr_done"])
            return go

        for j in range(4):
            steps.append(tile_step(j))
        steps.append(stats_step)
        for j in range(4):
            steps.append(out_step(j))
        return steps

    ln2_last_store = [0]

    qst = {}

    def nonlocal_state_out(tc, j, yb, r_done, mr_done):
        nonlocal free_xstage
        tt = tc * 4 + j
        for c in range(4):
            if tc == 1:
                k = (j * 2 + c // 2) % 2
                ch = c % 2
                if k == 0:
                    reg, key = xstage[:, ch * 512:(ch + 1) * 512], ("x", ch)
                else:
                    reg, key = evst[ch][:], ("e", ch)
            else:
                ch = c
                reg, key = xstage[:, c * 512:(c + 1) * 512], ("x", c)
            qs = qst.get(key)
            src = ytm[yb][:, j, c * 512:(c + 1) * 512]
            dst = d_out[tt * 128:(tt + 1) * 128, c * 512:(c + 1) * 512]
            if c % 2 == 0:
                def sec_oa(sc, j=j, rv=mr_done, qs=qs, reg=reg, src=src):
                    W.wait(sc, "scalar", s_ev, rv)
                    if qs is not None:
                        W.wait(sc, "scalar", qs[0], qs[1])
                    sc.activation(
                        reg, src, AF.Identity,
                        scale=stat2[:, 12 + j:13 + j], bias=stat[:, 12 + j:13 + j],
                    ).then_inc(s_ea.h, 1)
                block.scalar(sec_oa)
                s_ea.n += 1
                pv = (s_ea, s_ea.n)
            else:
                def sec_ov(v, j=j, rv=r_done, qs=qs, reg=reg, src=src):
                    W.wait(v, "vector", s_ev, rv)
                    if qs is not None:
                        W.wait(v, "vector", qs[0], qs[1])
                    v.tensor_scalar(
                        reg, src, stat2[:, j:j + 1], stat2[:, 12 + j:13 + j],
                        ALU.subtract, ALU.mult,
                    ).then_inc(s_ev.h, 1)
                block.vector(sec_ov)
                s_ev.n += 1
                pv = (s_ev, s_ev.n)
            osem = s_oq[c]
            if c % 2 == 0:
                def sec_sta(sy, pv=pv, dst=dst, reg=reg, osem=osem):
                    W.wait(sy, "sync", pv[0], pv[1])
                    sy.dma_start(out=dst, in_=reg).then_inc(osem.h, 16)
                block.sync(sec_sta)
            else:
                def sec_stg(g, pv=pv, dst=dst, reg=reg, osem=osem):
                    W.wait(g, "gpsimd", pv[0], pv[1])
                    g.dma_start(out=dst, in_=reg).then_inc(osem.h, 16)
                block.gpsimd(sec_stg)
            osem.n += 16
            qst[key] = (osem, osem.n)
        act_out_guard[0] = s_ea.n
        ytm_guard[yb] = s_st.n

    outproj_both()
    ln1_dve(0)
    ln1_pe(0)
    ln1_dve(1)
    ffn_gu(0)
    ln1_pe(1)
    f0 = ffn_down(0)
    ffn_gu(1)
    steps0 = ln2_steps(0, f0)
    sit = iter(steps0)
    def pump():
        s = next(sit, None)
        if s is not None:
            s()
    f1 = ffn_down(1, pump=pump)
    for s in sit:
        s()
    for s in ln2_steps(1, f1):
        s()

    def sec_fin(sy):
        sy.wait_ge(s_st.h, s_st.n)
        for c in range(4):
            sy.wait_ge(s_oq[c].h, s_oq[c].n)
    block.sync(sec_fin)
    block.flush()

    es.close()
    return nc


_nc_cache = None


def _get_program():
    global _nc_cache
    if _nc_cache is None:
        _nc_cache = build_program()
    return _nc_cache


def _pack_proj(Wnp):
    WT = np.ascontiguousarray(Wnp.T)
    t = WT.reshape(KD, 128, KS, 128)
    return np.ascontiguousarray(t.transpose(2, 1, 0, 3))


def _pack_wout(Wnp):
    WoT = np.ascontiguousarray(Wnp.T)
    t = WoT.reshape(KS, 128, KD, 128)
    return np.ascontiguousarray(t.transpose(2, 1, 0, 3))


def _int32_to_u8(a):
    a = np.ascontiguousarray(np.asarray(a, dtype=np.int32))
    assert a.dtype.byteorder in ("=", "<")
    return a.view(np.uint8).reshape(*a.shape, 4)[..., 0]


def _pack_gu(gate_w, up_w):
    def tiles(Wq):
        WT = np.ascontiguousarray(Wq.T)
        t = WT.reshape(KD, 128, KI, 128)
        return np.ascontiguousarray(t.transpose(2, 1, 0, 3))
    g = tiles(_int32_to_u8(gate_w))
    u = tiles(_int32_to_u8(up_w))
    return np.ascontiguousarray(np.stack([g, u], axis=1))


def _pack_down(down_w):
    Wq = _int32_to_u8(down_w)
    WT = np.ascontiguousarray(Wq.T)
    t = WT.reshape(KI, 128, KD, 128)
    t = t.transpose(2, 0, 1, 3)
    t = t.reshape(KD, 4, 16, 128, 128).transpose(0, 1, 3, 2, 4)
    return np.ascontiguousarray(t)


def kernel(L0_attn_x, L0_attn_rms_w, L0_attn_Wf_w, L0_attn_Wf_b, L0_attn_Wzg_w,
           L0_attn_Wz_w, L0_attn_Wga_w, L0_attn_Wgb_w, L0_attn_Wout_w,
           L0_attn_Wout_b, L0_ffn_gate_w, L0_ffn_gate_scale, L0_ffn_up_w,
           L0_ffn_up_scale, L0_ffn_down_w, L0_ffn_down_scale):
    x = np.ascontiguousarray(np.asarray(L0_attn_x, dtype=np.float32))
    f32 = lambda a: np.ascontiguousarray(np.asarray(a, dtype=np.float32))
    rms_w = f32(L0_attn_rms_w)

    shared = {
        "rms_bcast": np.ascontiguousarray(np.tile(rms_w[None, :], (128, 1)).astype(ml_dtypes.bfloat16)),
        "ident_bf": np.eye(128, dtype=ml_dtypes.bfloat16),
        "ident_f32": np.eye(128, dtype=np.float32),
        "f_bias": np.ascontiguousarray(f32(L0_attn_Wf_b).reshape(KS, 128).T),
        "out_bias": np.ascontiguousarray(f32(L0_attn_Wout_b).reshape(KD, 128).T),
        "gate_sc": np.ascontiguousarray(f32(L0_ffn_gate_scale).reshape(KI, 128).T),
        "up_sc": np.ascontiguousarray(f32(L0_ffn_up_scale).reshape(KI, 128).T),
        "down_sc": np.ascontiguousarray(f32(L0_ffn_down_scale).reshape(KD, 128).T),
        "wf": _pack_proj(f32(L0_attn_Wf_w)),
        "wzg": _pack_proj(f32(L0_attn_Wzg_w)),
        "wz": _pack_proj(f32(L0_attn_Wz_w)),
        "wga": _pack_proj(f32(L0_attn_Wga_w)),
        "wgb": _pack_proj(f32(L0_attn_Wgb_w)),
        "wo": _pack_wout(f32(L0_attn_Wout_w)),
        "wgu": _pack_gu(L0_ffn_gate_w, L0_ffn_up_w),
        "wdn": _pack_down(L0_ffn_down_w),
    }

    in_maps = []
    for core in range(8):
        b, half = core // 2, core % 2
        m = dict(shared)
        m["x_own"] = np.ascontiguousarray(x[b, half * NT:(half + 1) * NT, :])
        m["x_pre"] = np.ascontiguousarray(x[b, NT - PRE:NT, :])
        m["maskv"] = np.full((128, 1), 1.0 if half == 1 else 0.0, np.float32)
        in_maps.append(m)

    nc = _get_program()
    res = run_bass_kernel_spmd(nc, in_maps, core_ids=list(range(8)))

    out = np.empty((B, T, D), dtype=np.float32)
    for core in range(8):
        b, half = core // 2, core % 2
        out[b, half * NT:(half + 1) * NT, :] = res.results[core]["out"]
    return out


# version 29
# speedup vs baseline: 1.0913x; 1.0116x over previous
import contextlib
import numpy as np
import ml_dtypes
import concourse.bass as bass
import concourse.mybir as mybir
from concourse.bass_utils import run_bass_kernel_spmd

F32 = mybir.dt.float32
BF16 = mybir.dt.bfloat16
U8 = mybir.dt.uint8
AF = mybir.ActivationFunctionType
ALU = mybir.AluOpType
AX = mybir.AxisListType

B, T, D, S, I = 4, 2048, 2048, 2048, 8192
NT = 1024
NTT = NT // 128
KD = D // 128
KS = S // 128
KI = I // 128
EPS = 1e-5
PRE = 128

DEBUG_TAPS = False


class Sem:
    def __init__(self, es, nc, name):
        self.h = es.enter_context(nc.semaphore(name))
        self.n = 0
        self.name = name

    def inc(self, instr, k=1):
        instr.then_inc(self.h, k)
        self.n += k
        return self.n


class Waiter:

    def __init__(self):
        self.seen = {}

    def wait(self, eng, engname, sem, val):
        if val is None or val <= 0:
            return
        key = (engname, sem.name)
        if self.seen.get(key, -1) >= val:
            return
        eng.wait_ge(sem.h, val)
        self.seen[key] = val


def build_program():
    nc = bass.Bass("TRN2", target_bir_lowering=False)
    es = contextlib.ExitStack()

    d_x = nc.dram_tensor("x_own", [NT, D], F32, kind="ExternalInput").ap()
    d_xp = nc.dram_tensor("x_pre", [PRE, D], F32, kind="ExternalInput").ap()
    d_mask = nc.dram_tensor("maskv", [128, 1], F32, kind="ExternalInput").ap()
    d_rmsb = nc.dram_tensor("rms_bcast", [128, D], BF16, kind="ExternalInput").ap()
    d_idb = nc.dram_tensor("ident_bf", [128, 128], BF16, kind="ExternalInput").ap()
    d_idf = nc.dram_tensor("ident_f32", [128, 128], F32, kind="ExternalInput").ap()
    d_fb = nc.dram_tensor("f_bias", [128, KS], F32, kind="ExternalInput").ap()
    d_ob = nc.dram_tensor("out_bias", [128, KD], F32, kind="ExternalInput").ap()
    d_gsc = nc.dram_tensor("gate_sc", [128, KI], F32, kind="ExternalInput").ap()
    d_usc = nc.dram_tensor("up_sc", [128, KI], F32, kind="ExternalInput").ap()
    d_dsc = nc.dram_tensor("down_sc", [128, KD], F32, kind="ExternalInput").ap()
    d_wf = nc.dram_tensor("wf", [KS, 128, KD, 128], F32, kind="ExternalInput").ap()
    d_wzg = nc.dram_tensor("wzg", [KS, 128, KD, 128], F32, kind="ExternalInput").ap()
    d_wz = nc.dram_tensor("wz", [KS, 128, KD, 128], F32, kind="ExternalInput").ap()
    d_wga = nc.dram_tensor("wga", [KS, 128, KD, 128], F32, kind="ExternalInput").ap()
    d_wgb = nc.dram_tensor("wgb", [KS, 128, KD, 128], F32, kind="ExternalInput").ap()
    d_wo = nc.dram_tensor("wo", [KD, 128, KS, 128], F32, kind="ExternalInput").ap()
    d_gu = nc.dram_tensor("wgu", [KI, 2, 128, 16, 128], U8, kind="ExternalInput").ap()
    d_dn = nc.dram_tensor("wdn", [KD, 4, 128, 16, 128], U8, kind="ExternalInput").ap()

    d_out = nc.dram_tensor("out", [NT, D], F32, kind="ExternalOutput").ap()

    taps = {}
    if DEBUG_TAPS:
        taps["t_xn"] = nc.dram_tensor("t_xn", [128, KD, NT], BF16, kind="ExternalOutput").ap()
        taps["t_f"] = nc.dram_tensor("t_f", [128, KS, NT], BF16, kind="ExternalOutput").ap()
        taps["t_z"] = nc.dram_tensor("t_z", [128, KS, NT], BF16, kind="ExternalOutput").ap()
        taps["t_carry"] = nc.dram_tensor("t_carry", [128, KS], F32, kind="ExternalOutput").ap()
        taps["t_gated"] = nc.dram_tensor("t_gated", [128, KS, NT], BF16, kind="ExternalOutput").ap()
        taps["t_xln"] = nc.dram_tensor("t_xln", [NT, D], F32, kind="ExternalOutput").ap()
        taps["t_pf"] = nc.dram_tensor("t_pf", [128, KS, NT], BF16, kind="ExternalOutput").ap()
        taps["t_pz"] = nc.dram_tensor("t_pz", [128, KS, NT], BF16, kind="ExternalOutput").ap()
        taps["t_pc"] = nc.dram_tensor("t_pc", [128, KS], F32, kind="ExternalOutput").ap()
        taps["t_y"] = nc.dram_tensor("t_y", [128, 4, D], F32, kind="ExternalOutput").ap()
        taps["t_u"] = nc.dram_tensor("t_u", [128, 4, D], F32, kind="ExternalOutput").ap()
        taps["t_stat"] = nc.dram_tensor("t_stat", [128, 32], F32, kind="ExternalOutput").ap()

    sb = lambda name, shape, dt: es.enter_context(nc.sbuf_tensor(name, shape, dt))
    xn_fm = sb("xn_fm", [128, KD, NT], BF16)
    fbuf = sb("fbuf", [128, KS, NT], BF16)
    zbuf = sb("zbuf", [128, KS, NT], BF16)
    xlnfm = sb("xlnfm", [128, KD, NT], BF16)
    ytm = [sb(f"ytm{j}", [128, 4, D], BF16) for j in range(2)]
    NWB = 4
    wb = [sb(f"wb{j}", [128, 16, 128], BF16) for j in range(NWB)]
    xstage = sb("xstage", [128, D], F32)
    xntm = sb("xntm", [128, D], BF16)
    zgsig = [sb(f"zgsig{j}", [128, 512], BF16) for j in range(2)]
    gasil = [sb(f"gasil{j}", [128, 512], BF16) for j in range(3)]
    evst = [sb(f"evst{j}", [128, 512], F32) for j in range(2)]
    hrot = [sb(f"hrot{j}", [128, NT], BF16) for j in range(2)]
    carry = sb("carry", [128, KS], F32)
    rmsb = sb("rmsb", [128, D], BF16)
    idb = sb("idb", [128, 128], BF16)
    idf = sb("idf", [128, 128], F32)
    fbv = sb("fbv", [128, KS], F32)
    obv = sb("obv", [128, KD], F32)
    gscv = sb("gscv", [128, KI], F32)
    uscv = sb("uscv", [128, KI], F32)
    dscv = sb("dscv", [128, KD], F32)
    maskv = sb("maskv_sb", [128, 1], F32)
    stat = sb("stat", [128, 16], F32)
    stat2 = sb("stat2", [128, 16], F32)
    statp = sb("statp", [128, 16], F32)
    statq = sb("statq", [128, 128], F32)
    statq2 = sb("statq2", [128, 128], F32)

    psA = es.enter_context(nc.psum_tensor("psA", [128, 4, 512], F32))
    psT = [es.enter_context(nc.psum_tensor(f"psT{j}", [128, 4, 128], F32)) for j in range(2)]
    psB = es.enter_context(nc.psum_tensor("psB", [128, 2, 8, 128], BF16))

    s_w = Sem(es, nc, "s_w")
    s_wb = [Sem(es, nc, f"s_wb{j}") for j in range(4)]
    s_x2 = [Sem(es, nc, f"s_x2{j}") for j in range(3)]
    s_pl = [Sem(es, nc, f"s_pl{j}") for j in range(4)]
    s_x = Sem(es, nc, "s_x")
    s_mm = Sem(es, nc, "s_mm")
    s_ea = Sem(es, nc, "s_ea")
    s_ev = Sem(es, nc, "s_ev")
    s_st = Sem(es, nc, "s_st")
    s_oq = [Sem(es, nc, f"s_oq{c}") for c in range(4)]

    W = Waiter()

    free_psA = [None] * 4
    free_psT = [None] * 2
    free_psB = [None] * 2
    free_wb = [None] * NWB
    free_zgsig = [None] * 2
    free_gasil = [None] * 3
    free_evst = [None] * 2
    free_hrot = [None] * 2
    free_xstage = None
    free_xntm = None

    psA_ctr = [0]
    free_x2 = [None] * 4

    _block = es.enter_context(nc.Block())

    class _EngineBuffer:

        def __init__(self):
            self.pend = {k: [] for k in ("sync", "gpsimd", "scalar", "tensor", "vector")}

        def _add(self, eng, fn):
            self.pend[eng].append(fn)
            if len(self.pend[eng]) >= 48:
                self.flush(eng)

        def sync(self, fn):
            self._add("sync", fn)

        def gpsimd(self, fn):
            self._add("gpsimd", fn)

        def scalar(self, fn):
            self._add("scalar", fn)

        def tensor(self, fn):
            self._add("tensor", fn)

        def vector(self, fn):
            self._add("vector", fn)

        def flush(self, eng=None):
            engs = [eng] if eng else list(self.pend)
            for e in engs:
                fns = self.pend[e]
                if not fns:
                    continue
                self.pend[e] = []
                getattr(_block, e)(lambda handle, fns=fns: [f(handle) for f in fns])

    block = _EngineBuffer()


    def load_consts():
        def sec(sy):
            sy.dma_start(out=rmsb[:], in_=d_rmsb).then_inc(s_x.h, 16)
            sy.dma_start(out=idb[:], in_=d_idb).then_inc(s_x.h, 16)
            sy.dma_start(out=idf[:], in_=d_idf).then_inc(s_x.h, 16)
            sy.dma_start(out=fbv[:], in_=d_fb).then_inc(s_x.h, 16)
            sy.dma_start(out=obv[:], in_=d_ob).then_inc(s_x.h, 16)
            sy.dma_start(out=gscv[:], in_=d_gsc).then_inc(s_x.h, 16)
            sy.dma_start(out=uscv[:], in_=d_usc).then_inc(s_x.h, 16)
            sy.dma_start(out=dscv[:], in_=d_dsc).then_inc(s_x.h, 16)
            sy.dma_start(out=maskv[:], in_=d_mask).then_inc(s_x.h, 16)
        block.sync(sec)
        for _ in range(9):
            s_x.n += 16
    consts_done = [0]

    class RmsDriver:

        def __init__(self, d_src, dst_fm, ntt=NTT):
            self.d_src = d_src
            self.dst = dst_fm
            self.ntt = ntt
            self.tt = 0
            self.done_val = 0
            self.tile_done = []

        def step(self):
            if self.tt >= self.ntt:
                return False
            _rms_tile(self.d_src, self.dst, self.tt, self)
            self.tt += 1
            return True

        def finish(self):
            while self.step():
                pass
            if rms_pend[0] is not None:
                _emit_rms_tr(*rms_pend[0])
                rms_pend[0] = None
            return self.done_val

    rms_xnbuf_free = [None, None]
    rms_statp_free = [None, None]
    rms_pend = [None]

    def _emit_rms_tr(dst_fm, tt, xn_src, xnv, drv):
        for hb in range(2):
            fb_ = free_psB[hb]
            def sec_t(t, hb=hb, fb_=fb_, xnv=xnv, xn_src=xn_src):
                W.wait(t, "tensor", s_ev, xnv)
                W.wait(t, "tensor", s_x, consts_done[0])
                if fb_ is not None:
                    W.wait(t, "tensor", fb_[0], fb_[1])
                for j in range(8):
                    dc = hb * 8 + j
                    ins = t.transpose(psB[:, hb, j, :], xn_src[:, dc * 128:(dc + 1) * 128], idb[:])
                    if j == 7:
                        ins.then_inc(s_mm.h, 1)
            block.tensor(sec_t)
            s_mm.n += 1
            tr_done = s_mm.n
            def sec_vec3(v, tt=tt, hb=hb, tr_done=tr_done, dst_fm=dst_fm):
                W.wait(v, "vector", s_mm, tr_done)
                v.tensor_copy(
                    dst_fm[:, hb * 8:(hb + 1) * 8, tt * 128:(tt + 1) * 128],
                    psB[:, hb, :, :],
                ).then_inc(s_ev.h, 1)
            block.vector(sec_vec3)
            s_ev.n += 1
            free_psB[hb] = (s_ev, s_ev.n)
            drv.done_val = s_ev.n
        drv.tile_done.append(drv.done_val)
        rms_xnbuf_free[tt % 2] = (s_mm, s_mm.n)

    def _rms_tile(d_src, dst_fm, tt, drv):
        nonlocal free_xstage
        if True:
            if getattr(drv, "via_sync", False):
                slot = tt % 3
                x2 = xstage[:]
                xsem = s_x2[slot]
                def sec_gs(sy, tt=tt, x2=x2, xsem=xsem, d_src=d_src):
                    sy.dma_start(out=x2, in_=d_src[tt * 128:(tt + 1) * 128, :]).then_inc(xsem.h, 16)
                block.sync(sec_gs)
                xsem.n += 16
                x_loaded = (xsem, xsem.n)
            else:
                slot = tt % 3
                x2 = ytm[1][:, slot, :]
                fx = free_x2[slot]
                xsem = s_x2[slot]
                def sec_g(g, tt=tt, fx=fx, x2=x2, xsem=xsem):
                    if fx is not None:
                        W.wait(g, "gpsimd", fx[0], fx[1])
                    g.dma_start(out=x2, in_=d_src[tt * 128:(tt + 1) * 128, :]).then_inc(xsem.h, 16)
                block.gpsimd(sec_g)
                xsem.n += 16
                x_loaded = (xsem, xsem.n)

            ph = tt % 2
            fp = rms_statp_free[ph]
            def sec_sq4(sc, x_loaded=x_loaded, ph=ph, fp=fp, x2=x2):
                W.wait(sc, "scalar", x_loaded[0], x_loaded[1])
                if fp is not None:
                    W.wait(sc, "scalar", fp[0], fp[1])
                for c in range(4):
                    sc.activation(
                        ytm[0][:, ph, c * 512:(c + 1) * 512], x2[:, c * 512:(c + 1) * 512], AF.Square,
                        accum_out=statp[:, 4 * ph + c:4 * ph + c + 1],
                    ).then_inc(s_ea.h, 1)
            block.scalar(sec_sq4)
            s_ea.n += 4
            sq_done = s_ea.n

            base = s_ev.n
            def sec_vec(v, sq_done=sq_done, ph=ph, base=base):
                W.wait(v, "vector", s_ea, sq_done)
                v.tensor_add(stat[:, 0:1], statp[:, 4 * ph:4 * ph + 1], statp[:, 4 * ph + 1:4 * ph + 2]).then_inc(s_ev.h, 1)
                W.wait(v, "vector", s_ev, base + 1)
                v.tensor_add(stat[:, 0:1], stat[:, 0:1], statp[:, 4 * ph + 2:4 * ph + 3]).then_inc(s_ev.h, 1)
                W.wait(v, "vector", s_ev, base + 2)
                v.tensor_add(stat[:, 0:1], stat[:, 0:1], statp[:, 4 * ph + 3:4 * ph + 4]).then_inc(s_ev.h, 1)
                W.wait(v, "vector", s_ev, base + 3)
                v.tensor_scalar(stat[:, 1:2], stat[:, 0:1], 1.0 / D, EPS, ALU.mult, ALU.add).then_inc(s_ev.h, 1)
                W.wait(v, "vector", s_ev, base + 4)
                v.reciprocal(stat[:, 2:3], stat[:, 1:2]).then_inc(s_ev.h, 1)
            block.vector(sec_vec)
            s_ev.n += 5
            stats_done = s_ev.n
            rms_statp_free[ph] = (s_ev, stats_done)

            def sec_sc(sc, stats_done=stats_done):
                W.wait(sc, "scalar", s_ev, stats_done)
                sc.activation(stat[:, 3:4], stat[:, 2:3], AF.Sqrt).then_inc(s_ea.h, 1)
            block.scalar(sec_sc)
            s_ea.n += 1
            sqrt_done = s_ea.n

            xb = tt % 2
            xn_dst = xntm[:] if xb == 0 else ytm[1][:, 3, :]
            fb2 = rms_xnbuf_free[xb]
            def sec_vec2(v, sqrt_done=sqrt_done, x2=x2, xn_dst=xn_dst, fb2=fb2):
                W.wait(v, "vector", s_ea, sqrt_done)
                W.wait(v, "vector", s_x, consts_done[0])
                if fb2 is not None:
                    W.wait(v, "vector", fb2[0], fb2[1])
                v.scalar_tensor_tensor(
                    xn_dst, x2, stat[:, 3:4], rmsb[:], ALU.mult, ALU.mult,
                ).then_inc(s_ev.h, 1)
            block.vector(sec_vec2)
            s_ev.n += 1
            xntm_done = s_ev.n
            free_x2[slot] = (s_ev, xntm_done)

            if rms_pend[0] is not None:
                _emit_rms_tr(*rms_pend[0])
            rms_pend[0] = (dst_fm, tt, xn_dst, xntm_done, drv)

    def mm_tile(lhs_view, rhs_view, extra_waits_t, n_out=512):
        q = psA_ctr[0] % 4
        psA_ctr[0] += 1
        fa = free_psA[q]
        nk = KD
        def sec_t(t, q=q, fa=fa, waits=list(extra_waits_t), n_out=n_out):
            for sem, val in waits:
                W.wait(t, "tensor", sem, val)
            if fa is not None:
                W.wait(t, "tensor", fa[0], fa[1])
            for k in range(nk):
                ins = t.matmul(psA[:, q, 0:n_out], lhs_view(k), rhs_view(k),
                               start=(k == 0), stop=(k == nk - 1))
                if k == nk - 1:
                    ins.then_inc(s_mm.h, 1)
        block.tensor(sec_t)
        s_mm.n += 1
        return q, s_mm.n

    wslot_ctr = [0]

    def load_wblock(src_ap, n_tiles):
        j = wslot_ctr[0] % NWB
        wslot_ctr[0] += 1
        fw = free_wb[j]
        assert fw != "pending", "previous tenant of wb slot has no consumption milestone"
        sem = s_wb[j]
        def sec_g(g, j=j, fw=fw, src_ap=src_ap, n_tiles=n_tiles, sem=sem):
            if fw is not None:
                W.wait(g, "gpsimd", fw[0], fw[1])
            g.dma_start(out=wb[j][:, 0:n_tiles, :], in_=src_ap).then_inc(sem.h, 16)
        block.gpsimd(sec_g)
        sem.n += 16
        free_wb[j] = "pending"
        return j, (sem, sem.n)

    def wb_consumed(slot, sem, val):
        free_wb[slot] = (sem, val)

    load_consts()
    consts_done[0] = s_x.n

    _p1drv = RmsDriver(d_xp, xn_fm, ntt=PRE // 128)
    _p1drv.via_sync = True
    p1_done = _p1drv.finish()

    scan_done_ks = {}

    def trio_ks(ks, do_pre, do_own, w0, w1):
        wjf, wvf = load_wblock(d_wf[ks], KD)
        wjzg, wvzg = load_wblock(d_wzg[ks], KD)
        wjz, wvz = load_wblock(d_wz[ks], KD)
        mvf_last, mvzg_last, mvz_last = None, None, None
        if do_pre:
            qf, mvf = mm_tile(
                lambda k, wjf=wjf: wb[wjf][:, k, :],
                lambda k: xn_fm[:, k, 0:PRE],
                [wvf, (s_ev, p1_done)], n_out=PRE)
            mvf_last = mvf
            def sec_f(sc, qf=qf, mvf=mvf, ks=ks):
                W.wait(sc, "scalar", s_mm, mvf)
                sc.activation(
                    fbuf[:, ks, 0:PRE], psA[:, qf, 0:PRE], AF.Sigmoid,
                    bias=fbv[:, ks:ks + 1],
                ).then_inc(s_ea.h, 1)
            block.scalar(sec_f)
            s_ea.n += 1
            free_psA[qf] = (s_ea, s_ea.n)
            f_evac_val = s_ea.n
            qzg, mvzg = mm_tile(
                lambda k, wjzg=wjzg: wb[wjzg][:, k, :],
                lambda k: xn_fm[:, k, 0:PRE], [wvzg], n_out=PRE)
            mvzg_last = mvzg
            fz = free_zgsig[0]
            def sec_zg(sc, qzg=qzg, mvzg=mvzg, fz=fz):
                W.wait(sc, "scalar", s_mm, mvzg)
                if fz is not None:
                    W.wait(sc, "scalar", fz[0], fz[1])
                sc.activation(zgsig[0][:, 0:PRE], psA[:, qzg, 0:PRE], AF.Sigmoid).then_inc(s_ea.h, 1)
            block.scalar(sec_zg)
            s_ea.n += 1
            free_psA[qzg] = (s_ea, s_ea.n)
            zgv = s_ea.n
            qz, mvz = mm_tile(
                lambda k, wjz=wjz: wb[wjz][:, k, :],
                lambda k: xn_fm[:, k, 0:PRE], [wvz], n_out=PRE)
            mvz_last = mvz
            def sec_z(v, qz=qz, mvz=mvz, ks=ks, zgv=zgv):
                W.wait(v, "vector", s_mm, mvz)
                W.wait(v, "vector", s_ea, zgv)
                v.tensor_mul(zbuf[:, ks, 0:PRE], psA[:, qz, 0:PRE], zgsig[0][:, 0:PRE]).then_inc(s_ev.h, 1)
            block.vector(sec_z)
            s_ev.n += 1
            free_psA[qz] = (s_ev, s_ev.n)
            free_zgsig[0] = (s_ev, s_ev.n)
            zv = s_ev.n
            sv = s_ev.n + 1
            def sec_scan(v, ks=ks, f_evac_val=f_evac_val, sv=sv, zv=zv):
                W.wait(v, "vector", s_ea, f_evac_val)
                W.wait(v, "vector", s_ev, zv)
                v.tensor_tensor_scan(
                    hrot[0][:, 0:PRE], fbuf[:, ks, 0:PRE], zbuf[:, ks, 0:PRE],
                    0.0, ALU.mult, ALU.add,
                ).then_inc(s_ev.h, 1)
                W.wait(v, "vector", s_ev, sv)
                v.tensor_copy(carry[:, ks:ks + 1], hrot[0][:, PRE - 1:PRE]).then_inc(s_ev.h, 1)
            block.vector(sec_scan)
            s_ev.n += 2
            scan_done_ks[ks] = s_ev.n
        if do_own:
            sdk = scan_done_ks[ks]
            for tc, sl in enumerate([slice(0, 512), slice(512, 1024)]):
                qf, mvf = mm_tile(
                    lambda k, wjf=wjf: wb[wjf][:, k, :],
                    lambda k, sl=sl: xlnfm[:, k, sl],
                    [wvf, (s_ev, w0 if tc == 0 else w1)])
                mvf_last = mvf
                def sec_f2(sc, qf=qf, mvf=mvf, ks=ks, sl=sl, sdk=sdk):
                    W.wait(sc, "scalar", s_mm, mvf)
                    W.wait(sc, "scalar", s_ev, sdk)
                    sc.activation(
                        fbuf[:, ks, sl], psA[:, qf, :], AF.Sigmoid,
                        bias=fbv[:, ks:ks + 1],
                    ).then_inc(s_ea.h, 1)
                block.scalar(sec_f2)
                s_ea.n += 1
                free_psA[qf] = (s_ea, s_ea.n)
            for tc, sl in enumerate([slice(0, 512), slice(512, 1024)]):
                jz = 1 - tc
                wv_a1 = (s_ev, w0 if tc == 0 else w1)
                qzg, mvzg = mm_tile(
                    lambda k, wjzg=wjzg: wb[wjzg][:, k, :],
                    lambda k, sl=sl: xlnfm[:, k, sl], [wvzg, wv_a1])
                mvzg_last = mvzg
                fz = free_zgsig[jz]
                def sec_zg2(sc, qzg=qzg, mvzg=mvzg, jz=jz, fz=fz):
                    W.wait(sc, "scalar", s_mm, mvzg)
                    if fz is not None:
                        W.wait(sc, "scalar", fz[0], fz[1])
                    sc.activation(zgsig[jz][:], psA[:, qzg, :], AF.Sigmoid).then_inc(s_ea.h, 1)
                block.scalar(sec_zg2)
                s_ea.n += 1
                free_psA[qzg] = (s_ea, s_ea.n)
                zgv2 = s_ea.n
                qz, mvz = mm_tile(
                    lambda k, wjz=wjz: wb[wjz][:, k, :],
                    lambda k, sl=sl: xlnfm[:, k, sl], [wvz, wv_a1])
                mvz_last = mvz
                def sec_z2(v, qz=qz, mvz=mvz, jz=jz, ks=ks, sl=sl, zgv2=zgv2):
                    W.wait(v, "vector", s_mm, mvz)
                    W.wait(v, "vector", s_ea, zgv2)
                    v.tensor_mul(zbuf[:, ks, sl], psA[:, qz, :], zgsig[jz][:]).then_inc(s_ev.h, 1)
                block.vector(sec_z2)
                s_ev.n += 1
                free_psA[qz] = (s_ev, s_ev.n)
                free_zgsig[jz] = (s_ev, s_ev.n)
        wb_consumed(wjf, s_mm, mvf_last)
        wb_consumed(wjzg, s_mm, mvzg_last)
        wb_consumed(wjz, s_mm, mvz_last)

    a1_driver = RmsDriver(d_x, xlnfm, ntt=NTT)
    for ks in range(4):
        trio_ks(ks, True, False, 0, 0)
        a1_driver.step()
        a1_driver.step()
    a1_final = a1_driver.finish()
    a1_t3 = a1_driver.tile_done[3]
    a1_t7 = a1_driver.tile_done[7]
    for ks in range(4, KS):
        trio_ks(ks, True, True, a1_t3, a1_t7)

    if DEBUG_TAPS:
        pv = s_ev.n
        pa = s_ea.n
        def sec_tapp(sy, pv=pv, pa=pa):
            W.wait(sy, "sync", s_ev, pv)
            W.wait(sy, "sync", s_ea, pa)
            sy.dma_start(out=taps["t_pf"], in_=fbuf[:]).then_inc(s_st.h, 16)
            sy.dma_start(out=taps["t_pz"], in_=zbuf[:]).then_inc(s_st.h, 16)
            sy.dma_start(out=taps["t_pc"], in_=carry[:]).then_inc(s_st.h, 16)
        block.sync(sec_tapp)
        s_st.n += 48
        prefix_tap_done = s_st.n
        def sec_guard(v, g=prefix_tap_done):
            W.wait(v, "vector", s_st, g)
        block.vector(sec_guard)
        def sec_guard2(sc, g=prefix_tap_done):
            W.wait(sc, "scalar", s_st, g)
        block.scalar(sec_guard2)

    carry_w = s_ev.n
    def sec_mask(v, cw=carry_w):
        W.wait(v, "vector", s_ev, cw)
        v.tensor_scalar(carry[:], carry[:], maskv[:, 0:1], None, ALU.mult).then_inc(s_ev.h, 1)
    block.vector(sec_mask)
    s_ev.n += 1
    mask_done = s_ev.n

    if DEBUG_TAPS:
        cv = s_ev.n
        def sec_tapc(sy, cv=cv):
            W.wait(sy, "sync", s_ev, cv)
            sy.dma_start(out=taps["t_carry"], in_=carry[:]).then_inc(s_st.h, 16)
        block.sync(sec_tapc)
        s_st.n += 16

    for ks in range(4):
        trio_ks(ks, False, True, a1_final, a1_final)
    if rms_xnbuf_free[0] is not None:
        free_xntm = rms_xnbuf_free[0]
    rms_sq_watermark = s_ea.n

    if DEBUG_TAPS:
        av = a1_final
        def sec_tapxn(sy, av=av):
            W.wait(sy, "sync", s_ev, av)
            sy.dma_start(out=taps["t_xn"], in_=xn_fm[:]).then_inc(s_st.h, 16)
        block.sync(sec_tapxn)
        s_st.n += 16

    a2a_f_done = s_ea.n
    a2a_z_done = s_ev.n

    if DEBUG_TAPS:
        def sec_tapfz(sy, fa=a2a_f_done, za=a2a_z_done):
            W.wait(sy, "sync", s_ea, fa)
            W.wait(sy, "sync", s_ev, za)
            sy.dma_start(out=taps["t_f"], in_=fbuf[:]).then_inc(s_st.h, 16)
            sy.dma_start(out=taps["t_z"], in_=zbuf[:]).then_inc(s_st.h, 16)
        block.sync(sec_tapfz)
        s_st.n += 32


    og_guard = [0]
    mask_done_v = s_ev.n
    for ks in range(KS):
        wja, wva = load_wblock(d_wga[ks], KD)
        wjb, wvb = load_wblock(d_wgb[ks], KD)
        jh = ks % 2
        fh = free_hrot[jh]
        def sec_scan(v, ks=ks, jh=jh, fh=fh, zv=a2a_z_done):
            W.wait(v, "vector", s_ea, a2a_f_done)
            W.wait(v, "vector", s_ev, zv)
            if fh is not None:
                W.wait(v, "vector", fh[0], fh[1])
            v.tensor_tensor_scan(
                hrot[jh][:], fbuf[:, ks, :], zbuf[:, ks, :], carry[:, ks:ks + 1],
                ALU.mult, ALU.add,
            ).then_inc(s_ev.h, 1)
        block.vector(sec_scan)
        s_ev.n += 1
        scan_val = s_ev.n
        for tc in range(2):
            sl = slice(tc * 512, (tc + 1) * 512)
            ja = (ks * 2 + tc) % 3
            qa, mva = mm_tile(
                lambda k, wja=wja: wb[wja][:, k, :],
                lambda k, sl=sl: xlnfm[:, k, sl],
                [wva],
            )
            fg = free_gasil[ja]
            def sec_ga(sc, qa=qa, mva=mva, ja=ja, fg=fg):
                W.wait(sc, "scalar", s_mm, mva)
                if fg is not None:
                    W.wait(sc, "scalar", fg[0], fg[1])
                sc.activation(gasil[ja][:], psA[:, qa, :], AF.Silu).then_inc(s_ea.h, 1)
            block.scalar(sec_ga)
            s_ea.n += 1
            free_psA[qa] = (s_ea, s_ea.n)
            ga_done = s_ea.n
            qb, mvb = mm_tile(
                lambda k, wjb=wjb: wb[wjb][:, k, :],
                lambda k, sl=sl: xlnfm[:, k, sl],
                [wvb],
            )
            ogv = s_ev.n + 1
            def sec_gb(v, qb=qb, mvb=mvb, ja=ja, jh=jh, ks=ks, sl=sl, ga_done=ga_done,
                       scv=scan_val, ogv=ogv, ogg=og_guard[0]):
                W.wait(v, "vector", s_mm, mvb)
                W.wait(v, "vector", s_ea, ga_done)
                W.wait(v, "vector", s_ev, max(scv, ogg))
                v.tensor_mul(zgsig[0][:], psA[:, qb, :], gasil[ja][:]).then_inc(s_ev.h, 1)
                W.wait(v, "vector", s_ev, ogv)
                v.tensor_mul(fbuf[:, ks, sl], zgsig[0][:], hrot[jh][:, sl]).then_inc(s_ev.h, 1)
            block.vector(sec_gb)
            s_ev.n += 2
            og_guard[0] = s_ev.n
            free_psA[qb] = (s_ev, s_ev.n)
            free_gasil[ja] = (s_ev, s_ev.n)
            if tc == 1:
                wb_consumed(wja, s_mm, mva)
                wb_consumed(wjb, s_mm, mvb)
        free_hrot[jh] = (s_ev, s_ev.n)
    gated_done = s_ev.n

    preload_done = [0]
    for tc in range(2):
        for j in range(4):
            tt = tc * 4 + j
            dstt = ytm[tc][:, j, :]
            ch = s_pl[j]
            waits = []
            if ch.n > 0:
                waits.append((ch, ch.n))
            if tc == 0:
                waits.append((s_ea, rms_sq_watermark))
            if tc == 1:
                if j < 3 and free_x2[j] is not None:
                    waits.append(free_x2[j])
                if j == 3 and rms_xnbuf_free[1] is not None:
                    waits.append(rms_xnbuf_free[1])
            def sec_gx(g, tt=tt, dstt=dstt, waits=list(waits), ch=ch):
                for sem, val in waits:
                    W.wait(g, "gpsimd", sem, val)
                g.dma_start(out=dstt, in_=d_x[tt * 128:(tt + 1) * 128, :]).then_inc(ch.h, 16)
            block.gpsimd(sec_gx)
            ch.n += 16
            xv = ch.n
            def sec_x2(v, dstt=dstt, xv=xv, ch=ch):
                W.wait(v, "vector", ch, xv)
                v.tensor_scalar(dstt, dstt, 2.0, None, ALU.mult).then_inc(s_ev.h, 1)
            block.vector(sec_x2)
            s_ev.n += 1
    preload_done[0] = s_ev.n

    if DEBUG_TAPS:
        def sec_tapg(sy, gv=gated_done):
            W.wait(sy, "sync", s_ev, gv)
            sy.dma_start(out=taps["t_gated"], in_=fbuf[:]).then_inc(s_st.h, 16)
        block.sync(sec_tapg)
        s_st.n += 16

    ytm_guard = [0, 0]
    fz_guard = [s_st.n]
    xlnfm_done = {}
    ln1_pe_mm = {}
    statp_guard = [0]
    stat_guard = [0]
    act_out_guard = [0]
    ln1_norm = {}
    y_ready = {}
    hh_last = {}
    hviews_all = {}

    def outproj_both():
        def emit_tr(dc, tc, je, yev):
            yb = tc % 2
            jt = (dc * 2 + tc) % 2
            ft = free_psT[jt]
            def sec_tr(t, je=je, jt=jt, ft=ft, yev=yev):
                W.wait(t, "tensor", s_ea, yev)
                if ft is not None:
                    W.wait(t, "tensor", ft[0], ft[1])
                for j in range(4):
                    ins = t.transpose(psT[jt][:, j, :], evst[je][:, j * 128:(j + 1) * 128], idf[:])
                    if j == 3:
                        ins.then_inc(s_mm.h, 1)
            block.tensor(sec_tr)
            s_mm.n += 1
            trv = s_mm.n
            free_evst[je] = (s_mm, trv)
            def sec_ev(v, jt=jt, dc=dc, trv=trv, yg=ytm_guard[yb], yb=yb, tc=tc,
                       pv=preload_done[0]):
                W.wait(v, "vector", s_mm, trv)
                W.wait(v, "vector", s_st, yg)
                W.wait(v, "vector", s_ev, pv)
                for j in range(4):
                    v.scalar_tensor_tensor(
                        ytm[yb][:, j, dc * 128:(dc + 1) * 128], psT[jt][:, j, :], 1.0,
                        ytm[yb][:, j, dc * 128:(dc + 1) * 128], ALU.mult, ALU.add,
                        accum_out=statq[:, dc * 8 + tc * 4 + j:dc * 8 + tc * 4 + j + 1],
                    ).then_inc(s_ev.h, 1)
            block.vector(sec_ev)
            s_ev.n += 4
            free_psT[jt] = (s_ev, s_ev.n)

            def sec_sq(v, dc=dc, yb=yb, tc=tc):
                for j in range(4):
                    v.scalar_tensor_tensor(
                        xn_fm[:, 0, 0:128], ytm[yb][:, j, dc * 128:(dc + 1) * 128], 1.0,
                        ytm[yb][:, j, dc * 128:(dc + 1) * 128], ALU.mult, ALU.mult,
                        accum_out=statq2[:, dc * 8 + tc * 4 + j:dc * 8 + tc * 4 + j + 1],
                    ).then_inc(s_ev.h, 1)
            block.vector(sec_sq)
            s_ev.n += 4

        pending_tr = None
        for dc in range(KD):
            wjo, wvo = load_wblock(d_wo[dc], KS)
            mv_last = None
            for tc in range(2):
                sl = slice(tc * 512, (tc + 1) * 512)
                qy, mvy = mm_tile(
                    lambda k, wjo=wjo: wb[wjo][:, k, :],
                    lambda k, sl=sl: fbuf[:, k, sl],
                    [wvo, (s_ev, gated_done)],
                )
                mv_last = mvy
                je = (dc * 2 + tc) % 2
                fe = free_evst[je]
                def sec_y(sc, qy=qy, mvy=mvy, dc=dc, je=je, fe=fe):
                    W.wait(sc, "scalar", s_mm, mvy)
                    if fe is not None:
                        W.wait(sc, "scalar", fe[0], fe[1])
                    sc.activation(
                        evst[je][:], psA[:, qy, :], AF.Identity, bias=obv[:, dc:dc + 1],
                    ).then_inc(s_ea.h, 1)
                block.scalar(sec_y)
                s_ea.n += 1
                free_psA[qy] = (s_ea, s_ea.n)
                yev = s_ea.n
                if pending_tr is not None:
                    emit_tr(*pending_tr)
                pending_tr = (dc, tc, je, yev)
            wb_consumed(wjo, s_mm, mv_last)
        emit_tr(*pending_tr)
        y_ready[0] = s_ev.n
        y_ready[1] = s_ev.n

    def ln_stats_chain(stats_done):
        base = s_ev.n
        def sec_var(v, sd=stats_done, base=base, aog=act_out_guard[0]):
            W.wait(v, "vector", s_ev, sd)
            W.wait(v, "vector", s_ea, aog)
            v.tensor_scalar(stat2[:, 0:4], stat[:, 0:4], 1.0 / D, None, ALU.mult).then_inc(s_ev.h, 1)
            v.tensor_scalar(stat2[:, 4:8], stat[:, 4:8], 1.0 / D, None, ALU.mult).then_inc(s_ev.h, 1)
            W.wait(v, "vector", s_ev, base + 2)
            v.tensor_mul(stat2[:, 8:12], stat2[:, 0:4], stat2[:, 0:4]).then_inc(s_ev.h, 1)
            W.wait(v, "vector", s_ev, base + 3)
            v.tensor_sub(stat2[:, 4:8], stat2[:, 4:8], stat2[:, 8:12]).then_inc(s_ev.h, 1)
            W.wait(v, "vector", s_ev, base + 4)
            v.tensor_scalar(stat2[:, 4:8], stat2[:, 4:8], 1.0, EPS, ALU.mult, ALU.add).then_inc(s_ev.h, 1)
            W.wait(v, "vector", s_ev, base + 5)
            v.reciprocal(stat[:, 8:12], stat2[:, 4:8]).then_inc(s_ev.h, 1)
        block.vector(sec_var)
        s_ev.n += 6
        var_done = s_ev.n
        def sec_sq(sc, var_done=var_done):
            W.wait(sc, "scalar", s_ev, var_done)
            sc.activation(stat2[:, 8:12], stat[:, 8:12], AF.Sqrt).then_inc(s_ea.h, 1)
        block.scalar(sec_sq)
        s_ea.n += 1
        sq_done = s_ea.n
        def sec_nr(v, sq_done=sq_done):
            W.wait(v, "vector", s_ea, sq_done)
            v.tensor_scalar(stat2[:, 12:16], stat2[:, 8:12], 1.0, None, ALU.mult).then_inc(s_ev.h, 1)
        block.vector(sec_nr)
        s_ev.n += 1
        return s_ev.n

    def ln1_dve(tc):
        yb = tc % 2
        y_done = y_ready[tc]
        def sec_s1(v, yb=yb, tc=tc, y_done=y_done, stg=stat_guard[0]):
            W.wait(v, "vector", s_ev, y_done)
            W.wait(v, "vector", s_ev, stg)
            for j in range(4):
                v.reduce_sum(stat[:, j:j + 1], statq[:, tc * 4 + j:128:8], axis=AX.X).then_inc(s_ev.h, 1)
                v.reduce_sum(stat[:, 4 + j:5 + j], statq2[:, tc * 4 + j:128:8], axis=AX.X).then_inc(s_ev.h, 1)
        block.vector(sec_s1)
        s_ev.n += 8
        r_done = ln_stats_chain(s_ev.n)
        statp_guard[0] = s_ea.n
        stat_guard[0] = r_done
        for j in range(4):
            tt = tc * 4 + j
            def sec_norm(v, j=j, rv=r_done, yb=yb):
                W.wait(v, "vector", s_ev, rv)
                v.tensor_scalar(
                    ytm[yb][:, j, :], ytm[yb][:, j, :], stat2[:, j:j + 1], stat2[:, 12 + j:13 + j],
                    ALU.subtract, ALU.mult,
                ).then_inc(s_ev.h, 1)
            block.vector(sec_norm)
            s_ev.n += 1
            xlnv = s_ev.n
            ln1_norm[(tc, j)] = xlnv

    def ln1_pe(tc):
        yb = tc % 2
        for j in range(4):
            tt = tc * 4 + j
            xlnv = ln1_norm[(tc, j)]
            for hb in range(2):
                fb_ = free_psB[hb]
                def sec_t(t, hb=hb, fb_=fb_, xlnv=xlnv, j=j, yb=yb):
                    W.wait(t, "tensor", s_ev, xlnv)
                    if fb_ is not None:
                        W.wait(t, "tensor", fb_[0], fb_[1])
                    for jj in range(8):
                        dc = hb * 8 + jj
                        ins = t.transpose(psB[:, hb, jj, :], ytm[yb][:, j, dc * 128:(dc + 1) * 128], idb[:])
                        if jj == 7:
                            ins.then_inc(s_mm.h, 1)
                block.tensor(sec_t)
                s_mm.n += 1
                trv = s_mm.n
                def sec_ev2(v, hb=hb, tt=tt, trv=trv):
                    W.wait(v, "vector", s_mm, trv)
                    v.tensor_copy(
                        xn_fm[:, hb * 8:(hb + 1) * 8, tt * 128:(tt + 1) * 128],
                        psB[:, hb, :, :],
                    ).then_inc(s_ev.h, 1)
                block.vector(sec_ev2)
                s_ev.n += 1
                free_psB[hb] = (s_ev, s_ev.n)
        xlnfm_done[tc] = s_ev.n
        ln1_pe_mm[tc] = s_mm.n

    gu_loaded = [0]
    gu_q = []

    def preload_gu(n):
        for _ in range(n):
            if gu_loaded[0] >= 2 * KI:
                return
            ic = gu_loaded[0] % KI
            wjg, wvg = load_wblock(d_gu[ic, 0], 16)
            wju, wvu = load_wblock(d_gu[ic, 1], 16)
            def sec_m128(v, wjg=wjg, wju=wju, wvg=wvg, wvu=wvu):
                W.wait(v, "vector", wvg[0], wvg[1])
                W.wait(v, "vector", wvu[0], wvu[1])
                v.tensor_scalar(wb[wjg][:, 0:16, :], wb[wjg][:, 0:16, :], -128.0, None, ALU.add).then_inc(s_ev.h, 1)
                v.tensor_scalar(wb[wju][:, 0:16, :], wb[wju][:, 0:16, :], -128.0, None, ALU.add).then_inc(s_ev.h, 1)
            block.vector(sec_m128)
            s_ev.n += 2
            gu_q.append((wjg, wju, s_ev.n))
            gu_loaded[0] += 1

    def ffn_gu(tc):
        xln_ready = xlnfm_done[tc]
        sl = slice(tc * 512, (tc + 1) * 512)
        hviews = []
        for ic in range(KI):
            if ic < 32:
                buf, idx = fbuf, ic
            else:
                buf, idx = zbuf, ic - 32
            hviews.append(buf[:].rearrange("p a (b c) -> p (a b) c", b=2)[:, idx, :])
        hviews_all[tc] = hviews
        hh_done = 0
        for ic in range(KI):
            want = tc * KI + ic + 2
            while gu_loaded[0] < min(want, (tc + 1) * KI):
                preload_gu(1)
            wjg, wju, wready = gu_q.pop(0)
            qg, mvg = mm_tile(
                lambda k, wjg=wjg: wb[wjg][:, k, :],
                lambda k, sl=sl: xn_fm[:, k, sl],
                [(s_ev, wready), (s_ev, xln_ready)],
            )
            jg = ic % 3
            fg = free_gasil[jg]
            def sec_g(sc, qg=qg, mvg=mvg, ic=ic, jg=jg, fg=fg):
                W.wait(sc, "scalar", s_mm, mvg)
                if fg is not None:
                    W.wait(sc, "scalar", fg[0], fg[1])
                sc.activation(
                    gasil[jg][:], psA[:, qg, :], AF.Silu, scale=gscv[:, ic:ic + 1],
                ).then_inc(s_ea.h, 1)
            block.scalar(sec_g)
            s_ea.n += 1
            free_psA[qg] = (s_ea, s_ea.n)
            g_done = s_ea.n
            qu, mvu = mm_tile(
                lambda k, wju=wju: wb[wju][:, k, :],
                lambda k, sl=sl: xn_fm[:, k, sl],
                [],
            )
            def sec_u(v, qu=qu, mvu=mvu, ic=ic, jg=jg, g_done=g_done,
                      hv=hviews[ic], fzg=fz_guard[0]):
                W.wait(v, "vector", s_mm, mvu)
                W.wait(v, "vector", s_ea, g_done)
                W.wait(v, "vector", s_st, fzg)
                v.scalar_tensor_tensor(
                    hv, psA[:, qu, :], uscv[:, ic:ic + 1], gasil[jg][:],
                    ALU.mult, ALU.mult,
                ).then_inc(s_ev.h, 1)
            block.vector(sec_u)
            s_ev.n += 1
            free_psA[qu] = (s_ev, s_ev.n)
            free_gasil[jg] = (s_ev, s_ev.n)
            wb_consumed(wjg, s_mm, mvg)
            wb_consumed(wju, s_mm, mvu)
            hh_done = s_ev.n
        hh_last[tc] = hh_done

    def ffn_down(tc, pump=None):
        yb = tc % 2
        hviews = hviews_all[tc]
        hhv_all = hh_last[tc]

        def emit_dtr(hc, je, dev):
            jt = hc % 2
            ft = free_psT[jt]
            def sec_tr(t, je=je, jt=jt, ft=ft, dev=dev):
                W.wait(t, "tensor", s_ea, dev)
                if ft is not None:
                    W.wait(t, "tensor", ft[0], ft[1])
                for j in range(4):
                    ins = t.transpose(psT[jt][:, j, :], evst[je][:, j * 128:(j + 1) * 128], idf[:])
                    if j == 3:
                        ins.then_inc(s_mm.h, 1)
            block.tensor(sec_tr)
            s_mm.n += 1
            trv = s_mm.n
            free_evst[je] = (s_mm, trv)
            def sec_ev(v, jt=jt, hc=hc, trv=trv, yg=ytm_guard[yb], yb=yb, tc=tc):
                W.wait(v, "vector", s_mm, trv)
                W.wait(v, "vector", s_st, yg)
                for j in range(4):
                    v.scalar_tensor_tensor(
                        ytm[yb][:, j, hc * 128:(hc + 1) * 128], psT[jt][:, j, :], 1.0,
                        ytm[yb][:, j, hc * 128:(hc + 1) * 128], ALU.mult, ALU.add,
                        accum_out=statq[:, hc * 8 + tc * 4 + j:hc * 8 + tc * 4 + j + 1],
                    ).then_inc(s_ev.h, 1)
            block.vector(sec_ev)
            s_ev.n += 4
            free_psT[jt] = (s_ev, s_ev.n)

            def sec_sq(v, hc=hc, yb=yb, tc=tc):
                for j in range(4):
                    v.scalar_tensor_tensor(
                        xn_fm[:, 0, 0:128], ytm[yb][:, j, hc * 128:(hc + 1) * 128], 1.0,
                        ytm[yb][:, j, hc * 128:(hc + 1) * 128], ALU.mult, ALU.mult,
                        accum_out=statq2[:, hc * 8 + tc * 4 + j:hc * 8 + tc * 4 + j + 1],
                    ).then_inc(s_ev.h, 1)
            block.vector(sec_sq)
            s_ev.n += 4

        pending_dtr = None
        dn_q = []

        def load_dn(hc, g4):
            wj, wv = load_wblock(d_dn[hc, g4], 16)
            def sec_m128b(v, wj=wj, wv=wv):
                W.wait(v, "vector", wv[0], wv[1])
                v.tensor_scalar(wb[wj][:, 0:16, :], wb[wj][:, 0:16, :], -128.0, None, ALU.add).then_inc(s_ev.h, 1)
            block.vector(sec_m128b)
            s_ev.n += 1
            dn_q.append((wj, s_ev.n))

        dn_loaded = [0]

        def preload_dn(n):
            while n > 0 and dn_loaded[0] < 4 * KD:
                load_dn(dn_loaded[0] // 4, dn_loaded[0] % 4)
                dn_loaded[0] += 1
                n -= 1

        preload_dn(3)
        for hc in range(KD):
            q = psA_ctr[0] % 4
            psA_ctr[0] += 1
            fa = free_psA[q]
            for g4 in range(4):
                preload_dn(1)
                wj, wready = dn_q.pop(0)
                def sec_dn(t, q=q, fa=fa, g4=g4, wj=wj, wready=wready, hhv=hhv_all):
                    W.wait(t, "tensor", s_ev, wready)
                    W.wait(t, "tensor", s_ev, hhv)
                    if g4 == 0 and fa is not None:
                        W.wait(t, "tensor", fa[0], fa[1])
                    for k in range(16):
                        ins = t.matmul(psA[:, q, :], wb[wj][:, k, :], hviews[g4 * 16 + k],
                                       start=(g4 == 0 and k == 0), stop=(g4 == 3 and k == 15))
                        if k == 15:
                            ins.then_inc(s_mm.h, 1)
                block.tensor(sec_dn)
                s_mm.n += 1
                wb_consumed(wj, s_mm, s_mm.n)
            mvd = s_mm.n
            je = hc % 2
            fe = free_evst[je]
            def sec_de(sc, q=q, mvd=mvd, hc=hc, je=je, fe=fe):
                W.wait(sc, "scalar", s_mm, mvd)
                if fe is not None:
                    W.wait(sc, "scalar", fe[0], fe[1])
                sc.activation(
                    evst[je][:], psA[:, q, :], AF.Identity, scale=dscv[:, hc:hc + 1],
                ).then_inc(s_ea.h, 1)
            block.scalar(sec_de)
            s_ea.n += 1
            free_psA[q] = (s_ea, s_ea.n)
            dev = s_ea.n
            if pending_dtr is not None:
                emit_dtr(*pending_dtr)
            pending_dtr = (hc, je, dev)
            if pump is not None:
                pump()
        emit_dtr(*pending_dtr)
        return s_ev.n

    def ln2_steps(tc, ffn_done):
        steps = []
        hold = {}
        yb = tc % 2

        def tile_step(j):
            def go():
                def sec_s1(v, j=j, tc=tc, fd=ffn_done, stg=stat_guard[0]):
                    W.wait(v, "vector", s_ev, fd)
                    W.wait(v, "vector", s_ev, stg)
                    v.reduce_sum(stat[:, j:j + 1], statq[:, tc * 4 + j:128:8], axis=AX.X).then_inc(s_ev.h, 1)
                    v.reduce_sum(stat[:, 4 + j:5 + j], statq2[:, tc * 4 + j:128:8], axis=AX.X).then_inc(s_ev.h, 1)
                block.vector(sec_s1)
                s_ev.n += 2
            return go

        def stats_step():
            hold["r_done"] = ln_stats_chain(s_ev.n)
            def sec_mr(v, rv=hold["r_done"]):
                W.wait(v, "vector", s_ev, rv)
                v.scalar_tensor_tensor(
                    stat[:, 12:16], stat2[:, 0:4], -1.0, stat2[:, 12:16],
                    ALU.mult, ALU.mult,
                ).then_inc(s_ev.h, 1)
            block.vector(sec_mr)
            s_ev.n += 1
            hold["mr_done"] = s_ev.n
            statp_guard[0] = s_ea.n
            stat_guard[0] = hold["r_done"]

        def out_step(j):
            def go():
                nonlocal_state_out(tc, j, yb, hold["r_done"], hold["mr_done"])
            return go

        for j in range(4):
            steps.append(tile_step(j))
        steps.append(stats_step)
        for j in range(4):
            steps.append(out_step(j))
        return steps

    ln2_last_store = [0]

    qst = {}

    def nonlocal_state_out(tc, j, yb, r_done, mr_done):
        nonlocal free_xstage
        tt = tc * 4 + j
        for c in range(4):
            if tc == 1:
                idx = (j * 2 + c // 2) % 3
                pool = [("x", 0), ("x", 1), ("e", 0)] if c % 2 == 0 else \
                       [("x", 2), ("x", 3), ("e", 1)]
                key = pool[idx]
                reg = (xstage[:, key[1] * 512:(key[1] + 1) * 512]
                       if key[0] == "x" else evst[key[1]][:])
            else:
                reg, key = xstage[:, c * 512:(c + 1) * 512], ("x", c)
            qs = qst.get(key)
            src = ytm[yb][:, j, c * 512:(c + 1) * 512]
            dst = d_out[tt * 128:(tt + 1) * 128, c * 512:(c + 1) * 512]
            if c % 2 == 0:
                def sec_oa(sc, j=j, rv=mr_done, qs=qs, reg=reg, src=src):
                    W.wait(sc, "scalar", s_ev, rv)
                    if qs is not None:
                        W.wait(sc, "scalar", qs[0], qs[1])
                    sc.activation(
                        reg, src, AF.Identity,
                        scale=stat2[:, 12 + j:13 + j], bias=stat[:, 12 + j:13 + j],
                    ).then_inc(s_ea.h, 1)
                block.scalar(sec_oa)
                s_ea.n += 1
                pv = (s_ea, s_ea.n)
            else:
                def sec_ov(v, j=j, rv=r_done, qs=qs, reg=reg, src=src):
                    W.wait(v, "vector", s_ev, rv)
                    if qs is not None:
                        W.wait(v, "vector", qs[0], qs[1])
                    v.tensor_scalar(
                        reg, src, stat2[:, j:j + 1], stat2[:, 12 + j:13 + j],
                        ALU.subtract, ALU.mult,
                    ).then_inc(s_ev.h, 1)
                block.vector(sec_ov)
                s_ev.n += 1
                pv = (s_ev, s_ev.n)
            osem = s_oq[c]
            if c % 2 == 0:
                def sec_sta(sy, pv=pv, dst=dst, reg=reg, osem=osem):
                    W.wait(sy, "sync", pv[0], pv[1])
                    sy.dma_start(out=dst, in_=reg).then_inc(osem.h, 16)
                block.sync(sec_sta)
            else:
                def sec_stg(g, pv=pv, dst=dst, reg=reg, osem=osem):
                    W.wait(g, "gpsimd", pv[0], pv[1])
                    g.dma_start(out=dst, in_=reg).then_inc(osem.h, 16)
                block.gpsimd(sec_stg)
            osem.n += 16
            qst[key] = (osem, osem.n)
        act_out_guard[0] = s_ea.n
        ytm_guard[yb] = s_st.n

    outproj_both()
    ln1_dve(0)
    ln1_pe(0)
    ln1_dve(1)
    ffn_gu(0)
    ln1_pe(1)
    f0 = ffn_down(0)
    ffn_gu(1)
    steps0 = ln2_steps(0, f0)
    sit = iter(steps0)
    def pump():
        s = next(sit, None)
        if s is not None:
            s()
    f1 = ffn_down(1, pump=pump)
    for s in sit:
        s()
    for s in ln2_steps(1, f1):
        s()

    def sec_fin(sy):
        sy.wait_ge(s_st.h, s_st.n)
        for c in range(4):
            sy.wait_ge(s_oq[c].h, s_oq[c].n)
    block.sync(sec_fin)
    block.flush()

    es.close()
    return nc


_nc_cache = None


def _get_program():
    global _nc_cache
    if _nc_cache is None:
        _nc_cache = build_program()
    return _nc_cache


def _pack_proj(Wnp):
    WT = np.ascontiguousarray(Wnp.T)
    t = WT.reshape(KD, 128, KS, 128)
    return np.ascontiguousarray(t.transpose(2, 1, 0, 3))


def _pack_wout(Wnp):
    WoT = np.ascontiguousarray(Wnp.T)
    t = WoT.reshape(KS, 128, KD, 128)
    return np.ascontiguousarray(t.transpose(2, 1, 0, 3))


def _int32_to_u8(a):
    a = np.ascontiguousarray(np.asarray(a, dtype=np.int32))
    assert a.dtype.byteorder in ("=", "<")
    return a.view(np.uint8).reshape(*a.shape, 4)[..., 0]


def _pack_gu(gate_w, up_w):
    def tiles(Wq):
        WT = np.ascontiguousarray(Wq.T)
        t = WT.reshape(KD, 128, KI, 128)
        return np.ascontiguousarray(t.transpose(2, 1, 0, 3))
    g = tiles(_int32_to_u8(gate_w))
    u = tiles(_int32_to_u8(up_w))
    return np.ascontiguousarray(np.stack([g, u], axis=1))


def _pack_down(down_w):
    Wq = _int32_to_u8(down_w)
    WT = np.ascontiguousarray(Wq.T)
    t = WT.reshape(KI, 128, KD, 128)
    t = t.transpose(2, 0, 1, 3)
    t = t.reshape(KD, 4, 16, 128, 128).transpose(0, 1, 3, 2, 4)
    return np.ascontiguousarray(t)


def kernel(L0_attn_x, L0_attn_rms_w, L0_attn_Wf_w, L0_attn_Wf_b, L0_attn_Wzg_w,
           L0_attn_Wz_w, L0_attn_Wga_w, L0_attn_Wgb_w, L0_attn_Wout_w,
           L0_attn_Wout_b, L0_ffn_gate_w, L0_ffn_gate_scale, L0_ffn_up_w,
           L0_ffn_up_scale, L0_ffn_down_w, L0_ffn_down_scale):
    x = np.ascontiguousarray(np.asarray(L0_attn_x, dtype=np.float32))
    f32 = lambda a: np.ascontiguousarray(np.asarray(a, dtype=np.float32))
    rms_w = f32(L0_attn_rms_w)

    shared = {
        "rms_bcast": np.ascontiguousarray(np.tile(rms_w[None, :], (128, 1)).astype(ml_dtypes.bfloat16)),
        "ident_bf": np.eye(128, dtype=ml_dtypes.bfloat16),
        "ident_f32": np.eye(128, dtype=np.float32),
        "f_bias": np.ascontiguousarray(f32(L0_attn_Wf_b).reshape(KS, 128).T),
        "out_bias": np.ascontiguousarray(f32(L0_attn_Wout_b).reshape(KD, 128).T),
        "gate_sc": np.ascontiguousarray(f32(L0_ffn_gate_scale).reshape(KI, 128).T),
        "up_sc": np.ascontiguousarray(f32(L0_ffn_up_scale).reshape(KI, 128).T),
        "down_sc": np.ascontiguousarray(f32(L0_ffn_down_scale).reshape(KD, 128).T),
        "wf": _pack_proj(f32(L0_attn_Wf_w)),
        "wzg": _pack_proj(f32(L0_attn_Wzg_w)),
        "wz": _pack_proj(f32(L0_attn_Wz_w)),
        "wga": _pack_proj(f32(L0_attn_Wga_w)),
        "wgb": _pack_proj(f32(L0_attn_Wgb_w)),
        "wo": _pack_wout(f32(L0_attn_Wout_w)),
        "wgu": _pack_gu(L0_ffn_gate_w, L0_ffn_up_w),
        "wdn": _pack_down(L0_ffn_down_w),
    }

    in_maps = []
    for core in range(8):
        b, half = core // 2, core % 2
        m = dict(shared)
        m["x_own"] = np.ascontiguousarray(x[b, half * NT:(half + 1) * NT, :])
        m["x_pre"] = np.ascontiguousarray(x[b, NT - PRE:NT, :])
        m["maskv"] = np.full((128, 1), 1.0 if half == 1 else 0.0, np.float32)
        in_maps.append(m)

    nc = _get_program()
    res = run_bass_kernel_spmd(nc, in_maps, core_ids=list(range(8)))

    out = np.empty((B, T, D), dtype=np.float32)
    for core in range(8):
        b, half = core // 2, core % 2
        out[b, half * NT:(half + 1) * NT, :] = res.results[core]["out"]
    return out
